# version 06
# speedup vs baseline: 1.4220x; 1.4090x over previous
import numpy as np
import ml_dtypes
import concourse.bass as bass
import concourse.mybir as mybir
from concourse.bass_utils import run_bass_kernel_spmd
from concourse.bass_core_rust import IndirectOffsetOnAxis

F32 = mybir.dt.float32
BF16 = mybir.dt.bfloat16
U32 = mybir.dt.uint32
AF = mybir.ActivationFunctionType
ALU = mybir.AluOpType
AX = mybir.AxisListType.X

S, DM = 8192, 1024
DEBUG_A = False
DEBUG_B = False
C2_PHASES = 5
C1_PHASES = 2
D_PHASES = 3
TLOC, NTT, NKT = 2048, 16, 64
bfd = ml_dtypes.bfloat16

PH = [0, 4, 1, 5, 2, 6, 3, 7, 8, 12, 9, 13, 10, 14, 11, 15]
PERM1024 = np.concatenate([np.arange(h * 64, h * 64 + 64) for h in PH])


class Ct:
    def __init__(self):
        self.c = {}

    def add(self, n, k=1):
        self.c[n] = self.c.get(n, 0) + k
        return self.c[n]

    def v(self, n):
        return self.c.get(n, 0)


def mk_auto(ct, key, sem, eng):
    def run(f):
        inst = f()
        inst.then_inc(sem, 1)
        eng.wait_ge(sem, ct.add(key))
        return inst
    return run


def phase_a(nc, io):
    xb, xloc = io["xb"], io["xloc"]
    wqT, wkT, wvT, woT = io["wqT"], io["wkT"], io["wvT"], io["woT"]
    bq_l, bk_l, bv_r, bo_r = io["bq_l"], io["bk_l"], io["bv_r"], io["bo_r"]
    wqf, bqf_l = io["wqf"], io["bqf_l"]
    identb, shiftb = io["identb"], io["shiftb"]
    a0, bst, xnfm, xntm = io["a0"], io["bst"], io["xnfm"], io["xntm"]
    tab, tabb = io["tab"], io["tabb"]
    if DEBUG_A:
        dxn = dk = dv = dat = None

    R_XN = 0
    R_K = 16384
    R_V = 32768
    R_AT = 49408
    R_WQ = 65792
    R_W2 = 73984
    R_SC = 82176
    R_QC = 85248
    R_MS = 87296
    PBN = 90752
    F_A = 0
    F_ST = 128
    F_BI = 292
    F_RD = 320
    F_BO = 832
    F_ON = 1856
    PFN = 1984

    ct = Ct()
    import contextlib
    with contextlib.ExitStack() as _st:
        E = _st.enter_context
        pb = E(nc.sbuf_tensor("Apb", [128, PBN], BF16))
        cst = E(nc.sbuf_tensor("Acst", [128, 8], F32))
        for _i, _v in enumerate([1e-5, -15.0, 0.125]):
            nc.const_aps.register(F32, _v, cst[:, _i:_i + 1])
        pf = E(nc.sbuf_tensor("Apf", [128, PFN], F32))
        rb64 = E(nc.sbuf_tensor("Arb64", [64, 512], F32))
        idb = E(nc.sbuf_tensor("Aidb", [128, 128], BF16))
        shb = E(nc.sbuf_tensor("Ashb", [64, 128], BF16))
        psA = E(nc.psum_tensor("ApsA", [128, 512], F32))
        psB = E(nc.psum_tensor("ApsB", [128, 512], F32))
        psC = E(nc.psum_tensor("ApsC", [128, 512], F32))
        psD = E(nc.psum_tensor("ApsD", [128, 512], F32))
        psE = E(nc.psum_tensor("ApsE", [128, 512], F32))
        psF = E(nc.psum_tensor("ApsF", [128, 512], F32))
        psT0 = E(nc.psum_tensor("ApsT0", [128, 128], BF16))
        psT1 = E(nc.psum_tensor("ApsT1", [128, 128], BF16))
        LD = E(nc.semaphore("ALD"))
        GS = E(nc.semaphore("AGS"))
        VS = E(nc.semaphore("AVS"))
        SS = E(nc.semaphore("ASS"))
        TS = E(nc.semaphore("ATS"))
        SQS = E(nc.semaphore("ASQS"))
        EXS = E(nc.semaphore("AEXS"))
        AVS = E(nc.semaphore("AAVS"))
        VP5 = E(nc.semaphore("AVP5"))
        QPS = E(nc.semaphore("AQPS"))
        QES = E(nc.semaphore("AQES"))
        SHS = E(nc.semaphore("ASHS"))
        OS = E(nc.semaphore("AOS"))
        ATB = E(nc.semaphore("ATB"))
        block = E(nc.Block())
        PT0 = R_SC + 2048
        PT1 = R_SC + 2560

        @block.sync
        def _(sy):
            sy.dma_start(out=idb[:], in_=identb).then_inc(LD, 16)
            sy.dma_start(out=pf[:, 292:300], in_=bq_l).then_inc(LD, 16)
            sy.dma_start(out=pf[:, 300:302], in_=bk_l).then_inc(LD, 16)
            sy.dma_start(out=pf[:, 304:320], in_=bqf_l).then_inc(LD, 16)
            sy.dma_start(out=pf[0:1, F_BO:F_BO + DM], in_=bo_r).then_inc(LD, 16)
            sy.dma_start(out=shb[:], in_=shiftb).then_inc(LD, 16)
            ct.add("LD", 6 * 16)
        LD_c = ct.v("LD")


        @block.gpsimd
        def _(g):
            g.dma_start(out=pb[:, R_AT:R_AT + 16384]
                        .rearrange("p (i c) -> p i c", c=DM),
                        in_=xloc.rearrange("(i p) c -> p i c", p=128)).then_inc(GS, 16)
            ct.add("GS", 16)
        GS_xloc = ct.v("GS")

        @block.vector
        def _(v):
            V = mk_auto(ct, "VS", VS, v)
            for _i, _v in enumerate([1e-5, -15.0, 0.125]):
                V(lambda _i=_i, _v=_v: v.memset(cst[:, _i:_i + 1], _v))
            V(lambda: v.memset(pb[0:1, R_MS + 256:R_MS + 384], 1.0))
            V(lambda: v.memset(pf[0:1, F_ON:F_ON + 128], 1.0))
            V(lambda: v.memset(pf[64:65, F_ON:F_ON + 128], 1.0))
            V(lambda: v.memset(
                pb[:, R_V:R_V + 16640].rearrange("p (kt h e) -> p kt h e", h=4, e=65)
                [:, :, :, 64:65], 1.0))
        VS_init = ct.v("VS")

        @block.vector
        def _(v):
            V = mk_auto(ct, "VS", VS, v)
            v.wait_ge(GS, GS_xloc)
            V(lambda: v.tensor_reduce(
                pf[:, 0:16],
                pb[:, R_AT:R_AT + 16384].rearrange("p (i c) -> p i c", c=DM),
                axis=AX, op=ALU.add))
            V(lambda: v.tensor_scalar(pf[:, 16:32], pf[:, 0:16], -1.0 / DM, None,
                                      ALU.mult))
        VS_p1a = ct.v("VS")

        @block.scalar
        def _(s):
            Sc = mk_auto(ct, "SS", SS, s)
            s.wait_ge(GS, GS_xloc)
            for i in range(16):
                Sc(lambda i=i: s.activation(
                    pb[:, R_SC + 2048:R_SC + 3072],
                    pb[:, R_AT + i * 1024:R_AT + (i + 1) * 1024], AF.Square,
                    bias=0.0, scale=1.0, accum_out=pf[:, 32 + i:33 + i]))
        SS_p1sq = ct.v("SS")

        @block.vector
        def _(v):
            V = mk_auto(ct, "VS", VS, v)
            v.wait_ge(SS, SS_p1sq)
            V(lambda: v.tensor_scalar(pf[:, 32:48], pf[:, 32:48], 1.0 / DM, None,
                                      ALU.mult))
            V(lambda: v.tensor_tensor(pf[:, 48:64], pf[:, 16:32], pf[:, 16:32],
                                      ALU.mult))
            V(lambda: v.tensor_sub(pf[:, 32:48], pf[:, 32:48], pf[:, 48:64]))
        VS_p1b = ct.v("VS")

        @block.scalar
        def _(s):
            Sc = mk_auto(ct, "SS", SS, s)
            s.wait_ge(VS, VS_p1b)
            Sc(lambda: s.activation(pf[:, 48:64], pf[:, 32:48], AF.Sqrt,
                                    bias=1e-5, scale=1.0))
        SS_p1sd = ct.v("SS")

        @block.vector
        def _(v):
            V = mk_auto(ct, "VS", VS, v)
            v.wait_ge(SS, SS_p1sd)
            V(lambda: v.reciprocal(pf[:, 64:80], pf[:, 48:64]))
        VS_p1c = ct.v("VS")

        SS_e1base = ct.v("SS")
        for i in range(16):
            xnt = R_SC + (i % 2) * 1024

            @block.vector
            def _(v, i=i, xnt=xnt):
                V = mk_auto(ct, "VS", VS, v)
                if i >= 2:
                    v.wait_ge(TS, ct.v("TS_p1_%d" % (i - 2)))
                    v.wait_ge(OS, 16 + 16 * (i - 1))
                V(lambda: v.tensor_scalar(
                    pb[:, xnt:xnt + 1024],
                    pb[:, R_AT + i * 1024:R_AT + (i + 1) * 1024],
                    pf[:, 16 + i:17 + i], pf[:, 64 + i:65 + i],
                    ALU.add, ALU.mult))
                ct.c["VS_p1n_%d" % i] = ct.v("VS")

            @block.sync
            def _(sy, i=i, xnt=xnt):
                sy.wait_ge(VS, ct.v("VS_p1n_%d" % i))
                sy.dma_start(out=xntm[i * 128:(i + 1) * 128, :],
                             in_=pb[:, xnt:xnt + 1024]).then_inc(OS, 16)
                ct.add("OS", 16)

            @block.tensor
            def _(t, i=i, xnt=xnt):
                t.wait_ge(VS, ct.v("VS_p1n_%d" % i))
                for fc in range(8):
                    pt = psT0 if fc % 2 == 0 else psT1
                    if i * 8 + fc >= 2:
                        t.wait_ge(SS, SS_e1base + i * 8 + fc - 1)
                    t.transpose(pt[:], pb[:, xnt + fc * 128:xnt + (fc + 1) * 128],
                                idb[:]).then_inc(TS, 1)
                    ct.add("TS")
                    ct.c["TS_p1t_%d" % (i * 8 + fc)] = ct.v("TS")
                ct.c["TS_p1_%d" % i] = ct.v("TS")

            @block.scalar
            def _(s, i=i):
                Sc = mk_auto(ct, "SS", SS, s)
                for fc in range(8):
                    pt = psT0 if fc % 2 == 0 else psT1
                    s.wait_ge(TS, ct.v("TS_p1t_%d" % (i * 8 + fc)))
                    Sc(lambda pt=pt, fc=fc: s.activation(
                        pb[:, R_XN + fc * 2048 + i * 128:
                           R_XN + fc * 2048 + i * 128 + 128],
                        pt[:], AF.Copy, bias=0.0, scale=1.0))
                    ct.c["SS_p1e_%d" % (i * 8 + fc)] = ct.v("SS")
        SS_xn = ct.v("SS")
        VS_p1done = ct.v("VS")

        @block.gpsimd
        def _(g):
            g.wait_ge(VS, ct.v("VS_p1n_15"))
            g.wait_ge(SS, SS_xn)
            for fc in range(8):
                g.dma_start(out=pb[:, R_AT + fc * 2048:R_AT + (fc + 1) * 2048],
                            in_=wqf[fc * 128:(fc + 1) * 128, :]).then_inc(GS, 16)
            ct.add("GS", 8 * 16)
        GS_wqf = ct.v("GS")

        SS_p2base = ct.v("SS")

        @block.tensor
        def _(t):
            t.wait_ge(GS, GS_wqf)
            t.wait_ge(LD, LD_c)
            for j in range(64):
                ofc2, nb = j // 4, j % 4
                ps = psE if j % 2 == 0 else psF
                if j >= 2:
                    t.wait_ge(SS, SS_p2base + (j - 1) * 2)
                for fc in range(8):
                    inst = t.matmul(
                        ps[:],
                        pb[:, R_AT + fc * 2048 + ofc2 * 128:
                           R_AT + fc * 2048 + ofc2 * 128 + 128],
                        pb[:, R_XN + fc * 2048 + nb * 512:
                           R_XN + fc * 2048 + nb * 512 + 512],
                        start=(fc == 0), stop=(fc == 7))
                inst.then_inc(TS, 1)
                ct.add("TS")
        TS_p2base = ct.v("TS") - 64

        @block.scalar
        def _(s):
            Sc = mk_auto(ct, "SS", SS, s)
            for j in range(64):
                ofc2 = j // 4
                ps = psE if j % 2 == 0 else psF
                s.wait_ge(TS, TS_p2base + j + 1)
                q2b = pb[:, R_SC + 2048:R_SC + 2560]
                Sc(lambda ps=ps, q2b=q2b, ofc2=ofc2, j=j: s.activation(
                    q2b, ps[:], AF.Identity,
                    bias=pf[:, 304 + ofc2:305 + ofc2], scale=1.0,
                    accum_out=pf[:, F_ST + j:F_ST + j + 1]))
                Sc(lambda q2b=q2b, j=j: s.activation(
                    pb[:, R_SC + 2560:R_SC + 3072], q2b, AF.Square,
                    bias=0.0, scale=1.0,
                    accum_out=pf[:, F_ST + 64 + j:F_ST + 64 + j + 1]))
        SS_p2 = ct.v("SS")

        @block.vector
        def _(v):
            V = mk_auto(ct, "VS", VS, v)
            v.wait_ge(SS, SS_p2)
            V(lambda: v.tensor_reduce(
                pf[:, 256:272],
                pf[:, F_ST:F_ST + 64].rearrange("p (a b) -> p a b", b=4),
                axis=AX, op=ALU.add))
            V(lambda: v.tensor_reduce(
                pf[:, 272:288],
                pf[:, F_ST + 64:F_ST + 128].rearrange("p (a b) -> p a b", b=4),
                axis=AX, op=ALU.add))
            V(lambda: v.tensor_reduce(
                pf[:, 288:290],
                pf[:, 256:272].rearrange("p (a b) -> p b a", b=2),
                axis=AX, op=ALU.add))
            V(lambda: v.tensor_reduce(
                pf[:, 290:292],
                pf[:, 272:288].rearrange("p (a b) -> p b a", b=2),
                axis=AX, op=ALU.add))
        VS_st = ct.v("VS")

        @block.sync
        def _(sy):
            sy.wait_ge(VS, VS_st)
            sy.dma_start(out=bst, in_=pf[:, 288:292]).then_inc(OS, 16)
            ct.add("OS", 16)

        @block.gpsimd
        def _(g):
            for fc in range(8):
                g.dma_start(out=pb[:, R_WQ + fc * 1024:R_WQ + (fc + 1) * 1024],
                            in_=wqT[fc * 128:(fc + 1) * 128, :]).then_inc(GS, 16)
            for fc in range(8):
                g.dma_start(out=pb[:, R_W2 + fc * 256:R_W2 + (fc + 1) * 256],
                            in_=wkT[fc * 128:(fc + 1) * 128, :]).then_inc(GS, 16)
                g.dma_start(out=pb[:, R_W2 + 2048 + fc * 256:
                               R_W2 + 2048 + (fc + 1) * 256],
                            in_=wvT[fc * 128:(fc + 1) * 128, :]).then_inc(GS, 16)
            g.dma_start(out=pb[0:1, R_MS:R_MS + 256], in_=bv_r).then_inc(GS, 16)
            ct.add("GS", 25 * 16)
        GS_w = ct.v("GS")

        @block.tensor
        def _(t):
            t.wait_ge(GS, GS_w)
            t.wait_ge(VS, VS_init)
            t.wait_ge(LD, LD_c)
            t.matmul(psC[:, 0:256], pb[0:1, R_MS + 256:R_MS + 384],
                     pb[0:1, R_MS:R_MS + 256], start=True, stop=True).then_inc(TS, 1)
            ct.add("TS")
            ct.c["TS_bvb"] = ct.v("TS")
            for cb in range(2):
                ps = psD if cb == 0 else psE
                t.matmul(ps[:], pf[0:1, F_ON:F_ON + 128],
                         pf[0:1, F_BO + cb * 512:F_BO + (cb + 1) * 512],
                         start=True, stop=True).then_inc(TS, 1)
                ct.add("TS")
            ct.c["TS_bob"] = ct.v("TS")

        @block.vector
        def _(v):
            V = mk_auto(ct, "VS", VS, v)
            v.wait_ge(TS, ct.v("TS_bvb"))
            V(lambda: v.tensor_copy(pb[:, R_MS:R_MS + 256], psC[:, 0:256]))
            v.wait_ge(TS, ct.v("TS_bob"))
            V(lambda: v.tensor_copy(pb[:, R_MS + 384:R_MS + 896], psD[:]))
            V(lambda: v.tensor_copy(pb[:, R_MS + 896:R_MS + 1408], psE[:]))
        VS_bcast = ct.v("VS")

        SS_p4base = ct.v("SS")

        def ss_of_evac(e):
            return SS_p4base + 9 * (e // 64 + 1) + e + 1
        for gk in range(8):
            @block.gpsimd
            def _(g, gk=gk):
                if gk == 0:
                    g.wait_ge(SS, SS_p2)
                else:
                    g.wait_ge(SS, ct.v("SS_p4sq_%d" % (gk - 1)))
                    g.wait_ge(VS, ct.v("VS_p4n_%d" % (gk * 8 - 1)))
                g.dma_start(out=pb[:, R_AT:R_AT + 8192]
                            .rearrange("p (i c) -> p i c", c=DM),
                            in_=xb[gk * 1024:(gk + 1) * 1024, :]
                            .rearrange("(i p) c -> p i c", p=128)).then_inc(GS, 16)
                ct.add("GS", 16)
                ct.c["GS_p4_%d" % gk] = ct.v("GS")

            @block.vector
            def _(v, gk=gk):
                V = mk_auto(ct, "VS", VS, v)
                v.wait_ge(GS, ct.v("GS_p4_%d" % gk))
                V(lambda: v.tensor_reduce(
                    pf[:, 0:8],
                    pb[:, R_AT:R_AT + 8192].rearrange("p (i c) -> p i c", c=DM),
                    axis=AX, op=ALU.add))
                V(lambda: v.tensor_scalar(pf[:, 16:24], pf[:, 0:8], -1.0 / DM,
                                          None, ALU.mult))

            @block.scalar
            def _(s, gk=gk):
                Sc = mk_auto(ct, "SS", SS, s)
                s.wait_ge(GS, ct.v("GS_p4_%d" % gk))
                for i in range(8):
                    Sc(lambda i=i: s.activation(
                        pb[:, R_SC + 2048:R_SC + 3072],
                        pb[:, R_AT + i * 1024:R_AT + (i + 1) * 1024], AF.Square,
                        bias=0.0, scale=1.0, accum_out=pf[:, 32 + i:33 + i]))
                ct.c["SS_p4sq_%d" % gk] = ct.v("SS")

            @block.vector
            def _(v, gk=gk):
                V = mk_auto(ct, "VS", VS, v)
                v.wait_ge(SS, ct.v("SS_p4sq_%d" % gk))
                V(lambda: v.tensor_scalar(pf[:, 32:40], pf[:, 32:40], 1.0 / DM,
                                          None, ALU.mult))
                V(lambda: v.tensor_tensor(pf[:, 48:56], pf[:, 16:24],
                                          pf[:, 16:24], ALU.mult))
                V(lambda: v.tensor_sub(pf[:, 32:40], pf[:, 32:40], pf[:, 48:56]))
                ct.c["VS_p4b_%d" % gk] = ct.v("VS")

            @block.scalar
            def _(s, gk=gk):
                Sc = mk_auto(ct, "SS", SS, s)
                s.wait_ge(VS, ct.v("VS_p4b_%d" % gk))
                Sc(lambda: s.activation(pf[:, 48:56], pf[:, 32:40], AF.Sqrt,
                                        bias=1e-5, scale=1.0))
                ct.c["SS_p4sd_%d" % gk] = ct.v("SS")

            @block.vector
            def _(v, gk=gk):
                V = mk_auto(ct, "VS", VS, v)
                v.wait_ge(SS, ct.v("SS_p4sd_%d" % gk))
                V(lambda: v.reciprocal(pf[:, 64:72], pf[:, 48:56]))
                ct.c["VS_p4c_%d" % gk] = ct.v("VS")

            for i in range(8):
                kt = gk * 8 + i
                xnt = R_SC + (i % 2) * 1024

                @block.vector
                def _(v, gk=gk, i=i, kt=kt, xnt=xnt):
                    V = mk_auto(ct, "VS", VS, v)
                    if kt >= 2:
                        v.wait_ge(TS, ct.v("TS_p4t8_%d" % (kt - 2)))
                    V(lambda: v.tensor_scalar(
                        pb[:, xnt:xnt + 1024],
                        pb[:, R_AT + i * 1024:R_AT + (i + 1) * 1024],
                        pf[:, 16 + i:17 + i], pf[:, 64 + i:65 + i],
                        ALU.add, ALU.mult))
                    ct.c["VS_p4n_%d" % kt] = ct.v("VS")

                @block.tensor
                def _(t, kt=kt, xnt=xnt):
                    t.wait_ge(VS, ct.v("VS_p4n_%d" % kt))
                    for fc in range(8):
                        pt = psT0 if fc % 2 == 0 else psT1
                        if kt * 8 + fc >= 2:
                            t.wait_ge(SS, ss_of_evac(kt * 8 + fc - 2))
                        t.transpose(pt[:], pb[:, xnt + fc * 128:xnt + (fc + 1) * 128],
                                    idb[:]).then_inc(TS, 1)
                        ct.add("TS")
                        ct.c["TS_p4t_%d" % (kt * 8 + fc)] = ct.v("TS")
                    ct.c["TS_p4t8_%d" % kt] = ct.v("TS")

                @block.scalar
                def _(s, kt=kt):
                    Sc = mk_auto(ct, "SS", SS, s)
                    for fc in range(8):
                        pt = psT0 if fc % 2 == 0 else psT1
                        s.wait_ge(TS, ct.v("TS_p4t_%d" % (kt * 8 + fc)))
                        Sc(lambda pt=pt, fc=fc: s.activation(
                            pb[:, R_QC + fc * 128:R_QC + (fc + 1) * 128],
                            pt[:], AF.Copy, bias=0.0, scale=1.0))
                        ct.c["SS_p4t_%d" % (kt * 8 + fc)] = ct.v("SS")
                    ct.c["SS_p4fm_%d" % kt] = ct.v("SS")

                @block.tensor
                def _(t, kt=kt):
                    t.wait_ge(SS, ss_of_evac(kt * 8 + 7))
                    if kt == 0:
                        t.wait_ge(GS, GS_w)
                        t.wait_ge(VS, VS_bcast)
                    else:
                        t.wait_ge(VS, ct.v("VS_p4kv_%d" % (kt - 1)))
                    for nch in range(2):
                        ps = psA if nch == 0 else psB
                        for fc in range(8):
                            inst = t.matmul(
                                ps[:, 0:128],
                                pb[:, R_W2 + fc * 256 + nch * 128:
                                   R_W2 + fc * 256 + nch * 128 + 128],
                                pb[:, R_QC + fc * 128:R_QC + (fc + 1) * 128],
                                start=(fc == 0), stop=(fc == 7))
                        inst.then_inc(TS, 1)
                        ct.add("TS")
                    for fc in range(8):
                        inst = t.matmul(
                            psC[:, 0:256],
                            pb[:, R_QC + fc * 128:R_QC + (fc + 1) * 128],
                            pb[:, R_W2 + 2048 + fc * 256:
                               R_W2 + 2048 + (fc + 1) * 256],
                            start=(fc == 0), stop=(fc == 7))
                    inst.then_inc(TS, 1)
                    ct.add("TS")
                    ct.c["TS_p4kv_%d" % kt] = ct.v("TS")

                @block.vector
                def _(v, kt=kt):
                    V = mk_auto(ct, "VS", VS, v)
                    v.wait_ge(TS, ct.v("TS_p4kv_%d" % kt))
                    for nch in range(2):
                        ps = psA if nch == 0 else psB
                        V(lambda ps=ps, nch=nch: v.tensor_scalar(
                            pb[:, R_K + nch * 8192 + kt * 128:
                               R_K + nch * 8192 + kt * 128 + 128],
                            ps[:, 0:128], pf[:, 300 + nch:301 + nch], None,
                            ALU.add))
                    V(lambda: v.tensor_tensor(
                        pb[:, R_V + kt * 260:R_V + kt * 260 + 260]
                        .rearrange("p (h e) -> p h e", e=65)[:, :, 0:64],
                        psC[:, 0:256].rearrange("p (h e) -> p h e", e=64),
                        pb[:, R_MS:R_MS + 256].rearrange("p (h e) -> p h e", e=64),
                        ALU.add))
                    ct.c["VS_p4kv_%d" % kt] = ct.v("VS")
        SS_p4 = ct.v("SS")
        VS_p4 = ct.v("VS")

        def kvslice(h):
            kvh = h // 4
            return kvh // 2, (kvh % 2) * 64

        TS_p5base = ct.v("TS")

        @block.tensor
        def _(t):
            t.wait_ge(VS, VS_p4)
            t.wait_ge(SS, SS_p4)
            nS = 0
            for qc in range(8):
                for nb in range(4):
                    j = qc * 4 + nb
                    ps = psE if j % 2 == 0 else psF
                    if j >= 2:
                        t.wait_ge(QES, j - 1)
                    for fc in range(8):
                        inst = t.matmul(
                            ps[:],
                            pb[:, R_WQ + fc * 1024 + qc * 128:
                               R_WQ + fc * 1024 + qc * 128 + 128],
                            pb[:, R_XN + fc * 2048 + nb * 512:
                               R_XN + fc * 2048 + nb * 512 + 512],
                            start=(fc == 0), stop=(fc == 7))
                    inst.then_inc(QPS, 1)
                t.wait_ge(QES, (qc + 1) * 4)
                for hh in range(2):
                    h = PH[qc * 2 + hh]
                    doff = hh * 64
                    nch, koff = kvslice(h)
                    assert koff == doff, (h, koff, doff)
                    for qb in range(4):
                        m = (qc * 2 + hh) * 4 + qb
                        for kc in range(64):
                            n = nS
                            nS += 1
                            ps = psA if n % 2 == 0 else psB
                            if n >= 2:
                                t.wait_ge(EXS, n - 1)
                            t.matmul(
                                ps[:],
                                pb[koff:koff + 64,
                                   R_K + nch * 8192 + kc * 128:
                                   R_K + nch * 8192 + kc * 128 + 128],
                                pb[doff:doff + 64,
                                   R_QC + qb * 512:R_QC + qb * 512 + 512],
                                start=True, stop=True).then_inc(SQS, 1)
                            if kc == 0:
                                t.wait_ge(VP5, 4 * m)
                            if kc >= 1:
                                t.wait_ge(EXS, n)
                                t.matmul(
                                    psC[0:65, :],
                                    pb[:, R_V + (kc - 1) * 260 + (h // 4) * 65:
                                       R_V + (kc - 1) * 260 + (h // 4) * 65 + 65],
                                    pb[:, (PT0 if (n - 1) % 2 == 0 else PT1):
                                       (PT0 if (n - 1) % 2 == 0 else PT1) + 512],
                                    start=(kc == 1), stop=False)
                        t.wait_ge(EXS, nS)
                        t.matmul(
                            psC[0:65, :],
                            pb[:, R_V + 63 * 260 + (h // 4) * 65:
                               R_V + 63 * 260 + (h // 4) * 65 + 65],
                            pb[:, (PT0 if (nS - 1) % 2 == 0 else PT1):
                               (PT0 if (nS - 1) % 2 == 0 else PT1) + 512],
                            start=False, stop=True).then_inc(AVS, 1)
                        t.wait_ge(VP5, 4 * m + 1)
                        t.matmul(psD[0:64, :], pf[64:65, F_ON:F_ON + 64],
                                 pf[64:65, F_RD:F_RD + 512],
                                 start=True, stop=True).then_inc(AVS, 1)
                        if hh == 1:
                            t.wait_ge(VP5, 4 * m + 3)
                            t.matmul(psD[:], shb[:],
                                     pb[0:64, R_SC:R_SC + 512],
                                     start=True, stop=True).then_inc(SHS, 1)

        @block.scalar
        def _(s):
            s.wait_ge(LD, LD_c)
            s.wait_ge(VS, VS_init)
            for qc in range(8):
                for nb in range(4):
                    j = qc * 4 + nb
                    ps = psE if j % 2 == 0 else psF
                    s.wait_ge(QPS, j + 1)
                    s.activation(pb[:, R_QC + nb * 512:R_QC + (nb + 1) * 512],
                                 ps[:], AF.Identity,
                                 bias=pf[:, 292 + qc:293 + qc],
                                 scale=1.0).then_inc(QES, 1)
                for hh in range(2):
                    for qb in range(4):
                        m = (qc * 2 + hh) * 4 + qb
                        for kc in range(64):
                            n = m * 64 + kc
                            ps = psA if n % 2 == 0 else psB
                            s.wait_ge(SQS, n + 1)
                            s.activation(
                                pb[:, (PT0 if n % 2 == 0 else PT1):
                                   (PT0 if n % 2 == 0 else PT1) + 512],
                                ps[:], AF.Exp, bias=-15.0,
                                scale=0.125).then_inc(EXS, 1)

        @block.vector
        def _(v):
            shs_i = 0
            for qc in range(8):
                for hh in range(2):
                    for qb in range(4):
                        m = (qc * 2 + hh) * 4 + qb
                        v.wait_ge(AVS, 2 * m + 1)
                        v.reciprocal(pf[64:65, F_RD:F_RD + 512],
                                     psC[64:65, :]).then_inc(VP5, 1)
                        v.wait_ge(VP5, 4 * m + 1)
                        v.wait_ge(AVS, 2 * m + 2)
                        v.tensor_copy(rb64[:], psD[0:64, :]).then_inc(VP5, 1)
                        v.wait_ge(VP5, 4 * m + 2)
                        if hh == 0:
                            v.scalar_tensor_tensor(
                                pb[0:64,
                                   R_AT + qc * 2048 + qb * 512:
                                   R_AT + qc * 2048 + qb * 512 + 512],
                                psC[0:64, :], 1.0, rb64[:],
                                ALU.mult, ALU.mult).then_inc(VP5, 1)
                            v.wait_ge(VP5, 4 * m + 3)
                            v.engine_nop().then_inc(VP5, 1)
                            v.wait_ge(VP5, 4 * m + 4)
                        else:
                            v.scalar_tensor_tensor(
                                pb[0:64, R_SC:R_SC + 512],
                                psC[0:64, :], 1.0, rb64[:],
                                ALU.mult, ALU.mult).then_inc(VP5, 1)
                            v.wait_ge(VP5, 4 * m + 3)
                            shs_i += 1
                            v.wait_ge(SHS, shs_i)
                            v.tensor_copy(
                                pb[64:128,
                                   R_AT + qc * 2048 + qb * 512:
                                   R_AT + qc * 2048 + qb * 512 + 512],
                                psD[64:128, :]).then_inc(VP5, 1)
                            v.wait_ge(VP5, 4 * m + 4)
        VS_after_p5 = ct.v("VS")

        @block.gpsimd
        def _(g):
            g.wait_ge(TS, ct.v("TS_p4kv_63"))
            for fc in range(8):
                g.dma_start(out=pb[:, R_W2 + fc * 1024:R_W2 + (fc + 1) * 1024],
                            in_=woT[fc * 128:(fc + 1) * 128, :]).then_inc(GS, 16)
            ct.add("GS", 8 * 16)
        GS_wo = ct.v("GS")

        @block.tensor
        def _(t):
            t.wait_ge(GS, GS_wo)
            t.wait_ge(VP5, 4 * 64)
            for tt in range(16):
                for cb in range(2):
                    j = tt * 2 + cb
                    ps = psE if j % 2 == 0 else psF
                    if j >= 2:
                        t.wait_ge(VS, VS_p4 + (j - 1))
                    for ec in range(8):
                        inst = t.matmul(
                            ps[:],
                            pb[:, R_AT + ec * 2048 + tt * 128:
                               R_AT + ec * 2048 + tt * 128 + 128],
                            pb[:, R_W2 + ec * 1024 + cb * 512:
                               R_W2 + ec * 1024 + (cb + 1) * 512],
                            start=(ec == 0), stop=(ec == 7))
                    inst.then_inc(TS, 1)
                    ct.add("TS")
                    ct.c["TS_p6_%d" % j] = ct.v("TS")

        @block.vector
        def _(v):
            for tt in range(16):
                for cb in range(2):
                    j = tt * 2 + cb
                    ps = psE if j % 2 == 0 else psF
                    v.wait_ge(TS, ct.v("TS_p6_%d" % j))
                    if j >= 2:
                        v.wait_ge(GS, GS_wo + ((j - 1) // 2) * 16)
                    v.scalar_tensor_tensor(
                        pb[:, R_MS + 1408 + (tt % 2) * 1024 + cb * 512:
                           R_MS + 1408 + (tt % 2) * 1024 + (cb + 1) * 512],
                        ps[:], 1.0,
                        pb[:, R_MS + 384 + cb * 512:R_MS + 384 + (cb + 1) * 512],
                        ALU.mult, ALU.add).then_inc(VS, 1)
                    ct.add("VS")
                    ct.c["VS_p6_%d" % j] = ct.v("VS")

        @block.gpsimd
        def _(g):
            for tt in range(16):
                g.wait_ge(VS, ct.v("VS_p6_%d" % (tt * 2 + 1)))
                g.dma_start(out=a0[tt * 128:(tt + 1) * 128, :],
                            in_=pb[:, R_MS + 1408 + (tt % 2) * 1024:
                                   R_MS + 1408 + (tt % 2) * 1024 + 1024]
                            ).then_inc(GS, 16)
                ct.add("GS", 16)
        GS_end = ct.v("GS")

        if DEBUG_A:
            @block.gpsimd
            def _(g):
                g.wait_ge(GS, GS_end)
                g.dma_start(out=dxn, in_=pb[:, R_XN:R_XN + 16384]).then_inc(GS, 16)
                g.dma_start(out=dk, in_=pb[:, R_K:R_K + 16384]).then_inc(GS, 16)
                g.dma_start(out=dv, in_=pb[:, R_V:R_V + 16640]).then_inc(GS, 16)
                g.dma_start(out=dat, in_=pb[:, R_AT:R_AT + 16384]).then_inc(GS, 16)
                ct.add("GS", 64)
            GS_end2 = ct.v("GS")
        else:
            GS_end2 = GS_end

        @block.sync
        def _(sy):
            sy.wait_ge(GS, GS_end2)
            sy.wait_ge(OS, ct.v("OS"))
            sy.dma_start(out=xnfm, in_=pb[:, R_XN:R_XN + 16384]).then_inc(OS, 16)
            sy.wait_ge(OS, ct.v("OS") + 16)



MAGIC = 12582912.0


def phase_b(nc, io):
    xloc, a0, xnfm, xntm = io["xloc"], io["a0"], io["xnfm"], io["xntm"]
    bsta = io["bstG"].rearrange("(c r) j -> r c j", c=8)
    wqf, bqf_l = io["wqf"], io["bqf_l"]
    gam_l, bet_l = io["gam_l"], io["bet_l"]
    keys0, keys1, tab, identb = io["keys0"], io["keys1"], io["tab"], io["identb"]
    iotad, negc, iota16 = io["iotad"], io["negc"], io["iota16"]
    x1, tabb = io["x1"], io["tabb"]
    if DEBUG_B:
        didx = dwt = dq = drn = None

    R_XN = 0
    R_WQF = 16384
    R_Q = 32768
    R_Q2 = 36864
    R_KH = 38912
    R_GT = 39424
    R_JK = 43520
    R_XT = 44544
    R_ON = 46592
    R_GT2 = 46624
    PBN = 50720

    def gbuf(u):
        m = u % 4
        return (R_GT + m * 2048) if m < 2 else (R_GT2 + (m - 2) * 2048)
    F_S0 = 0
    F_E = 256
    F_CE = 512
    F_D = 768
    F_OH = 1024
    F_W = 1536
    F_ST = 1600
    F_NR = 1728
    F_RN = 3776
    F_KEY = 3808
    F_EI = 3936
    F_ON1 = 3944
    F_YA = 3968
    F_XA = 6016
    F_WT = 10112
    PFN = 12160

    ct = Ct()
    import contextlib
    with contextlib.ExitStack() as _st:
        E = _st.enter_context
        pb = E(nc.sbuf_tensor("Bpb", [128, PBN], BF16))
        pf = E(nc.sbuf_tensor("Bpf", [128, PFN], F32))
        cst = E(nc.sbuf_tensor("Bcst", [128, 8], F32))
        for _i, _v in enumerate([1e-5, 1e-12]):
            nc.const_aps.register(F32, _v, cst[:, _i:_i + 1])
        cts = E(nc.sbuf_tensor("Bcts", [128, 416], F32))
        idxu = E(nc.sbuf_tensor("Bidxu", [128, 2048], U32))
        idb = E(nc.sbuf_tensor("Bidb", [128, 128], BF16))
        psA = E(nc.psum_tensor("BpsA", [128, 512], F32))
        psB = E(nc.psum_tensor("BpsB", [128, 512], F32))
        psC = E(nc.psum_tensor("BpsC", [128, 512], F32))
        psE = E(nc.psum_tensor("BpsE", [128, 512], F32))
        psT0 = E(nc.psum_tensor("BpsT0", [128, 128], BF16))
        LD = E(nc.semaphore("BLD"))
        GS = E(nc.semaphore("BGS"))
        VS = E(nc.semaphore("BVS"))
        SS = E(nc.semaphore("BSS"))
        TS = E(nc.semaphore("BTS"))
        BQP = E(nc.semaphore("BBQP"))
        BQE = E(nc.semaphore("BBQE"))
        GT = E(nc.semaphore("BGT"))
        DSS = E(nc.semaphore("BDSS"))
        GLS = E(nc.semaphore("BGLS"))
        UPS = E(nc.semaphore("BUPS"))
        OS = E(nc.semaphore("BOS"))
        block = E(nc.Block())

        def VA(v, inst):
            inst.then_inc(VS, 1)
            v.wait_ge(VS, ct.add("VS"))
            return inst

        @block.sync
        def _(sy):
            sy.dma_start(out=idb[:], in_=identb).then_inc(LD, 16)
            sy.dma_start(out=pf[:, 1600:1632].rearrange("r (c j) -> r c j", j=4),
                         in_=bsta).then_inc(LD, 16)
            sy.dma_start(out=pf[:, 1660:1676], in_=bqf_l).then_inc(LD, 16)
            sy.dma_start(out=pf[:, 1676:1678], in_=gam_l).then_inc(LD, 16)
            sy.dma_start(out=pf[:, 1678:1680], in_=bet_l).then_inc(LD, 16)
            sy.dma_start(out=pf[:, F_KEY:F_KEY + 128], in_=keys0).then_inc(LD, 16)
            sy.dma_start(out=cts[:, 0:128], in_=iotad).then_inc(LD, 16)
            sy.dma_start(out=cts[:, 128:384], in_=negc).then_inc(LD, 16)
            sy.dma_start(out=cts[:, 384:400], in_=iota16).then_inc(LD, 16)
            ct.add("LD", 9 * 16)
        LD_c = ct.v("LD")

        @block.gpsimd
        def _(g):
            g.dma_start(out=pb[:, R_XN:R_XN + 16384], in_=xnfm).then_inc(GS, 16)
            for fc in range(8):
                g.dma_start(out=pb[:, R_WQF + fc * 2048:R_WQF + (fc + 1) * 2048],
                            in_=wqf[fc * 128:(fc + 1) * 128, :]).then_inc(GS, 16)
            ct.add("GS", 9 * 16)
        GS_w = ct.v("GS")

        @block.gpsimd
        def _(g):
            for i in range(16):
                g.dma_start(out=tabb[i * 1024:(i + 1) * 1024, :],
                            in_=tab[i * 1024:(i + 1) * 1024, :]
                            ).then_inc(GS, 16)
                ct.add("GS", 16)
            g.wait_ge(GS, ct.v("GS"))
        GS_tab = ct.v("GS")

        @block.vector
        def _(v):
            V = mk_auto(ct, "VS", VS, v)
            v.wait_ge(LD, LD_c)
            for _i, _v2 in enumerate([1e-5, 1e-12]):
                V(lambda _i=_i, _v2=_v2: v.memset(cst[:, _i:_i + 1], _v2))
            V(lambda: v.memset(pb[:, R_ON:R_ON + 1], 1.0))
            V(lambda: v.memset(pf[:, F_ON1:F_ON1 + 1], 1.0))
            V(lambda: v.tensor_reduce(
                pf[:, 1632:1636],
                pf[:, 1600:1632].rearrange("p (c s) -> p s c", s=4),
                axis=AX, op=ALU.add))
            NINV = 1.0 / 131072.0
            V(lambda: v.tensor_scalar(pf[:, 1636:1638], pf[:, 1632:1634], -NINV,
                                      None, ALU.mult))
            V(lambda: v.tensor_scalar(pf[:, 1638:1640], pf[:, 1634:1636], NINV,
                                      None, ALU.mult))
            V(lambda: v.tensor_tensor(pf[:, 1640:1642], pf[:, 1636:1638],
                                      pf[:, 1636:1638], ALU.mult))
            V(lambda: v.tensor_sub(pf[:, 1638:1640], pf[:, 1638:1640],
                                   pf[:, 1640:1642]))
        VS_bn1 = ct.v("VS")

        @block.scalar
        def _(s):
            Sc = mk_auto(ct, "SS", SS, s)
            s.wait_ge(VS, VS_bn1)
            Sc(lambda: s.activation(pf[:, 1640:1642], pf[:, 1638:1640], AF.Sqrt,
                                    bias=1e-5, scale=1.0))
        SS_bn = ct.v("SS")

        @block.vector
        def _(v):
            V = mk_auto(ct, "VS", VS, v)
            v.wait_ge(SS, SS_bn)
            V(lambda: v.reciprocal(pf[:, 1642:1644], pf[:, 1640:1642]))
            V(lambda: v.tensor_tensor(pf[:, 1700:1702], pf[:, 1642:1644],
                                      pf[:, 1676:1678], ALU.mult))
            for half in range(2):
                V(lambda half=half: v.tensor_scalar(
                    pf[:, 1704 + half:1720:2], pf[:, 1660 + half:1676:2],
                    pf[:, 1636 + half:1637 + half],
                    pf[:, 1700 + half:1701 + half], ALU.add, ALU.mult))
                V(lambda half=half: v.tensor_scalar(
                    pf[:, 1704 + half:1720:2], pf[:, 1704 + half:1720:2],
                    pf[:, 1678 + half:1679 + half], None, ALU.add))
        VS_bn2 = ct.v("VS")

        for ki in range(2):
            if ki == 1:
                @block.sync
                def _(sy):
                    sy.wait_ge(VS, ct.v("VS_key_0"))
                    sy.dma_start(out=pf[:, F_KEY:F_KEY + 128],
                                 in_=keys1).then_inc(LD, 16)
                    ct.add("LD", 16)
            LD_key = ct.v("LD")

            @block.scalar
            def _(s, ki=ki, LD_key=LD_key):
                Sc = mk_auto(ct, "SS", SS, s)
                s.wait_ge(LD, LD_key)
                if ki == 0:
                    s.wait_ge(VS, VS_bn2)
                Sc(lambda: s.activation(pb[:, R_JK:R_JK + 128],
                                        pf[:, F_KEY:F_KEY + 128], AF.Square,
                                        bias=0.0, scale=1.0,
                                        accum_out=pf[:, 1644:1645]))
                Sc(lambda: s.activation(pf[:, 1645:1646], pf[:, 1644:1645],
                                        AF.Sqrt, bias=0.0, scale=1.0))
            ct.c["SS_key_%d" % ki] = ct.v("SS")

            @block.vector
            def _(v, ki=ki):
                V = mk_auto(ct, "VS", VS, v)
                v.wait_ge(SS, ct.v("SS_key_%d" % ki))
                V(lambda: v.tensor_scalar(pf[:, 1645:1646], pf[:, 1645:1646],
                                          1e-12, None, ALU.max))
                V(lambda: v.reciprocal(pf[:, 1646:1647], pf[:, 1645:1646]))
                V(lambda: v.tensor_scalar(
                    pb[:, R_KH + 256:R_KH + 384], pf[:, F_KEY:F_KEY + 128],
                    pf[:, 1646:1647], None, ALU.mult))
                ct.c["VS_key_%d" % ki] = ct.v("VS")

            @block.tensor
            def _(t, ki=ki):
                t.wait_ge(VS, ct.v("VS_key_%d" % ki))
                t.wait_ge(LD, LD_c)
                t.transpose(psT0[:], pb[:, R_KH + 256:R_KH + 384],
                            idb[:]).then_inc(TS, 1)
                ct.add("TS")
                ct.c["TS_key_%d" % ki] = ct.v("TS")

            @block.scalar
            def _(s, ki=ki):
                Sc = mk_auto(ct, "SS", SS, s)
                s.wait_ge(TS, ct.v("TS_key_%d" % ki))
                Sc(lambda ki=ki: s.activation(
                    pb[:, R_KH + ki * 128:R_KH + (ki + 1) * 128], psT0[:],
                    AF.Copy, bias=0.0, scale=1.0))
        SS_keys = ct.v("SS")

        nbank = [psA, psB, psC, psE]
        for h in range(8):
            for half in range(2):
                ofc2 = h * 2 + half

                @block.tensor
                def _(t, h=h, half=half, ofc2=ofc2):
                    g0 = h * 8 + half * 4
                    if h == 0 and half == 0:
                        t.wait_ge(GS, GS_w)
                        t.wait_ge(SS, SS_keys)
                    for nb in range(4):
                        j = g0 + nb
                        ps = psA if j % 2 == 0 else psB
                        if j >= 2:
                            t.wait_ge(BQE, j - 1)
                        for fc in range(8):
                            inst = t.matmul(
                                ps[:],
                                pb[:, R_WQF + fc * 2048 + ofc2 * 128:
                                   R_WQF + fc * 2048 + ofc2 * 128 + 128],
                                pb[:, R_XN + fc * 2048 + nb * 512:
                                   R_XN + fc * 2048 + nb * 512 + 512],
                                start=(fc == 0), stop=(fc == 7))
                        inst.then_inc(BQP, 1)

                @block.vector
                def _(v, h=h, half=half, ofc2=ofc2):
                    g0 = h * 8 + half * 4
                    for nb in range(4):
                        j = g0 + nb
                        ps = psA if j % 2 == 0 else psB
                        v.wait_ge(BQP, j + 1)
                        v.tensor_scalar(
                            pb[:, R_Q + half * 2048 + nb * 512:
                               R_Q + half * 2048 + (nb + 1) * 512],
                            ps[:], pf[:, 1700 + half:1701 + half],
                            pf[:, 1704 + ofc2:1705 + ofc2],
                            ALU.mult, ALU.add).then_inc(BQE, 1)

                @block.scalar
                def _(s, h=h, half=half):
                    Sc = mk_auto(ct, "SS", SS, s)
                    s.wait_ge(BQE, h * 8 + half * 4 + 4)
                    Sc(lambda half=half: s.activation(
                        pb[:, R_Q2:R_Q2 + 2048],
                        pb[:, R_Q + half * 2048:R_Q + half * 2048 + 2048],
                        AF.Square, bias=0.0, scale=1.0))
                ct.c["SS_sq_%d_%d" % (h, half)] = ct.v("SS")

                @block.tensor
                def _(t, h=h, half=half):
                    t.wait_ge(SS, ct.v("SS_sq_%d_%d" % (h, half)))
                    if h * 2 + half > 0:
                        t.wait_ge(VS, ct.v("VS_rn_%d" % (h * 2 + half - 1)))
                    for nb in range(4):
                        t.matmul(nbank[nb][0:1, :], pb[:, R_ON:R_ON + 1],
                                 pb[:, R_Q2 + nb * 512:R_Q2 + (nb + 1) * 512],
                                 start=True, stop=True).then_inc(TS, 1)
                        ct.add("TS")
                        ct.c["TS_nr_%d" % nb] = ct.v("TS")

                @block.vector
                def _(v, h=h, half=half):
                    V = mk_auto(ct, "VS", VS, v)
                    for nb in range(4):
                        v.wait_ge(TS, ct.v("TS_nr_%d" % nb))
                        V(lambda nb=nb: v.tensor_copy(
                            pf[0:1, F_NR + nb * 512:F_NR + (nb + 1) * 512],
                            nbank[nb][0:1, :]))
                    ct.c["VS_nrr_%d" % (h * 2 + half)] = ct.v("VS")

                @block.tensor
                def _(t, h=h, half=half):
                    t.wait_ge(VS, ct.v("VS_nrr_%d" % (h * 2 + half)))
                    for tt in range(16):
                        inst = t.matmul(psE[:, 256 + tt:257 + tt],
                                        pf[0:1, F_NR + tt * 128:
                                           F_NR + (tt + 1) * 128],
                                        pf[0:1, F_ON1:F_ON1 + 1],
                                        start=True, stop=True)
                    inst.then_inc(TS, 1)
                    ct.add("TS")
                    ct.c["TS_nc_%d" % (h * 2 + half)] = ct.v("TS")

                @block.vector
                def _(v, h=h, half=half):
                    V = mk_auto(ct, "VS", VS, v)
                    v.wait_ge(TS, ct.v("TS_nc_%d" % (h * 2 + half)))
                    V(lambda half=half: v.tensor_copy(
                        pf[:, F_RN + half * 16:F_RN + half * 16 + 16],
                        psE[:, 256:272]))
                    ct.c["VS_rn_%d" % (h * 2 + half)] = ct.v("VS")

            @block.scalar
            def _(s, h=h):
                Sc = mk_auto(ct, "SS", SS, s)
                s.wait_ge(VS, ct.v("VS_rn_%d" % (h * 2 + 1)))
                Sc(lambda: s.activation(pf[:, F_RN:F_RN + 32],
                                        pf[:, F_RN:F_RN + 32], AF.Sqrt,
                                        bias=0.0, scale=1.0))
            ct.c["SS_rsq_%d" % h] = ct.v("SS")

            @block.vector
            def _(v, h=h):
                V = mk_auto(ct, "VS", VS, v)
                v.wait_ge(SS, ct.v("SS_rsq_%d" % h))
                V(lambda: v.tensor_scalar(pf[:, F_RN:F_RN + 32],
                                          pf[:, F_RN:F_RN + 32], 1e-12, None,
                                          ALU.max))
                V(lambda: v.reciprocal(pf[:, F_RN:F_RN + 32],
                                       pf[:, F_RN:F_RN + 32]))
            ct.c["VS_rsq_%d" % h] = ct.v("VS")

            for tt in range(16):
                @block.tensor
                def _(t, h=h, tt=tt):
                    t.wait_ge(VS, ct.v("VS_rsq_%d" % h))
                    if tt > 0:
                        t.wait_ge(VS, ct.v("VS_sc_%d_%d" % (h, tt - 1)))
                    for half in range(2):
                        ps = psC if half == 0 else psE
                        t.matmul(
                            ps[:, 0:128],
                            pb[:, R_Q + half * 2048 + tt * 128:
                               R_Q + half * 2048 + tt * 128 + 128],
                            pb[:, R_KH + half * 128:R_KH + (half + 1) * 128],
                            start=True, stop=True).then_inc(TS, 1)
                        ct.add("TS")
                        ct.c["TS_sc_%d" % half] = ct.v("TS")

                @block.vector
                def _(v, h=h, tt=tt):
                    V = mk_auto(ct, "VS", VS, v)
                    D = F_D
                    for half in range(2):
                        ps = psC if half == 0 else psE
                        v.wait_ge(TS, ct.v("TS_sc_%d" % half))
                        V(lambda ps=ps, half=half: v.tensor_scalar(
                            pf[:, F_S0 + half * 128:F_S0 + (half + 1) * 128],
                            ps[:, 0:128],
                            pf[:, F_RN + half * 16 + tt:
                               F_RN + half * 16 + tt + 1],
                            None, ALU.mult))
                    ct.c["VS_sc_%d_%d" % (h, tt)] = ct.v("VS")
                    for half in range(2):
                        s0 = pf[:, F_S0 + half * 128:F_S0 + (half + 1) * 128]
                        e0 = pf[:, F_E:F_E + 128]
                        ev = pf[:, D + half * 16:D + half * 16 + 16]
                        V(lambda s0=s0, e0=e0: v.tensor_scalar(
                            e0, s0, 32768.0, MAGIC, ALU.mult, ALU.add))
                        V(lambda e0=e0: v.tensor_scalar(
                            e0, e0, MAGIC, 256.0, ALU.subtract, ALU.mult))
                        V(lambda e0=e0: v.tensor_add(e0, e0, cts[:, 0:128]))
                        V(lambda e0=e0, ev=ev: v.max(out=ev[:, 0:8], in_=e0))
                        V(lambda e0=e0, ev=ev: v.match_replace(
                            out=e0, in_to_replace=ev[:, 0:8], in_values=e0,
                            imm_value=-1e9))
                        V(lambda e0=e0, ev=ev: v.max(out=ev[:, 8:16], in_=e0))
                        u = pf[:, D + 48:D + 64]
                        Q = pf[:, D + 64:D + 80]
                        r = pf[:, D + 80:D + 96]
                        V(lambda ev=ev, u=u: v.tensor_scalar(
                            u, ev, 1.0 / 256.0, MAGIC, ALU.mult, ALU.add))
                        V(lambda u=u, Q=Q: v.tensor_scalar(
                            Q, u, MAGIC, None, ALU.subtract))
                        V(lambda Q=Q, ev=ev, r=r: v.scalar_tensor_tensor(
                            r, Q, -256.0, ev, ALU.mult, ALU.add))
                        V(lambda r=r, half=half: v.tensor_scalar(
                            pf[:, D + 96 + half * 16:D + 112 + half * 16],
                            r, -1.0, 127.0, ALU.mult, ALU.add))
                        V(lambda Q=Q, half=half: v.tensor_scalar(
                            pf[:, D + 128 + half * 16:D + 144 + half * 16],
                            Q, 2.0 ** -15, None, ALU.mult))
                    ce = pf[:, F_CE:F_CE + 256]
                    v0 = pf[:, D + 128:D + 144]
                    v1 = pf[:, D + 144:D + 160]
                    V(lambda: v.tensor_tensor(
                        ce.rearrange("p (a b) -> p a b", b=16),
                        v0[:, :].unsqueeze(2).to_broadcast([128, 16, 16]),
                        v1[:, :].unsqueeze(1).to_broadcast([128, 16, 16]),
                        ALU.add))
                    V(lambda: v.tensor_scalar(ce, ce, 16384.0, MAGIC,
                                              ALU.mult, ALU.add))
                    V(lambda: v.tensor_scalar(ce, ce, MAGIC, 512.0,
                                              ALU.subtract, ALU.mult))
                    V(lambda: v.tensor_add(ce, ce, cts[:, 128:384]))
                    cv = pf[:, D + 160:D + 176]
                    V(lambda: v.max(out=cv[:, 0:8], in_=ce))
                    V(lambda: v.match_replace(out=ce, in_to_replace=cv[:, 0:8],
                                              in_values=ce, imm_value=-1e9))
                    V(lambda: v.max(out=cv[:, 8:16], in_=ce))
                    u2 = pf[:, D + 176:D + 192]
                    Q2 = pf[:, D + 192:D + 208]
                    r2 = pf[:, D + 208:D + 224]
                    V(lambda: v.tensor_scalar(u2, cv, 1.0 / 512.0, MAGIC,
                                              ALU.mult, ALU.add))
                    V(lambda: v.tensor_scalar(Q2, u2, MAGIC, None, ALU.subtract))
                    V(lambda: v.scalar_tensor_tensor(r2, Q2, -512.0, cv,
                                                     ALU.mult, ALU.add))
                    code = pf[:, D + 224:D + 240]
                    V(lambda: v.tensor_scalar(code, r2, -1.0, 255.0,
                                              ALU.mult, ALU.add))
                    vc = pf[:, F_W + 32:F_W + 48]
                    V(lambda: v.tensor_scalar(vc, Q2, 2.0 ** -14, None,
                                              ALU.mult))
                    aa = pf[:, D + 240:D + 256]
                    V(lambda: v.tensor_scalar(aa, code, 0.0625, -0.46875,
                                              ALU.mult, ALU.add))
                    V(lambda: v.tensor_scalar(aa, aa, MAGIC, MAGIC,
                                              ALU.add, ALU.subtract))
                    bb2 = pf[:, F_E:F_E + 16]
                    V(lambda: v.scalar_tensor_tensor(bb2, aa, -16.0, code,
                                                     ALU.mult, ALU.add))
                    oh = pf[:, F_OH:F_OH + 256]
                    t4 = pf[:, F_OH + 256:F_OH + 512]
                    isel = pf[:, F_E + 16:F_E + 32]
                    jsel = pf[:, F_E + 32:F_E + 48]
                    i0l = pf[:, D + 96:D + 112]
                    j1l = pf[:, D + 112:D + 128]
                    V(lambda: v.tensor_tensor(
                        oh.rearrange("p (s w) -> p s w", w=16),
                        aa[:, :].unsqueeze(2).to_broadcast([128, 16, 16]),
                        cts[:, 384:400].unsqueeze(1).to_broadcast([128, 16, 16]),
                        ALU.is_equal))
                    V(lambda: v.tensor_tensor(
                        t4.rearrange("p (s w) -> p s w", w=16),
                        oh.rearrange("p (s w) -> p s w", w=16),
                        i0l[:, :].unsqueeze(1).to_broadcast([128, 16, 16]),
                        ALU.mult))
                    V(lambda: v.tensor_reduce(
                        isel, t4.rearrange("p (s w) -> p s w", w=16),
                        axis=AX, op=ALU.add))
                    V(lambda: v.tensor_tensor(
                        oh.rearrange("p (s w) -> p s w", w=16),
                        bb2[:, :].unsqueeze(2).to_broadcast([128, 16, 16]),
                        cts[:, 384:400].unsqueeze(1).to_broadcast([128, 16, 16]),
                        ALU.is_equal))
                    V(lambda: v.tensor_tensor(
                        t4.rearrange("p (s w) -> p s w", w=16),
                        oh.rearrange("p (s w) -> p s w", w=16),
                        j1l[:, :].unsqueeze(1).to_broadcast([128, 16, 16]),
                        ALU.mult))
                    V(lambda: v.tensor_reduce(
                        jsel, t4.rearrange("p (s w) -> p s w", w=16),
                        axis=AX, op=ALU.add))
                    nsel = pf[:, F_E + 48:F_E + 64]
                    V(lambda: v.scalar_tensor_tensor(nsel, isel, 128.0, jsel,
                                                     ALU.mult, ALU.add))
                    ui = (tt * 8 + h) * 16
                    V(lambda ui=ui: v.tensor_copy(idxu[:, ui:ui + 16], nsel))
                    V(lambda: v.tensor_scalar(pf[:, F_W + 16:F_W + 17],
                                              vc[:, 0:1], -1.0, None,
                                              ALU.mult))
                    ct.c["VS_tk_%d_%d" % (h, tt)] = ct.v("VS")

                @block.scalar
                def _(s, h=h, tt=tt):
                    Sc = mk_auto(ct, "SS", SS, s)
                    s.wait_ge(VS, ct.v("VS_tk_%d_%d" % (h, tt)))
                    Sc(lambda: s.activation(
                        pf[:, F_W:F_W + 16], pf[:, F_W + 32:F_W + 48], AF.Exp,
                        bias=pf[:, F_W + 16:F_W + 17], scale=1.0,
                        accum_out=pf[:, F_W + 17:F_W + 18]))
                ct.c["SS_w_%d_%d" % (h, tt)] = ct.v("SS")

                @block.vector
                def _(v, h=h, tt=tt):
                    V = mk_auto(ct, "VS", VS, v)
                    v.wait_ge(SS, ct.v("SS_w_%d_%d" % (h, tt)))
                    V(lambda: v.reciprocal(pf[:, F_W + 18:F_W + 19],
                                           pf[:, F_W + 17:F_W + 18]))
                    ui = (tt * 8 + h) * 16
                    V(lambda ui=ui: v.tensor_scalar(
                        pf[:, F_WT + ui:F_WT + ui + 16], pf[:, F_W:F_W + 16],
                        pf[:, F_W + 18:F_W + 19], None, ALU.mult))
                    ct.c["VS_w_%d_%d" % (h, tt)] = ct.v("VS")
        VS_route = ct.v("VS")

        NU = 2048

        @block.gpsimd
        def _(g):
            g.wait_ge(VS, VS_route)
            for u in range(NU):
                if u >= 4:
                    g.wait_ge(UPS, u - 3)
                buf = gbuf(u)
                g.indirect_dma_start(
                    out=pb[:, buf:buf + 2048], out_offset=None,
                    in_=tabb,
                    in_offset=IndirectOffsetOnAxis(ap=idxu[:, u:u + 1], axis=0),
                ).then_inc(GT, 16)

        LD_b2base = ct.v("LD")

        @block.scalar
        def _(s):
            s.wait_ge(VS, VS_route)
            for u in range(NU):
                s.wait_ge(DSS, u + 1)
                s.activation(pf[:, F_EI + 2 + u % 2:F_EI + 3 + u % 2],
                             pf[:, F_EI + u % 2:F_EI + u % 2 + 1], AF.Gelu,
                             bias=0.0, scale=1.0).then_inc(GLS, 1)

        def emit_cmul_up(v, up):
            utt = up // 128
            uhk = up % 128
            uh, uk = uhk // 16, uhk % 16
            wcol = F_WT + (utt * 8 + uh) * 16 + uk
            ub = gbuf(up)
            ya = F_YA + (utt % 2) * 1024
            v.wait_ge(GLS, up + 1)
            v.tensor_tensor(
                pf[:, F_EI + 4 + up % 2:F_EI + 5 + up % 2],
                pf[:, F_EI + 2 + up % 2:F_EI + 3 + up % 2],
                pf[:, wcol:wcol + 1], ALU.mult).then_inc(VS, 1)
            v.wait_ge(VS, ct.add("VS"))
            v.scalar_tensor_tensor(
                pf[:, ya:ya + 1024], pb[:, ub + 1024:ub + 2048],
                pf[:, F_EI + 4 + up % 2:F_EI + 5 + up % 2],
                pf[:, ya:ya + 1024], ALU.mult, ALU.add).then_inc(UPS, 1)

        def emit_epilogue(v, et):
            ya = F_YA + (et % 2) * 1024
            xa = F_XA + (et % 2) * 1024
            aa2 = F_XA + 2048 + (et % 2) * 1024
            v.tensor_add(pf[:, ya:ya + 1024], pf[:, ya:ya + 1024],
                         pf[:, xa:xa + 1024]).then_inc(VS, 1)
            v.wait_ge(VS, ct.add("VS"))
            v.tensor_add(pf[:, ya:ya + 1024], pf[:, ya:ya + 1024],
                         pf[:, aa2:aa2 + 1024]).then_inc(VS, 1)
            v.wait_ge(VS, ct.add("VS"))
            ct.c["VS_x1_%d" % et] = ct.v("VS")

        @block.vector
        def _(v):
            for tt in range(16):
                v.wait_ge(LD, LD_b2base + 48 * (tt + 1))
                if tt >= 2:
                    v.wait_ge(OS, (tt - 1) * 16)
                v.memset(pf[:, F_YA + (tt % 2) * 1024:
                            F_YA + (tt % 2) * 1024 + 1024], 0.0).then_inc(VS, 1)
                v.wait_ge(VS, ct.add("VS"))
                for hk in range(128):
                    u = tt * 128 + hk
                    buf = gbuf(u)
                    v.wait_ge(GT, 16 * (u + 1))
                    v.scalar_tensor_tensor(
                        pb[:, R_JK:R_JK + 1024], pb[:, buf:buf + 1024], 1.0,
                        pb[:, R_XT + (tt % 2) * 1024:
                           R_XT + (tt % 2) * 1024 + 1024],
                        ALU.mult, ALU.mult,
                        accum_out=pf[:, F_EI + u % 2:F_EI + u % 2 + 1]
                    ).then_inc(DSS, 1)
                    if u >= 1:
                        emit_cmul_up(v, u - 1)
                    if hk == 0 and tt > 0:
                        emit_epilogue(v, tt - 1)
            emit_cmul_up(v, NU - 1)
            emit_epilogue(v, 15)

        @block.sync
        def _(sy):
            def x1dma(k):
                sy.wait_ge(VS, ct.v("VS_x1_%d" % k))
                sy.dma_start(out=x1[k * 128:(k + 1) * 128, :],
                             in_=pf[:, F_YA + (k % 2) * 1024:
                                    F_YA + (k % 2) * 1024 + 1024]
                             ).then_inc(OS, 16)
            for tt in range(16):
                if tt >= 2:
                    x1dma(tt - 2)
                    sy.wait_ge(VS, ct.v("VS_x1_%d" % (tt - 2)))
                    sy.wait_ge(DSS, (tt - 1) * 128)
                sy.dma_start(out=pf[:, F_XA + (tt % 2) * 1024:
                                   F_XA + (tt % 2) * 1024 + 1024],
                             in_=xloc[tt * 128:(tt + 1) * 128, :]
                             ).then_inc(LD, 16)
                sy.dma_start(out=pf[:, F_XA + 2048 + (tt % 2) * 1024:
                                   F_XA + 2048 + (tt % 2) * 1024 + 1024],
                             in_=a0[tt * 128:(tt + 1) * 128, :]).then_inc(LD, 16)
                sy.dma_start(out=pb[:, R_XT + (tt % 2) * 1024:
                                   R_XT + (tt % 2) * 1024 + 1024],
                             in_=xntm[tt * 128:(tt + 1) * 128, :]
                             ).then_inc(LD, 16)
                ct.add("LD", 48)
            x1dma(14)
            x1dma(15)
            sy.wait_ge(OS, 16 * 16)
            if DEBUG_B:
                sy.dma_start(out=didx, in_=idxu[:]).then_inc(OS, 16)
                sy.dma_start(out=dwt, in_=pf[:, F_WT:F_WT + 2048]).then_inc(OS, 16)
                sy.dma_start(out=dq, in_=pb[:, R_Q:R_Q + 4096]).then_inc(OS, 16)
                sy.dma_start(out=drn, in_=pf[:, F_RN:F_RN + 32]).then_inc(OS, 16)
                sy.wait_ge(OS, 16 * 16 + 64)


def phase_c1(nc, io):
    x1 = io["x1"]
    wr1, br1_l, wr2, br2_r = io["wr1"], io["br1_l"], io["wr2"], io["br2_r"]
    we1, be1_l, we2, be2_9 = io["we1"], io["be1_l"], io["we2"], io["be2_9"]
    oneh8, identb = io["oneh8"], io["identb"]
    accD, xnfm1, xntm1 = io["accD"], io["xnfm1"], io["xntm1"]

    R_XN = 0
    R_W1 = 16384
    R_H = 49152
    R_W2 = 65536
    R_WB = 69632
    R_R1 = 70656
    R_WR1 = 74752
    R_WR2 = 76800
    R_W8 = 76816
    R_WRO = 76824
    R_SC = 76832
    R_Y = 78880
    R_ON = 80928
    R_OH8 = 81064
    R_BE2 = 82088
    R_BE2B = 83112
    PBN = 84136
    F_LN = 0
    F_BR1 = 128
    F_LG = 136
    F_X1 = 192
    F_BE = 2240
    F_BE2 = 2528
    F_ONR = 2600
    PFN = 2752

    ct = Ct()
    import contextlib
    with contextlib.ExitStack() as _st:
        E = _st.enter_context
        pb = E(nc.sbuf_tensor("Cpb", [128, PBN], BF16))
        pf = E(nc.sbuf_tensor("Cpf", [128, PFN], F32))
        cst = E(nc.sbuf_tensor("Ccst", [128, 8], F32))
        for _i, _v in enumerate([1e-5]):
            nc.const_aps.register(F32, _v, cst[:, _i:_i + 1])
        wrows = E(nc.sbuf_tensor("Cwrows", [8, 2048], BF16))
        idb = E(nc.sbuf_tensor("Cidb", [128, 128], BF16))
        LD = E(nc.semaphore("CLD"))
        GS = E(nc.semaphore("CGS"))
        VS = E(nc.semaphore("CVS"))
        SS = E(nc.semaphore("CSS"))
        TS = E(nc.semaphore("CTS"))
        HTS = E(nc.semaphore("CHTS"))
        HES = E(nc.semaphore("CHES"))
        HGS = E(nc.semaphore("CHGS"))
        YTS = E(nc.semaphore("CYTS"))
        YES = E(nc.semaphore("CYES"))
        W2S = E(nc.semaphore("CW2S"))
        WBS = E(nc.semaphore("CWBS"))
        OS = E(nc.semaphore("COS"))
        INITS = E(nc.semaphore("CINITS"))
        block = E(nc.Block())

        with (nc.psum_tensor("CpsT0", [128, 128], BF16) as psT0,
              nc.psum_tensor("CpsT1", [128, 128], BF16) as psT1,
              nc.psum_tensor("CpsR0", [128, 512], F32) as psR0,
              nc.psum_tensor("CpsR1", [128, 512], F32) as psR1):

            @block.sync
            def _(sy):
                sy.dma_start(out=idb[:], in_=identb).then_inc(LD, 16)
                sy.dma_start(out=pf[:, 2240:2528], in_=be1_l).then_inc(LD, 16)
                sy.dma_start(out=pf[0:1, 2528:2536], in_=br2_r).then_inc(LD, 16)
                sy.dma_start(out=pf[:, 128:130], in_=br1_l).then_inc(LD, 16)
                sy.dma_start(out=pb[0:8, R_OH8:R_OH8 + 1024],
                             in_=oneh8).then_inc(LD, 16)
                ct.add("LD", 5 * 16)
            LD_c = ct.v("LD")

            @block.gpsimd
            def _(g):
                g.dma_start(out=accD, in_=x1).then_inc(INITS, 16)
                g.dma_start(out=pb[0:8, R_BE2:R_BE2 + 1024],
                            in_=be2_9[0:8, :]).then_inc(GS, 16)
                g.dma_start(out=pb[0:1, R_BE2B:R_BE2B + 1024],
                            in_=be2_9[8:9, :]).then_inc(GS, 16)
                ct.add("GS", 32)
                g.dma_start(out=pb[:, R_W1:R_W1 + 16384]
                            .rearrange("p (i c) -> p i c", c=DM),
                            in_=x1.rearrange("(i p) c -> p i c", p=128)
                            ).then_inc(GS, 16)
                ct.add("GS", 16)
            GS_x1 = ct.v("GS")

            @block.vector
            def _(v):
                V = mk_auto(ct, "VS", VS, v)
                V(lambda: v.memset(cst[:, 0:1], 1e-5))
                V(lambda: v.memset(pb[:, R_ON:R_ON + 1], 1.0))
                V(lambda: v.memset(pb[0:1, R_ON + 2:R_ON + 130], 1.0))
                V(lambda: v.memset(pf[0:1, F_ONR:F_ONR + 128], 1.0))
            VS_init = ct.v("VS")

            @block.vector
            def _(v):
                V = mk_auto(ct, "VS", VS, v)
                v.wait_ge(GS, GS_x1)
                V(lambda: v.tensor_reduce(
                    pf[:, 0:16],
                    pb[:, R_W1:R_W1 + 16384].rearrange("p (i c) -> p i c", c=DM),
                    axis=AX, op=ALU.add))
                V(lambda: v.tensor_scalar(pf[:, 16:32], pf[:, 0:16], -1.0 / DM,
                                          None, ALU.mult))

            @block.scalar
            def _(s):
                Sc = mk_auto(ct, "SS", SS, s)
                s.wait_ge(GS, GS_x1)
                for i in range(16):
                    Sc(lambda i=i: s.activation(
                        pb[:, R_H:R_H + 1024],
                        pb[:, R_W1 + i * 1024:R_W1 + (i + 1) * 1024], AF.Square,
                        bias=0.0, scale=1.0, accum_out=pf[:, 32 + i:33 + i]))
            SS_p1sq = ct.v("SS")

            @block.vector
            def _(v):
                V = mk_auto(ct, "VS", VS, v)
                v.wait_ge(SS, SS_p1sq)
                V(lambda: v.tensor_scalar(pf[:, 32:48], pf[:, 32:48], 1.0 / DM,
                                          None, ALU.mult))
                V(lambda: v.tensor_tensor(pf[:, 48:64], pf[:, 16:32],
                                          pf[:, 16:32], ALU.mult))
                V(lambda: v.tensor_sub(pf[:, 32:48], pf[:, 32:48],
                                       pf[:, 48:64]))
            VS_p1b = ct.v("VS")

            @block.scalar
            def _(s):
                Sc = mk_auto(ct, "SS", SS, s)
                s.wait_ge(VS, VS_p1b)
                Sc(lambda: s.activation(pf[:, 48:64], pf[:, 32:48], AF.Sqrt,
                                        bias=1e-5, scale=1.0))
            SS_p1sd = ct.v("SS")

            @block.vector
            def _(v):
                V = mk_auto(ct, "VS", VS, v)
                v.wait_ge(SS, SS_p1sd)
                V(lambda: v.reciprocal(pf[:, 64:80], pf[:, 48:64]))

            if C1_PHASES <= 0.3:
                @block.sync
                def _(sy):
                    sy.wait_ge(OS, ct.v("OS"))
                    sy.wait_ge(INITS, 16)
                return nc
            SS_e1base = ct.v("SS")
            for i in range(16):
                xnt = R_SC + (i % 2) * 1024

                @block.vector
                def _(v, i=i, xnt=xnt):
                    V = mk_auto(ct, "VS", VS, v)
                    if i >= 2:
                        v.wait_ge(TS, ct.v("TS_p1_%d" % (i - 2)))
                        v.wait_ge(OS, 16 * (i - 1))
                    V(lambda: v.tensor_scalar(
                        pb[:, xnt:xnt + 1024],
                        pb[:, R_W1 + i * 1024:R_W1 + (i + 1) * 1024],
                        pf[:, 16 + i:17 + i], pf[:, 64 + i:65 + i],
                        ALU.add, ALU.mult))
                    ct.c["VS_p1n_%d" % i] = ct.v("VS")

                @block.sync
                def _(sy, i=i, xnt=xnt):
                    sy.wait_ge(VS, ct.v("VS_p1n_%d" % i))
                    sy.dma_start(out=xntm1[i * 128:(i + 1) * 128, :],
                                 in_=pb[:, xnt:xnt + 1024]).then_inc(OS, 16)
                    ct.add("OS", 16)

                @block.tensor
                def _(t, i=i, xnt=xnt):
                    t.wait_ge(VS, ct.v("VS_p1n_%d" % i))
                    t.wait_ge(LD, LD_c)
                    for fc in range(8):
                        pt = psT0 if fc % 2 == 0 else psT1
                        if i * 8 + fc >= 2:
                            t.wait_ge(SS, SS_e1base + i * 8 + fc - 1)
                        t.transpose(pt[:],
                                    pb[:, xnt + fc * 128:xnt + (fc + 1) * 128],
                                    idb[:]).then_inc(TS, 1)
                        ct.add("TS")
                        ct.c["TS_p1t_%d" % (i * 8 + fc)] = ct.v("TS")
                    ct.c["TS_p1_%d" % i] = ct.v("TS")

                @block.scalar
                def _(s, i=i):
                    Sc = mk_auto(ct, "SS", SS, s)
                    for fc in range(8):
                        pt = psT0 if fc % 2 == 0 else psT1
                        s.wait_ge(TS, ct.v("TS_p1t_%d" % (i * 8 + fc)))
                        Sc(lambda pt=pt, fc=fc: s.activation(
                            pb[:, R_XN + fc * 2048 + i * 128:
                               R_XN + fc * 2048 + i * 128 + 128],
                            pt[:], AF.Copy, bias=0.0, scale=1.0))
            SS_xn = ct.v("SS")
            VS_xn = ct.v("VS_p1n_15")

            @block.sync
            def _(sy):
                sy.wait_ge(SS, SS_xn)
                sy.dma_start(out=xnfm1,
                             in_=pb[:, R_XN:R_XN + 16384]).then_inc(OS, 16)
                ct.add("OS", 16)

            if C1_PHASES <= 0.5:
                @block.sync
                def _(sy):
                    sy.wait_ge(OS, ct.v("OS"))
                    sy.wait_ge(INITS, 16)
                return nc
            @block.gpsimd
            def _(g):
                for fc in range(8):
                    g.dma_start(out=pb[:, R_WR1 + fc * 256:
                                   R_WR1 + (fc + 1) * 256],
                                in_=wr1[fc * 128:(fc + 1) * 128, :]
                                ).then_inc(GS, 16)
                g.dma_start(out=pb[:, R_WR2:R_WR2 + 16]
                            .rearrange("p (a b) -> p a b", b=8),
                            in_=wr2.rearrange("(a p) b -> p a b", p=128)
                            ).then_inc(GS, 16)
                g.dma_start(out=pf[0:1, 130:138][0:1, 0:8]
                            if False else pf[0:1, 2728:2736],
                            in_=br2_r).then_inc(GS, 16)
                ct.add("GS", 10 * 16)
            GS_r = ct.v("GS")

            @block.tensor
            def _(t):
                t.wait_ge(GS, GS_r)
                t.wait_ge(SS, SS_xn)
                ssb = ct.v("SS")
                for rc in range(2):
                    for nb in range(4):
                        j = rc * 4 + nb
                        ps = psR0 if j % 2 == 0 else psR1
                        if j >= 2:
                            t.wait_ge(SS, ssb + j - 1)
                        for fc in range(8):
                            inst = t.matmul(
                                ps[:],
                                pb[:, R_WR1 + fc * 256 + rc * 128:
                                   R_WR1 + fc * 256 + rc * 128 + 128],
                                pb[:, R_XN + fc * 2048 + nb * 512:
                                   R_XN + fc * 2048 + nb * 512 + 512],
                                start=(fc == 0), stop=(fc == 7))
                        inst.then_inc(TS, 1)
                        ct.add("TS")
                        ct.c["TS_r1_%d" % j] = ct.v("TS")
            ct.c["SS_r1base"] = ct.v("SS")

            @block.scalar
            def _(s):
                Sc = mk_auto(ct, "SS", SS, s)
                for rc in range(2):
                    for nb in range(4):
                        j = rc * 4 + nb
                        ps = psR0 if j % 2 == 0 else psR1
                        s.wait_ge(TS, ct.v("TS_r1_%d" % j))
                        Sc(lambda ps=ps, rc=rc, nb=nb: s.activation(
                            pb[:, R_R1 + rc * 2048 + nb * 512:
                               R_R1 + rc * 2048 + (nb + 1) * 512],
                            ps[:], AF.Relu, bias=pf[:, 128 + rc:129 + rc],
                            scale=1.0))
            SS_r1 = ct.v("SS")
            if C1_PHASES <= 0.7:
                @block.sync
                def _(sy):
                    sy.wait_ge(OS, ct.v("OS"))
                    sy.wait_ge(SS, SS_r1)
                    sy.wait_ge(INITS, 16)
                return nc

            @block.tensor
            def _(t):
                t.wait_ge(VS, VS_init)
                t.wait_ge(GS, GS_r)
                t.matmul(psR0[:, 0:8], pf[0:1, F_ONR:F_ONR + 128],
                         pf[0:1, 2728:2736], start=True,
                         stop=True).then_inc(TS, 1)
                ct.add("TS")
                ct.c["TS_br2"] = ct.v("TS")

            @block.vector
            def _(v):
                V = mk_auto(ct, "VS", VS, v)
                v.wait_ge(TS, ct.v("TS_br2"))
                V(lambda: v.tensor_copy(pf[:, 2736:2744], psR0[:, 0:8]))
            VS_br2 = ct.v("VS")

            @block.tensor
            def _(t):
                t.wait_ge(VS, VS_br2)
                t.wait_ge(GS, GS_x1)
                for cb in range(2):
                    ps = psR0 if cb == 0 else psR1
                    t.matmul(ps[:], pb[0:1, R_ON + 2:R_ON + 130],
                             pb[0:1, R_BE2B + cb * 512:R_BE2B + (cb + 1) * 512],
                             start=True, stop=True).then_inc(TS, 1)
                    ct.add("TS")
                ct.c["TS_be2b"] = ct.v("TS")

            @block.vector
            def _(v):
                V = mk_auto(ct, "VS", VS, v)
                v.wait_ge(TS, ct.v("TS_be2b"))
                V(lambda: v.tensor_copy(pb[:, R_BE2B:R_BE2B + 512],
                                        psR0[:]))
                V(lambda: v.tensor_copy(pb[:, R_BE2B + 512:R_BE2B + 1024],
                                        psR1[:]))
            VS_be2b = ct.v("VS")

            for tt in range(16):
                @block.tensor
                def _(t, tt=tt):
                    t.wait_ge(SS, SS_r1)
                    t.wait_ge(VS, VS_be2b)
                    if tt >= 1:
                        t.wait_ge(VS, ct.v("VS_lga_%d" % (tt - 1)))
                    for rc in range(2):
                        inst = t.matmul(
                            psR0[:, 0:8],
                            pb[:, R_R1 + rc * 2048 + tt * 128:
                               R_R1 + rc * 2048 + tt * 128 + 128],
                            pb[:, R_WR2 + rc * 8:R_WR2 + (rc + 1) * 8],
                            start=(rc == 0), stop=(rc == 1))
                    inst.then_inc(TS, 1)
                    ct.add("TS")
                    ct.c["TS_lg_%d" % tt] = ct.v("TS")

                @block.vector
                def _(v, tt=tt):
                    V = mk_auto(ct, "VS", VS, v)
                    v.wait_ge(TS, ct.v("TS_lg_%d" % tt))
                    lg = pf[:, 136:144]
                    V(lambda: v.tensor_tensor(lg, psR0[:, 0:8],
                                              pf[:, 2736:2744], ALU.add))
                    ct.c["VS_lga_%d" % tt] = ct.v("VS")
                    V(lambda: v.tensor_reduce(pf[:, 144:145], lg, axis=AX,
                                              op=ALU.max))
                    V(lambda: v.tensor_scalar(pf[:, 152:160], lg,
                                              pf[:, 144:145], None,
                                              ALU.is_equal))
                    V(lambda: v.scalar_tensor_tensor(
                        pf[:, 160:168], pf[:, 152:160], -1e30, lg,
                        ALU.mult, ALU.add))
                    V(lambda: v.tensor_reduce(pf[:, 145:146], pf[:, 160:168],
                                              axis=AX, op=ALU.max))
                    V(lambda: v.tensor_scalar(pf[:, 168:176], pf[:, 160:168],
                                              pf[:, 145:146], None,
                                              ALU.is_equal))
                    V(lambda: v.tensor_scalar(pf[:, 146:147], pf[:, 144:145],
                                              -1.0, None, ALU.mult))
                    ct.c["VS_lg1_%d" % tt] = ct.v("VS")

                @block.scalar
                def _(s, tt=tt):
                    Sc = mk_auto(ct, "SS", SS, s)
                    s.wait_ge(VS, ct.v("VS_lg1_%d" % tt))
                    Sc(lambda: s.activation(pf[:, 184:192], pf[:, 136:144],
                                            AF.Exp, bias=pf[:, 146:147],
                                            scale=1.0,
                                            accum_out=pf[:, 147:148]))
                    Sc(lambda: s.activation(pf[:, 148:149], pf[:, 145:146],
                                            AF.Exp, bias=pf[:, 146:147],
                                            scale=1.0))
                ct.c["SS_lg_%d" % tt] = ct.v("SS")

                @block.vector
                def _(v, tt=tt):
                    V = mk_auto(ct, "VS", VS, v)
                    v.wait_ge(SS, ct.v("SS_lg_%d" % tt))
                    V(lambda: v.reciprocal(pf[:, 149:150], pf[:, 147:148]))
                    V(lambda: v.tensor_tensor(pf[:, 150:151], pf[:, 148:149],
                                              pf[:, 149:150], ALU.mult))
                    ct.c["VS_lg2_%d" % tt] = ct.v("VS")

                @block.scalar
                def _(s, tt=tt):
                    Sc = mk_auto(ct, "SS", SS, s)
                    s.wait_ge(VS, ct.v("VS_lg2_%d" % tt))
                    Sc(lambda: s.activation(pf[:, 151:152], pf[:, 149:150],
                                            AF.Exp, bias=0.0, scale=0.5))
                    Sc(lambda: s.activation(pf[:, 184:185], pf[:, 150:151],
                                            AF.Exp, bias=0.0, scale=0.5))
                ct.c["SS_lg2_%d" % tt] = ct.v("SS")

                @block.vector
                def _(v, tt=tt):
                    V = mk_auto(ct, "VS", VS, v)
                    v.wait_ge(SS, ct.v("SS_lg2_%d" % tt))
                    V(lambda: v.tensor_add(pf[:, 185:186], pf[:, 151:152],
                                           pf[:, 184:185]))
                    V(lambda: v.reciprocal(pf[:, 186:187], pf[:, 185:186]))
                    V(lambda: v.tensor_tensor(pf[:, 187:188], pf[:, 151:152],
                                              pf[:, 186:187], ALU.mult))
                    V(lambda: v.tensor_tensor(pf[:, 188:189], pf[:, 184:185],
                                              pf[:, 186:187], ALU.mult))
                    V(lambda: v.tensor_scalar(pf[:, 160:168], pf[:, 152:160],
                                              pf[:, 187:188], None, ALU.mult))
                    V(lambda: v.scalar_tensor_tensor(
                        pf[:, 160:168], pf[:, 168:176], pf[:, 188:189],
                        pf[:, 160:168], ALU.mult, ALU.add))
                    V(lambda: v.tensor_copy(pb[:, R_W8:R_W8 + 8],
                                            pf[:, 160:168]))
                    ct.c["VS_lg_%d" % tt] = ct.v("VS")

                @block.tensor
                def _(t, tt=tt):
                    t.wait_ge(VS, ct.v("VS_lg_%d" % tt))
                    if tt >= 2:
                        t.wait_ge(SS, ct.v("SS_w8_%d" % (tt - 2)))
                    t.transpose(psT0[0:8, :] if tt % 2 == 0 else psT1[0:8, :],
                                pb[:, R_W8:R_W8 + 8], idb[:]).then_inc(TS, 1)
                    ct.add("TS")
                    ct.c["TS_w8_%d" % tt] = ct.v("TS")

                @block.scalar
                def _(s, tt=tt):
                    Sc = mk_auto(ct, "SS", SS, s)
                    s.wait_ge(TS, ct.v("TS_w8_%d" % tt))
                    Sc(lambda tt=tt: s.activation(
                        wrows[:, tt * 128:(tt + 1) * 128],
                        (psT0[0:8, :] if tt % 2 == 0 else psT1[0:8, :]),
                        AF.Copy, bias=0.0, scale=1.0))
                ct.c["SS_w8_%d" % tt] = ct.v("SS")
            SS_router = ct.v("SS")
        if C1_PHASES <= 1:
            @block.sync
            def _(sy):
                sy.wait_ge(OS, ct.v("OS"))
                sy.wait_ge(INITS, 16)
            return nc
        with (nc.psum_tensor("CpsY0", [128, 1024], F32) as psY0,
              nc.psum_tensor("CpsY1", [128, 1024], F32) as psY1,
              nc.psum_tensor("CpsY2", [128, 1024], F32) as psY2,
              nc.psum_tensor("CpsY3", [128, 1024], F32) as psY3):
            psY = [psY0, psY1, psY2, psY3]
            with nc.semaphore("CWBE2") as WBE2, nc.semaphore("CCCS2") as CCS2:

                @block.gpsimd
                def _(g):
                    for e in range(9):
                        if e == 0:
                            g.wait_ge(SS, SS_xn)
                            g.wait_ge(VS, VS_xn)
                        else:
                            g.wait_ge(HTS, e * 128)
                        for fc in range(8):
                            g.dma_start(out=pb[:, R_W1 + fc * 4096:
                                           R_W1 + (fc + 1) * 4096],
                                        in_=we1[e, fc * 128:(fc + 1) * 128, :]
                                        ).then_inc(GS, 16)
                        ct.add("GS", 8 * 16)
                        ct.c["GS_we1_%d" % e] = ct.v("GS")
                        for nb in range(4):
                            m = e * 4 + nb
                            for ii in range(32):
                                k = m * 32 + ii
                                if k >= 4:
                                    g.wait_ge(YTS, k - 3)
                                g.dma_start(out=pb[:, R_W2 + (k % 4) * 1024:
                                               R_W2 + (k % 4) * 1024 + 1024],
                                            in_=we2[e, ii * 128:
                                                    (ii + 1) * 128, :]
                                            ).then_inc(W2S, 16)
                                ct.add("W2S", 16)
                            for tt2 in range(4):
                                j = m * 4 + tt2
                                if j == 0:
                                    g.wait_ge(INITS, 16)
                                g.wait_ge(YES, j + 1)
                                g.dma_start(
                                    out=accD[nb * 512 + tt2 * 128:
                                             nb * 512 + (tt2 + 1) * 128, :],
                                    in_=pb[:, R_Y + (j % 2) * 1024:
                                           R_Y + (j % 2) * 1024 + 1024],
                                    accum_op=ALU.add).then_inc(CCS2, 16)
                                ct.add("CCS2", 16)
                GS_exp = ct.v("GS")

                @block.tensor
                def _(t):
                    t.wait_ge(SS, SS_router)
                    for e in range(9):
                        t.wait_ge(GS, ct.v("GS_we1_%d" % e))
                        for nb in range(4):
                            m = e * 4 + nb
                            if e < 8:
                                if m >= 1:
                                    t.wait_ge(YES, m * 4)
                                t.matmul(psY2[:, 0:512],
                                         pb[0:8, R_OH8 + e * 128:
                                            R_OH8 + (e + 1) * 128],
                                         wrows[0:8, nb * 512:(nb + 1) * 512],
                                         start=True, stop=True).then_inc(WBS, 1)
                                ct.add("WBS")
                            for ii in range(32):
                                hg = m * 32 + ii
                                ph = psY3[:, (hg % 2) * 512:
                                          (hg % 2) * 512 + 512]
                                if ii < 2 and m >= 1:
                                    t.wait_ge(YES, m * 4)
                                if hg >= 2:
                                    t.wait_ge(HES, hg - 1)
                                for fc in range(8):
                                    inst = t.matmul(
                                        ph,
                                        pb[:, R_W1 + fc * 4096 + ii * 128:
                                           R_W1 + fc * 4096 + ii * 128 + 128],
                                        pb[:, R_XN + fc * 2048 + nb * 512:
                                           R_XN + fc * 2048 + nb * 512 + 512],
                                        start=(fc == 0), stop=(fc == 7))
                                inst.then_inc(HTS, 1)
                            for ii in range(32):
                                k = m * 32 + ii
                                t.wait_ge(W2S, 16 * (k + 1))
                                if e < 8:
                                    t.wait_ge(HGS, k + 1)
                                else:
                                    t.wait_ge(HES, k + 1)
                                if ii == 0:
                                    t.wait_ge(HES, m * 32 + 32)
                                    if e < 8:
                                        t.wait_ge(WBE2, m + 1)
                                    if m >= 1:
                                        t.wait_ge(YES, m * 4)
                                for tc in range(8):
                                    tt2, cb = tc // 2, tc % 2
                                    inst = t.matmul(
                                        psY[tt2][:, cb * 512:(cb + 1) * 512],
                                        pb[:, R_H + ii * 512 + tt2 * 128:
                                           R_H + ii * 512 + tt2 * 128 + 128],
                                        pb[:, R_W2 + (k % 4) * 1024 + cb * 512:
                                           R_W2 + (k % 4) * 1024 +
                                           (cb + 1) * 512],
                                        start=(ii == 0), stop=(ii == 31))
                                inst.then_inc(YTS, 1)
                    t.wait_ge(YES, 36 * 4)
                    for tt in range(16):
                        if tt >= 1:
                            t.wait_ge(YES, 144 + tt)
                        for cb in range(2):
                            inst = t.matmul(
                                psY0[:, cb * 512:(cb + 1) * 512],
                                wrows[0:8, tt * 128:(tt + 1) * 128],
                                pb[0:8, R_BE2 + cb * 512:R_BE2 +
                                   (cb + 1) * 512],
                                start=True, stop=True)
                        inst.then_inc(YTS, 1)

                @block.scalar
                def _(s):
                    for e in range(9):
                        for nb in range(4):
                            m = e * 4 + nb
                            for ii in range(32):
                                hg = m * 32 + ii
                                ph = psY3[:, (hg % 2) * 512:
                                          (hg % 2) * 512 + 512]
                                if ii == 0 and m >= 1:
                                    s.wait_ge(YTS, m * 32)
                                s.wait_ge(HTS, hg + 1)
                                s.activation(
                                    pb[:, R_H + ii * 512:R_H + (ii + 1) * 512],
                                    ph, AF.Silu,
                                    bias=pf[:, 2240 + e * 32 + ii:
                                            2241 + e * 32 + ii],
                                    scale=1.0).then_inc(HES, 1)

                @block.vector
                def _(v):
                    for e in range(9):
                        for nb in range(4):
                            m = e * 4 + nb
                            if e < 8:
                                v.wait_ge(WBS, m + 1)
                                v.tensor_copy(pb[:, R_WB + (m % 2) * 512:
                                                 R_WB + (m % 2) * 512 + 512],
                                              psY2[:, 0:512]).then_inc(WBE2, 1)
                                v.wait_ge(WBE2, m + 1)
                                for ii in range(32):
                                    hg = m * 32 + ii
                                    v.wait_ge(HES, hg + 1)
                                    v.tensor_tensor(
                                        pb[:, R_H + ii * 512:
                                           R_H + (ii + 1) * 512],
                                        pb[:, R_H + ii * 512:
                                           R_H + (ii + 1) * 512],
                                        pb[:, R_WB + (m % 2) * 512:
                                           R_WB + (m % 2) * 512 + 512],
                                        ALU.mult).then_inc(HGS, 1)
                            for tt2 in range(4):
                                j = m * 4 + tt2
                                v.wait_ge(YTS, (m + 1) * 32)
                                if j >= 2:
                                    v.wait_ge(CCS2, 16 * (j - 1))
                                if e == 8:
                                    v.scalar_tensor_tensor(
                                        pb[:, R_Y + (j % 2) * 1024:
                                           R_Y + (j % 2) * 1024 + 1024],
                                        psY[tt2][:, 0:1024], 1.0,
                                        pb[:, R_BE2B:R_BE2B + 1024],
                                        ALU.mult, ALU.add).then_inc(YES, 1)
                                else:
                                    v.tensor_copy(
                                        pb[:, R_Y + (j % 2) * 1024:
                                           R_Y + (j % 2) * 1024 + 1024],
                                        psY[tt2][:, 0:1024]).then_inc(YES, 1)
                    for tt in range(16):
                        j = 144 + tt
                        v.wait_ge(YTS, 36 * 32 + tt + 1)
                        if j >= 2:
                            v.wait_ge(CCS2, 16 * (j - 1))
                        v.tensor_copy(pb[:, R_Y + (j % 2) * 1024:
                                         R_Y + (j % 2) * 1024 + 1024],
                                      psY0[:, 0:1024]).then_inc(YES, 1)

                @block.gpsimd
                def _(g):
                    for tt in range(16):
                        j = 144 + tt
                        g.wait_ge(YES, j + 1)
                        g.dma_start(out=accD[tt * 128:(tt + 1) * 128, :],
                                    in_=pb[:, R_Y + (j % 2) * 1024:
                                           R_Y + (j % 2) * 1024 + 1024],
                                    accum_op=ALU.add).then_inc(CCS2, 16)
                        ct.add("CCS2", 16)

                @block.sync
                def _(sy):
                    sy.wait_ge(CCS2, ct.v("CCS2"))
                    sy.wait_ge(OS, ct.v("OS"))

def phase_c2(nc, io):
    xnfm1 = io["xnfm1"]
    wqT, wkT, wvT = io["q1T"], io["k1T"], io["v1T"]
    wbT, bb_c = io["wbT"], io["bb_c"]
    cosf, rsf = io["cosf"], io["rsf"]
    swapb, selb, oneh16, identb = (io["swapb"], io["selb"], io["oneh16"],
                                   io["identb"])
    qh, kh, khtm, vsb, aalld = (io["qh"], io["kh"], io["khtm"], io["vsb"],
                                io["aalld"])
    summ = io["sumcat"][:, 0:4160]
    sump = io["sumcat"][:, 4160:4161]

    R_XN = 0
    R_KT = 16384
    R_V = 32768
    R_W = 49408
    R_PB = 65792
    R_RT = 67840
    R_HB = 69376
    R_SW = 71424
    R_SEL = 71552
    R_OH = 72576
    R_AB = 73600
    R_SC2 = 77696
    PBN = 79744
    F_BT = 0
    F_AA = 2048
    F_RI = 4096
    F_AE = 6144
    F_S = 6416
    F_CS = 6560
    PFN = 10656

    ct = Ct()
    import contextlib
    with contextlib.ExitStack() as _st:
        E = _st.enter_context
        pb = E(nc.sbuf_tensor("Epb", [128, PBN], BF16))
        pf = E(nc.sbuf_tensor("Epf", [128, PFN], F32))
        pfoh = E(nc.sbuf_tensor("Epfoh", [16, 1024], F32))
        idb = E(nc.sbuf_tensor("Eidb", [128, 128], BF16))
        psA = E(nc.psum_tensor("EpsA", [128, 512], F32))
        psB = E(nc.psum_tensor("EpsB", [128, 512], F32))
        psC = E(nc.psum_tensor("EpsC", [128, 512], F32))
        psD = E(nc.psum_tensor("EpsD", [128, 512], F32))
        psS = E(nc.psum_tensor("EpsS", [64, 512], F32))
        psT0 = E(nc.psum_tensor("EpsT0", [128, 128], BF16))
        psT1 = E(nc.psum_tensor("EpsT1", [128, 128], BF16))
        LD = E(nc.semaphore("ELD"))
        GS = E(nc.semaphore("EGS"))
        VS = E(nc.semaphore("EVS"))
        SS = E(nc.semaphore("ESS"))
        TS = E(nc.semaphore("ETS"))
        OS = E(nc.semaphore("EOS"))
        block = E(nc.Block())

        @block.sync
        def _(sy):
            sy.dma_start(out=idb[:], in_=identb).then_inc(LD, 16)
            sy.dma_start(out=pf[:, 6560:8608], in_=cosf).then_inc(LD, 16)
            sy.dma_start(out=pf[:, 8608:10656], in_=rsf).then_inc(LD, 16)
            sy.dma_start(out=pf[0:16, 6547:6548], in_=bb_c).then_inc(LD, 16)
            sy.dma_start(out=pfoh[0:16, :], in_=oneh16).then_inc(LD, 16)
            ct.add("LD", 80)
        LD_c = ct.v("LD")

        @block.gpsimd
        def _(g):
            g.dma_start(out=pb[:, R_XN:R_XN + 16384], in_=xnfm1).then_inc(GS, 16)
            g.dma_start(out=pb[:, R_SW:R_SW + 128], in_=swapb).then_inc(GS, 16)
            g.dma_start(out=pb[0:16, R_SEL:R_SEL + 1024], in_=selb).then_inc(GS, 16)
            for fc in range(8):
                g.dma_start(out=pb[:, R_RT + fc * 16:R_RT + (fc + 1) * 16],
                            in_=wbT[fc * 128:(fc + 1) * 128, :]).then_inc(GS, 16)
            ct.add("GS", 11 * 16)
        GS_b = ct.v("GS")

        @block.tensor
        def _(t):
            t.wait_ge(GS, GS_b)
            t.wait_ge(LD, LD_c)
            ssb = ct.v("SS")
            for nb in range(4):
                for fc in range(8):
                    inst = t.matmul(
                        psS[0:16, :],
                        pb[:, R_RT + fc * 16:R_RT + (fc + 1) * 16],
                        pb[:, R_XN + fc * 2048 + nb * 512:
                           R_XN + fc * 2048 + nb * 512 + 512],
                        start=(fc == 0), stop=(fc == 7))
                inst.then_inc(TS, 1)
                ct.add("TS")
                ct.c["TS_bt_%d" % nb] = ct.v("TS")
                if nb < 3:
                    t.wait_ge(SS, ssb + nb + 1)
        ct.c["SS_bt0"] = ct.v("SS")

        @block.scalar
        def _(s):
            Sc = mk_auto(ct, "SS", SS, s)
            for nb in range(4):
                s.wait_ge(TS, ct.v("TS_bt_%d" % nb))
                Sc(lambda nb=nb: s.activation(
                    pf[0:16, F_BT + nb * 512:F_BT + (nb + 1) * 512],
                    psS[0:16, :], AF.Sigmoid, bias=pf[0:16, 6547:6548],
                    scale=1.0))
        SS_bt = ct.v("SS")

        @block.vector
        def _(v):
            V = mk_auto(ct, "VS", VS, v)
            v.wait_ge(SS, SS_bt)
            V(lambda: v.tensor_scalar(pf[0:16, F_BT:F_BT + 2048],
                                      pf[0:16, F_BT:F_BT + 2048], 0.999, None,
                                      ALU.min))
            V(lambda: v.tensor_scalar(pf[0:16, F_BT:F_BT + 2048],
                                      pf[0:16, F_BT:F_BT + 2048], 0.8, None,
                                      ALU.max))
            for c in range(16):
                V(lambda c=c: v.tensor_tensor_scan(
                    pf[0:16, F_AA + c * 128:F_AA + (c + 1) * 128],
                    pf[0:16, F_BT + c * 128:F_BT + (c + 1) * 128],
                    pf[0:16, F_BT + c * 128:F_BT + (c + 1) * 128],
                    1.0, ALU.mult, ALU.bypass))
            V(lambda: v.tensor_copy(pf[0:16, 6144:6160],
                                    pf[0:16, F_AA:F_AA + 2048]
                                    .rearrange("p (c u) -> p c u", u=128)
                                    [:, :, 127:128].rearrange("p c u -> p (c u)")))
            V(lambda: v.reciprocal(pf[0:16, F_RI:F_RI + 2048],
                                   pf[0:16, F_AA:F_AA + 2048]))
            V(lambda: v.tensor_copy(pb[0:16, R_AB:R_AB + 2048],
                                    pf[0:16, F_AA:F_AA + 2048]))
            V(lambda: v.tensor_copy(pb[0:16, R_AB + 2048:R_AB + 4096],
                                    pf[0:16, F_RI:F_RI + 2048]))
        VS_a = ct.v("VS")

        @block.sync
        def _(sy):
            sy.wait_ge(VS, VS_a)
            sy.dma_start(out=aalld, in_=pf[0:16, F_AA:F_AA + 2048]).then_inc(OS, 16)
            ct.add("OS", 16)

        if C2_PHASES <= 1:
            @block.sync
            def _(sy):
                sy.wait_ge(OS, ct.v("OS"))
                sy.wait_ge(GS, ct.v("GS"))
            return nc

        VS_acbbase = ct.v("VS")

        @block.tensor
        def _(t):
            t.wait_ge(VS, VS_a)
            for h in range(16):
                if h >= 1:
                    t.wait_ge(VS, VS_acbbase + h)
                t.matmul(psS[0:64, 0:16],
                         pfoh[0:16, h * 64:(h + 1) * 64],
                         pf[0:16, 6144:6160], start=True,
                         stop=True).then_inc(TS, 1)
                ct.add("TS")
                ct.c["TS_acb_%d" % h] = ct.v("TS")

        @block.vector
        def _(v):
            V = mk_auto(ct, "VS", VS, v)
            for h in range(16):
                v.wait_ge(TS, ct.v("TS_acb_%d" % h))
                V(lambda h=h: v.tensor_copy(pf[0:64, 6160 + h * 16:
                                               6160 + (h + 1) * 16],
                                            psS[0:64, 0:16]))
        VS_acb = ct.v("VS")

        if C2_PHASES <= 2:
            @block.sync
            def _(sy):
                sy.wait_ge(OS, ct.v("OS"))
                sy.wait_ge(GS, ct.v("GS"))
            return nc

        @block.vector
        def _(v):
            V = mk_auto(ct, "VS", VS, v)
            V(lambda: v.memset(
                pb[:, R_V:R_V + 16640].rearrange("p (h c e) -> p h c e",
                                                 c=16, e=65)
                [:, :, :, 64:65], 1.0))
        VS_vones = ct.v("VS")

        @block.gpsimd
        def _(g):
            for fc in range(8):
                g.dma_start(out=pb[:, R_W + 8192 + fc * 1024:
                               R_W + 8192 + (fc + 1) * 1024],
                            in_=wvT[fc * 128:(fc + 1) * 128, :]).then_inc(GS, 16)
            ct.add("GS", 8 * 16)
        GS_wv = ct.v("GS")

        for c in range(16):
            @block.tensor
            def _(t, c=c):
                t.wait_ge(GS, GS_wv)
                if c == 0:
                    t.wait_ge(VS, VS_vones)
                else:
                    t.wait_ge(VS, ct.v("VS_v_%d" % (c - 1)))
                for cb in range(2):
                    ps = psA if cb == 0 else psB
                    for fc in range(8):
                        inst = t.matmul(
                            ps[:],
                            pb[:, R_XN + fc * 2048 + c * 128:
                               R_XN + fc * 2048 + c * 128 + 128],
                            pb[:, R_W + 8192 + fc * 1024 + cb * 512:
                               R_W + 8192 + fc * 1024 + (cb + 1) * 512],
                            start=(fc == 0), stop=(fc == 7))
                    inst.then_inc(TS, 1)
                    ct.add("TS")
                    ct.c["TS_v_%d_%d" % (c, cb)] = ct.v("TS")

            @block.vector
            def _(v, c=c):
                V = mk_auto(ct, "VS", VS, v)
                for cb in range(2):
                    ps = psA if cb == 0 else psB
                    v.wait_ge(TS, ct.v("TS_v_%d_%d" % (c, cb)))
                    V(lambda ps=ps, cb=cb, c=c: v.tensor_copy(
                        pb[:, R_V:R_V + 16640]
                        .rearrange("p (h ce) -> p h ce", ce=1040)
                        [:, cb * 8:(cb + 1) * 8, c * 65:c * 65 + 64],
                        ps[:].rearrange("p (h e) -> p h e", e=64)))
                ct.c["VS_v_%d" % c] = ct.v("VS")
        VS_v = ct.v("VS")

        @block.sync
        def _(sy):
            sy.wait_ge(VS, VS_v)
            sy.dma_start(out=vsb, in_=pb[:, R_V:R_V + 16640]).then_inc(OS, 16)
            ct.add("OS", 16)

        if C2_PHASES <= 3:
            @block.sync
            def _(sy):
                sy.wait_ge(OS, ct.v("OS"))
                sy.wait_ge(GS, ct.v("GS"))
            return nc

        @block.gpsimd
        def _(g):
            for fc in range(8):
                g.dma_start(out=pb[:, R_W + fc * 1024:R_W + (fc + 1) * 1024],
                            in_=wqT[fc * 128:(fc + 1) * 128, :]).then_inc(GS, 16)
            ct.add("GS", 8 * 16)
            ct.c["GS_wq"] = ct.v("GS")
            g.wait_ge(TS, ct.v("TS_v_15_1"))
            for fc in range(8):
                g.dma_start(out=pb[:, R_W + 8192 + fc * 1024:
                               R_W + 8192 + (fc + 1) * 1024],
                            in_=wkT[fc * 128:(fc + 1) * 128, :]).then_inc(GS, 16)
            ct.add("GS", 8 * 16)
            ct.c["GS_wk"] = ct.v("GS")

        OS_qk0 = ct.v("OS")
        ct.c["SS_pj0"] = ct.v("SS")
        for qk in range(2):
            wbase = R_W if qk == 0 else R_W + 8192
            scbase = R_AB if qk == 0 else R_AB + 2048
            dout = qh if qk == 0 else kh
            for qc in range(8):
                @block.tensor
                def _(t, qk=qk, qc=qc, wbase=wbase):
                    t.wait_ge(GS, ct.v("GS_wq" if qk == 0 else "GS_wk"))
                    t.wait_ge(VS, VS_acb)
                    if qk == 0 and qc == 0:
                        t.wait_ge(VS, VS_v)
                    for nb in range(4):
                        j = (qk * 8 + qc) * 4 + nb
                        ps = psA if j % 2 == 0 else psB
                        if j >= 2:
                            jj = j - 2
                            if jj < 32:
                                tgt = (jj // 4) * 8 + (jj % 4) + 1
                            else:
                                tgt = 64 + ((jj - 32) // 4) * 40 + (jj % 4) + 1
                            t.wait_ge(SS, ct.v("SS_pj0") + tgt)
                        for fc in range(8):
                            inst = t.matmul(
                                ps[:],
                                pb[:, wbase + fc * 1024 + qc * 128:
                                   wbase + fc * 1024 + qc * 128 + 128],
                                pb[:, R_XN + fc * 2048 + nb * 512:
                                   R_XN + fc * 2048 + nb * 512 + 512],
                                start=(fc == 0), stop=(fc == 7))
                        inst.then_inc(TS, 1)
                        ct.add("TS")
                        ct.c["TS_pj_%d" % j] = ct.v("TS")

                @block.scalar
                def _(s, qk=qk, qc=qc):
                    Sc = mk_auto(ct, "SS", SS, s)
                    for nb in range(4):
                        j = (qk * 8 + qc) * 4 + nb
                        ps = psA if j % 2 == 0 else psB
                        s.wait_ge(TS, ct.v("TS_pj_%d" % j))
                        if j >= 4:
                            s.wait_ge(VS, ct.v("VS_rp_%d" % (j - 4)))
                        Sc(lambda ps=ps, nb=nb: s.activation(
                            pb[:, R_PB + nb * 512:R_PB + (nb + 1) * 512],
                            ps[:], AF.Copy, bias=0.0, scale=1.0))
                        ct.c["SS_pj_%d" % j] = ct.v("SS")

                for nb in range(4):
                    j = (qk * 8 + qc) * 4 + nb

                    @block.tensor
                    def _(t, j=j, nb=nb, qc=qc, scbase=scbase):
                        t.wait_ge(SS, ct.v("SS_pj_%d" % j))
                        if j >= 1:
                            t.wait_ge(VS, ct.v("VS_rot_%d" % (j - 1)))
                        t.matmul(psC[:], pb[:, R_SW:R_SW + 128],
                                 pb[:, R_PB + nb * 512:R_PB + (nb + 1) * 512],
                                 start=True, stop=True).then_inc(TS, 1)
                        ct.add("TS")
                        ct.c["TS_rot_%d" % j] = ct.v("TS")
                        if j >= 1:
                            t.wait_ge(VS, ct.v("VS_ab_%d" % (j - 1)))
                        t.matmul(psD[:],
                                 pb[0:16, R_SEL + qc * 128:
                                    R_SEL + (qc + 1) * 128],
                                 pb[0:16, scbase + nb * 512:
                                    scbase + (nb + 1) * 512],
                                 start=True, stop=True).then_inc(TS, 1)
                        ct.add("TS")
                        ct.c["TS_abc_%d" % j] = ct.v("TS")

                    @block.vector
                    def _(v, j=j, nb=nb, qk=qk, qc=qc):
                        V = mk_auto(ct, "VS", VS, v)
                        RT0, T1, RP = R_RT, R_RT + 512, R_RT + 1024
                        MN, MX, EE, AB = (R_SC2, R_SC2 + 512, R_SC2 + 1024,
                                          R_SC2 + 1536)
                        v.wait_ge(TS, ct.v("TS_rot_%d" % j))
                        V(lambda: v.tensor_copy(pb[:, RT0:RT0 + 512], psC[:]))
                        ct.c["VS_rot_%d" % j] = ct.v("VS")
                        V(lambda: v.tensor_tensor(
                            pb[:, T1:T1 + 512],
                            pb[:, R_PB + nb * 512:R_PB + (nb + 1) * 512],
                            pf[:, 6560 + nb * 512:6560 + (nb + 1) * 512],
                            ALU.mult))
                        V(lambda: v.tensor_tensor(
                            pb[:, RP:RP + 512], pb[:, RT0:RT0 + 512],
                            pf[:, 8608 + nb * 512:8608 + (nb + 1) * 512],
                            ALU.mult))
                        V(lambda: v.tensor_add(pb[:, RP:RP + 512],
                                               pb[:, RP:RP + 512],
                                               pb[:, T1:T1 + 512]))
                        V(lambda: v.tensor_scalar(pb[:, MN:MN + 512],
                                                  pb[:, RP:RP + 512], 0.0,
                                                  None, ALU.min))
                        V(lambda: v.tensor_scalar(pb[:, MX:MX + 512],
                                                  pb[:, RP:RP + 512], 0.0,
                                                  None, ALU.max))
                        ct.c["VS_mn_%d" % j] = ct.v("VS")

                    @block.scalar
                    def _(s, j=j):
                        Sc = mk_auto(ct, "SS", SS, s)
                        MN, EE = R_SC2, R_SC2 + 1024
                        s.wait_ge(VS, ct.v("VS_mn_%d" % j))
                        Sc(lambda: s.activation(pb[:, EE:EE + 512],
                                                pb[:, MN:MN + 512], AF.Exp,
                                                bias=0.0, scale=1.0))
                    ct.c["SS_ee_%d" % j] = ct.v("SS")

                    @block.vector
                    def _(v, j=j, nb=nb, qk=qk, qc=qc):
                        V = mk_auto(ct, "VS", VS, v)
                        MX, EE, AB = R_SC2 + 512, R_SC2 + 1024, R_SC2 + 1536
                        v.wait_ge(SS, ct.v("SS_ee_%d" % j))
                        V(lambda: v.tensor_add(pb[:, EE:EE + 512],
                                               pb[:, EE:EE + 512],
                                               pb[:, MX:MX + 512]))
                        v.wait_ge(TS, ct.v("TS_abc_%d" % j))
                        V(lambda: v.tensor_copy(pb[:, AB:AB + 512], psD[:]))
                        ct.c["VS_ab_%d" % j] = ct.v("VS")
                        if qk == 1 and nb == 0 and qc > 0:
                            v.wait_ge(TS, ct.v("TS_kt_%d" % (qc - 1)))
                        if nb == 0:
                            v.wait_ge(OS, OS_qk0 + 16 * (qk * 8 + qc))
                        V(lambda: v.tensor_tensor(
                            pb[:, R_HB + nb * 512:R_HB + (nb + 1) * 512],
                            pb[:, EE:EE + 512], pb[:, AB:AB + 512], ALU.mult))
                        ct.c["VS_rp_%d" % j] = ct.v("VS")

                @block.sync
                def _(sy, qk=qk, qc=qc, dout=dout):
                    sy.wait_ge(VS, ct.v("VS_rp_%d" % ((qk * 8 + qc) * 4 + 3)))
                    sy.dma_start(out=dout[:, qc * 2048:(qc + 1) * 2048],
                                 in_=pb[:, R_HB:R_HB + 2048]).then_inc(OS, 16)
                    ct.add("OS", 16)

                if qk == 1:
                    if qc == 0:
                        ct.c["SS_qk1base"] = ct.v("SS") - 8

                    @block.tensor
                    def _(t, qc=qc):
                        t.wait_ge(VS, ct.v("VS_rp_%d" % ((8 + qc) * 4 + 3)))
                        for c in range(16):
                            pt = psT0 if c % 2 == 0 else psT1
                            e = qc * 16 + c
                            if e >= 2:
                                e2 = e - 2
                                t.wait_ge(SS, ct.v("SS_qk1base") +
                                          (e2 // 16) * 40 + 8 +
                                          2 * ((e2 % 16) + 1))
                            t.transpose(pt[:],
                                        pb[:, R_HB + c * 128:R_HB + (c + 1) * 128],
                                        idb[:]).then_inc(TS, 1)
                            ct.add("TS")
                            ct.c["TS_ktt_%d" % (qc * 16 + c)] = ct.v("TS")
                        ct.c["TS_kt_%d" % qc] = ct.v("TS")

                    @block.scalar
                    def _(s, qc=qc):
                        Sc = mk_auto(ct, "SS", SS, s)
                        for c in range(16):
                            pt = psT0 if c % 2 == 0 else psT1
                            s.wait_ge(TS, ct.v("TS_ktt_%d" % (qc * 16 + c)))
                            Sc(lambda pt=pt, c=c, qc=qc: s.activation(
                                pb[:, R_KT + (2 * qc) * 1024 + c * 64:
                                   R_KT + (2 * qc) * 1024 + c * 64 + 64],
                                pt[:, 0:64], AF.Copy, bias=0.0, scale=1.0))
                            Sc(lambda pt=pt, c=c, qc=qc: s.activation(
                                pb[:, R_KT + (2 * qc + 1) * 1024 + c * 64:
                                   R_KT + (2 * qc + 1) * 1024 + c * 64 + 64],
                                pt[:, 64:128], AF.Copy, bias=0.0, scale=1.0))
        SS_kt = ct.v("SS")

        @block.sync
        def _(sy):
            sy.wait_ge(SS, SS_kt)
            sy.dma_start(out=khtm, in_=pb[:, R_KT:R_KT + 16384]).then_inc(OS, 16)
            ct.add("OS", 16)

        if C2_PHASES <= 4:
            @block.sync
            def _(sy):
                sy.wait_ge(OS, ct.v("OS"))
                sy.wait_ge(GS, ct.v("GS"))
            return nc

        @block.vector
        def _(v):
            V = mk_auto(ct, "VS", VS, v)
            V(lambda: v.memset(pf[0:16, 6546:6547], 1.0))
        VS_pr = ct.v("VS")

        VS_subase = ct.v("VS")

        def vs_su(m):
            return VS_subase + (m // 16) * 33 + 1 + ((m % 16) + 1) * 2

        @block.tensor
        def _(t):
            t.wait_ge(SS, SS_kt)
            t.wait_ge(VS, VS_v)
            for h in range(16):
                qc, dd = h // 2, (h % 2) * 64
                for c in range(16):
                    if h * 16 + c >= 1:
                        t.wait_ge(VS, vs_su(h * 16 + c - 1))
                    t.matmul(psS[0:64, 0:65],
                             pb[:, R_KT + h * 1024 + c * 64:
                                R_KT + h * 1024 + c * 64 + 64],
                             pb[:, R_V + h * 1040 + c * 65:
                                R_V + h * 1040 + (c + 1) * 65],
                             start=True, stop=True).then_inc(TS, 1)
                    ct.add("TS")
                    ct.c["TS_su_%d" % (h * 16 + c)] = ct.v("TS")

        OS_su0 = ct.v("OS")

        @block.vector
        def _(v):
            V = mk_auto(ct, "VS", VS, v)
            for h in range(16):
                if h >= 1:
                    v.wait_ge(OS, OS_su0 + 16 * h)
                V(lambda: v.memset(pf[0:64, 6416:6481], 0.0))
                for c in range(16):
                    v.wait_ge(TS, ct.v("TS_su_%d" % (h * 16 + c)))
                    V(lambda h=h, c=c: v.tensor_scalar(
                        pf[0:64, 6481:6546], pf[0:64, 6416:6481],
                        pf[0:64, 6160 + h * 16 + c:6160 + h * 16 + c + 1],
                        None, ALU.mult))
                    V(lambda h=h, c=c: v.scalar_tensor_tensor(
                        pf[0:64, 6416:6481], psS[0:64, 0:65],
                        pf[0:64, 6160 + h * 16 + c:6160 + h * 16 + c + 1],
                        pf[0:64, 6481:6546], ALU.mult, ALU.add))
                    ct.c["VS_su_%d" % (h * 16 + c)] = ct.v("VS")
                ct.c["VS_sum_%d" % h] = ct.v("VS")
            V(lambda: v.tensor_copy(pf[0:16, 6546:6547],
                                    pf[0:16, 6144:6145]))
            for c in range(1, 16):
                V(lambda c=c: v.tensor_tensor(pf[0:16, 6546:6547],
                                              pf[0:16, 6546:6547],
                                              pf[0:16, 6144 + c:6145 + c],
                                              ALU.mult))
        VS_sum = ct.v("VS")

        @block.sync
        def _(sy):
            for h in range(16):
                sy.wait_ge(VS, ct.v("VS_sum_%d" % h))
                sy.dma_start(out=summ[h:h + 1, :]
                             .rearrange("o (d e) -> (o d) e", e=65),
                             in_=pf[0:64, 6416:6481]).then_inc(OS, 16)
                ct.add("OS", 16)
            sy.wait_ge(VS, VS_sum)
            sy.dma_start(out=sump, in_=pf[0:16, 6546:6547],
                         allow_slow_non_contiguous=True).then_inc(OS, 16)
            ct.add("OS", 16)
            sy.wait_ge(OS, ct.v("OS"))




def phase_d(nc, io):
    accDi, xntm1 = io["accD"], io["xntm1"]
    qh, kh, khtm, vsb, aalld = (io["qh"], io["kh"], io["khtm"], io["vsb"],
                                io["aalld"])
    sumcatG, selw = io["sumcatG"], io["selw"]
    wo1T, bo1_r, sc1_r = io["wo1T"], io["bo1_r"], io["sc1_r"]
    causal, shiftb, oneh16 = io["causal"], io["shiftb"], io["oneh16"]
    outf = io["outf"]
    if DEBUG_B:
        ddat = dhp0 = drst = None

    R_AT = 0
    R_WO = 16384
    R_CA = 24576
    R_SW = 24704
    R_SB = 24832
    R_TMP = 24960
    R_XT = 25216
    R_JK = 27264
    PBN = 28288
    F_SL = 0
    F_SI = 3120
    F_AA = 4160
    F_AE = 6208
    F_W = 6528
    F_B = 6864
    F_HP = 8912
    F_AC = 9936
    F_ON = 11984
    PFN = 12112

    ct = Ct()
    import contextlib
    with contextlib.ExitStack() as _st:
        E = _st.enter_context
        pb = E(nc.sbuf_tensor("Dpb", [128, PBN], BF16))
        pf = E(nc.sbuf_tensor("Dpf", [128, PFN], F32))
        cst = E(nc.sbuf_tensor("Dcst", [128, 8], F32))
        nc.const_aps.register(F32, 1e-6, cst[:, 0:1])
        pfoh = E(nc.sbuf_tensor("Dpfoh", [16, 1024], F32))
        qhb = E(nc.sbuf_tensor("Dqhb", [64, 2 * 2048], BF16))
        khb = E(nc.sbuf_tensor("Dkhb", [64, 2 * 2048], BF16))
        ktb = E(nc.sbuf_tensor("Dktb", [128, 2 * 1024], BF16))
        vhb = E(nc.sbuf_tensor("Dvhb", [128, 2 * 1040], BF16))
        rb64 = E(nc.sbuf_tensor("Drb64", [64, 128], F32))
        sgth = E(nc.sbuf_tensor("Dsgth", [64, 8320], F32))
        sgtp = E(nc.sbuf_tensor("Dsgtp", [16, 8], F32))
        selwb = E(nc.sbuf_tensor("Dselwb", [64, 24], F32))
        psA = E(nc.psum_tensor("DpsA", [128, 512], F32))
        psC = E(nc.psum_tensor("DpsC", [128, 512], F32))
        psD = E(nc.psum_tensor("DpsD", [128, 512], F32))
        psS = E(nc.psum_tensor("DpsS", [64, 512], F32))
        psE = E(nc.psum_tensor("DpsE", [128, 512], F32))
        psF = E(nc.psum_tensor("DpsF", [128, 512], F32))
        LD = E(nc.semaphore("DLD"))
        GS = E(nc.semaphore("DGS"))
        VS = E(nc.semaphore("DVS"))
        SS = E(nc.semaphore("DSS"))
        TS = E(nc.semaphore("DTS"))
        HLD = E(nc.semaphore("DHLD"))
        OS = E(nc.semaphore("DOS"))
        block = E(nc.Block())

        @block.sync
        def _(sy):
            sy.dma_start(out=sgth[:].rearrange("d (m e) -> d m e", e=65),
                         in_=sumcatG[:, 0:4160]
                         .rearrange("m (d e) -> d m e", e=65)).then_inc(LD, 16)
            sy.dma_start(out=sgtp[:],
                         in_=sumcatG[:, 4160:4161]
                         .rearrange("(c h) o -> h (c o)", h=16),
                         allow_slow_non_contiguous=True).then_inc(LD, 16)
            sy.dma_start(out=selwb[:], in_=selw).then_inc(LD, 16)
            sy.dma_start(out=pf[0:16, F_AA:F_AA + 2048][0:16, :],
                         in_=aalld).then_inc(LD, 16)
            sy.dma_start(out=pfoh[:], in_=oneh16).then_inc(LD, 16)
            sy.dma_start(out=pb[:, R_CA:R_CA + 128], in_=causal).then_inc(LD, 16)
            sy.dma_start(out=pb[0:64, R_SW:R_SW + 128], in_=shiftb).then_inc(LD, 16)
            sy.dma_start(out=pf[0:1, F_B:F_B + 1024], in_=bo1_r).then_inc(LD, 16)
            sy.dma_start(out=pf[0:1, F_B + 1024:F_B + 2048],
                         in_=sc1_r).then_inc(LD, 16)
            ct.add("LD", 9 * 16)
        LD_c = ct.v("LD")

        @block.gpsimd
        def _(g):
            for fc in range(8):
                g.dma_start(out=pb[:, R_WO + fc * 1024:R_WO + (fc + 1) * 1024],
                            in_=wo1T[fc * 128:(fc + 1) * 128, :]).then_inc(GS, 16)
            ct.add("GS", 8 * 16)
        GS_wo = ct.v("GS")

        @block.vector
        def _(v):
            V = mk_auto(ct, "VS", VS, v)
            v.wait_ge(LD, LD_c)
            V(lambda: v.memset(cst[:, 0:1], 1e-6))
            V(lambda: v.memset(pf[0:1, F_ON:F_ON + 128], 1.0))
            V(lambda: v.memset(pf[64:65, F_ON + 64:F_ON + 128], 1.0))
            V(lambda: v.tensor_copy(pf[0:16, 6208:6224],
                                    pf[0:16, F_AA:F_AA + 2048]
                                    .rearrange("p (c u) -> p c u", u=128)
                                    [:, :, 127:128]
                                    .rearrange("p c u -> p (c u)")))
            for k in (1, 2, 3):
                base = F_SL + (k - 1) * 1040
                pcol = 6208 + 320 + (k - 1)
                for c in range(8):
                    idx = (k - 1) * 8 + c
                    if c == 0:
                        V(lambda base=base, c=c, idx=idx: v.tensor_scalar(
                            pf[0:64, base:base + 1040],
                            sgth[0:64, c * 1040:(c + 1) * 1040],
                            selwb[0:64, idx:idx + 1], None, ALU.mult))
                        V(lambda pcol=pcol, c=c, idx=idx: v.tensor_scalar(
                            pf[0:16, pcol:pcol + 1], sgtp[0:16, c:c + 1],
                            selwb[0:16, idx:idx + 1], None, ALU.mult))
                    else:
                        V(lambda base=base, c=c, idx=idx:
                          v.scalar_tensor_tensor(
                              pf[0:64, base:base + 1040],
                              sgth[0:64, c * 1040:(c + 1) * 1040],
                              selwb[0:64, idx:idx + 1],
                              pf[0:64, base:base + 1040],
                              ALU.mult, ALU.add))
                        V(lambda pcol=pcol, c=c, idx=idx:
                          v.scalar_tensor_tensor(
                              pf[0:16, pcol:pcol + 1], sgtp[0:16, c:c + 1],
                              selwb[0:16, idx:idx + 1],
                              pf[0:16, pcol:pcol + 1],
                              ALU.mult, ALU.add))
        VS_i = ct.v("VS")

        VS_acbbase = ct.v("VS")

        @block.tensor
        def _(t):
            t.wait_ge(VS, VS_i)
            for h in range(16):
                if h >= 1:
                    t.wait_ge(VS, VS_acbbase + 2 * h)
                t.matmul(psS[0:64, 0:16], pfoh[0:16, h * 64:(h + 1) * 64],
                         pf[0:16, 6208:6224], start=True,
                         stop=True).then_inc(TS, 1)
                ct.add("TS")
                t.matmul(psS[0:64, 16:19], pfoh[0:16, h * 64:(h + 1) * 64],
                         pf[0:16, 6528:6531] if False
                         else pf[0:16, 6208 + 320:6208 + 323],
                         start=True, stop=True).then_inc(TS, 1)
                ct.add("TS")
                ct.c["TS_acb_%d" % h] = ct.v("TS")

        @block.vector
        def _(v):
            V = mk_auto(ct, "VS", VS, v)
            for h in range(16):
                v.wait_ge(TS, ct.v("TS_acb_%d" % h))
                V(lambda h=h: v.tensor_copy(
                    pf[0:64, 6224 + h * 16:6224 + (h + 1) * 16],
                    psS[0:64, 0:16]))
                V(lambda h=h: v.tensor_copy(
                    pf[0:64, 6480 + h * 3:6480 + (h + 1) * 3],
                    psS[0:64, 16:19]))
                ct.c["VS_acb_%d" % h] = ct.v("VS")
            for h in range(16):
                si = pf[0:64, F_SI + h * 65:F_SI + (h + 1) * 65]
                L = [pf[0:64, F_SL + (k * 16 + h) * 65:
                        F_SL + (k * 16 + h) * 65 + 65] for k in range(3)]
                V(lambda si=si, L=L: v.tensor_copy(si, L[2]))
                for k in (2, 1):
                    V(lambda si=si, L=L, k=k, h=h: v.scalar_tensor_tensor(
                        si, si, pf[0:64, 6480 + h * 3 + k - 1:
                                   6480 + h * 3 + k], L[k - 1],
                        ALU.mult, ALU.add))
        VS_sin = ct.v("VS")

        if D_PHASES <= 1:
            @block.sync
            def _(sy):
                sy.wait_ge(OS, ct.v("OS"))
                sy.wait_ge(GS, ct.v("GS"))
            return nc

        def VP(m, k):
            return (m // 16) * 145 + 1 + (m % 16) * 9 + k

        ATS = E(nc.semaphore("DATS"))
        OTS = E(nc.semaphore("DOTS"))
        BTS = E(nc.semaphore("DBTS"))
        SHS = E(nc.semaphore("DSHS"))
        ADS = E(nc.semaphore("DADS"))
        DVS = E(nc.semaphore("DDVS"))

        @block.sync
        def _(sy):
            for h in range(16):
                qc, dd = h // 2, (h % 2) * 64
                if h >= 2:
                    sy.wait_ge(ADS, (h - 1) * 16)
                sy.dma_start(out=qhb[:, (h % 2) * 2048:(h % 2) * 2048 + 2048],
                             in_=qh[dd:dd + 64, qc * 2048:(qc + 1) * 2048]
                             ).then_inc(HLD, 16)
                sy.dma_start(out=khb[:, (h % 2) * 2048:(h % 2) * 2048 + 2048],
                             in_=kh[dd:dd + 64, qc * 2048:(qc + 1) * 2048]
                             ).then_inc(HLD, 16)
                sy.dma_start(out=ktb[:, (h % 2) * 1024:(h % 2) * 1024 + 1024],
                             in_=khtm[:, h * 1024:(h + 1) * 1024]
                             ).then_inc(HLD, 16)
                sy.dma_start(out=vhb[:, (h % 2) * 1040:(h % 2) * 1040 + 1040],
                             in_=vsb[:, h * 1040:(h + 1) * 1040]
                             ).then_inc(HLD, 16)
                ct.add("HLD", 64)

        @block.tensor
        def _(t):
            t.wait_ge(VS, VS_sin)
            t.wait_ge(LD, LD_c)
            shs = 0
            for h in range(16):
                hb = h % 2
                t.wait_ge(HLD, 64 * (h + 1))
                for c in range(16):
                    m = h * 16 + c
                    if m >= 1:
                        t.wait_ge(DVS, VP(m - 1, 1))
                    t.matmul(psA[:, 0:128],
                             khb[:, hb * 2048 + c * 128:
                                 hb * 2048 + c * 128 + 128],
                             qhb[:, hb * 2048 + c * 128:
                                 hb * 2048 + c * 128 + 128],
                             start=True, stop=True).then_inc(ATS, 1)
                    if m >= 1:
                        t.wait_ge(DVS, VP(m - 1, 5))
                        t.wait_ge(DVS, (h * 145 + 1) if c == 0
                                  else VP(m - 1, 9))
                    else:
                        t.wait_ge(DVS, 1)
                    t.matmul(psC[0:65, 0:128], pb[0:64, R_SB:R_SB + 65],
                             qhb[:, hb * 2048 + c * 128:
                                 hb * 2048 + c * 128 + 128],
                             start=True, stop=False)
                    t.wait_ge(DVS, VP(m, 1))
                    t.matmul(psC[0:65, 0:128],
                             vhb[:, hb * 1040 + c * 65:hb * 1040 + (c + 1) * 65],
                             pb[:, R_TMP + 128:R_TMP + 256],
                             start=False, stop=True).then_inc(OTS, 1)
                    t.wait_ge(DVS, VP(m, 3))
                    if m >= 1:
                        t.wait_ge(DVS, VP(m - 1, 6))
                    t.matmul(psD[0:64, 0:128], pf[64:65, F_ON + 64:F_ON + 128],
                             pf[64:65, F_W + 128:F_W + 256],
                             start=True, stop=True).then_inc(BTS, 1)
                    if h % 2 == 1:
                        t.wait_ge(DVS, VP(m, 5))
                        t.matmul(psD[:, 0:128], pb[0:64, R_SW:R_SW + 128],
                                 pb[0:64, R_TMP:R_TMP + 128],
                                 start=True, stop=True).then_inc(SHS, 1)
                        shs += 1
                    if m >= 1:
                        t.wait_ge(DVS, VP(m - 1, 8))
                    t.matmul(psS[0:64, 0:65],
                             ktb[:, hb * 1024 + c * 64:hb * 1024 + c * 64 + 64],
                             vhb[:, hb * 1040 + c * 65:hb * 1040 + (c + 1) * 65],
                             start=True, stop=True).then_inc(ADS, 1)

        @block.vector
        def _(v):
            shs = 0
            dvc = [0]

            def DV(f):
                inst = f()
                inst.then_inc(DVS, 1)
                dvc[0] += 1
                v.wait_ge(DVS, dvc[0])
            for h in range(16):
                hb = h % 2
                qc = h // 2
                v.wait_ge(VS, VS_sin)
                DV(lambda h=h: v.tensor_copy(
                    pb[0:64, R_SB:R_SB + 65],
                    pf[0:64, F_SI + h * 65:F_SI + (h + 1) * 65]))
                for c in range(16):
                    m = h * 16 + c
                    v.wait_ge(ATS, m + 1)
                    DV(lambda: v.scalar_tensor_tensor(
                        pb[:, R_TMP + 128:R_TMP + 256], psA[:, 0:128], 1.0,
                        pb[:, R_CA:R_CA + 128], ALU.mult, ALU.mult))
                    v.wait_ge(OTS, m + 1)
                    DV(lambda: v.tensor_scalar(pf[64:65, F_W:F_W + 128],
                                               psC[64:65, 0:128], 1e-6, None,
                                               ALU.add))
                    DV(lambda: v.reciprocal(pf[64:65, F_W + 128:F_W + 256],
                                            pf[64:65, F_W:F_W + 128]))
                    v.wait_ge(BTS, m + 1)
                    DV(lambda: v.tensor_copy(rb64[:], psD[0:64, 0:128]))
                    if h % 2 == 0:
                        DV(lambda qc=qc, c=c: v.scalar_tensor_tensor(
                            pb[0:64, R_AT + qc * 2048 + c * 128:
                               R_AT + qc * 2048 + c * 128 + 128],
                            psC[0:64, 0:128], 1.0, rb64[:],
                            ALU.mult, ALU.mult))
                        DV(lambda: v.engine_nop())
                    else:
                        DV(lambda: v.scalar_tensor_tensor(
                            pb[0:64, R_TMP:R_TMP + 128],
                            psC[0:64, 0:128], 1.0, rb64[:],
                            ALU.mult, ALU.mult))
                        shs += 1
                        v.wait_ge(SHS, shs)
                        DV(lambda qc=qc, c=c: v.tensor_copy(
                            pb[64:128, R_AT + qc * 2048 + c * 128:
                               R_AT + qc * 2048 + c * 128 + 128],
                            psD[64:128, 0:128]))
                    DV(lambda h=h, c=c: v.tensor_scalar(
                        pf[0:64, F_W + 256:F_W + 321],
                        pf[0:64, F_SI + h * 65:F_SI + (h + 1) * 65],
                        pf[0:64, 6224 + h * 16 + c:6224 + h * 16 + c + 1],
                        None, ALU.mult))
                    v.wait_ge(ADS, m + 1)
                    DV(lambda h=h, c=c: v.scalar_tensor_tensor(
                        pf[0:64, F_SI + h * 65:F_SI + (h + 1) * 65],
                        psS[0:64, 0:65],
                        pf[0:64, 6224 + h * 16 + c:6224 + h * 16 + c + 1],
                        pf[0:64, F_W + 256:F_W + 321],
                        ALU.mult, ALU.add))
                    DV(lambda h=h: v.tensor_copy(
                        pb[0:64, R_SB:R_SB + 65],
                        pf[0:64, F_SI + h * 65:F_SI + (h + 1) * 65]))

        if D_PHASES <= 2:
            @block.sync
            def _(sy):
                sy.wait_ge(OS, ct.v("OS"))
                sy.wait_ge(GS, ct.v("GS"))
            return nc

        ct.c["VS_bcbase"] = ct.v("VS")

        @block.tensor
        def _(t):
            t.wait_ge(DVS, VP(255, 9))
            t.wait_ge(VS, VS_i)
            t.wait_ge(GS, GS_wo)
            if D_PHASES <= 2.3:
                return
            for cb in range(2):
                if cb == 1:
                    t.wait_ge(VS, ct.v("VS_bcbase") + 2)
                t.matmul(psE[:], pf[0:1, F_ON:F_ON + 128],
                         pf[0:1, F_B + cb * 512:F_B + (cb + 1) * 512],
                         start=True, stop=True).then_inc(TS, 1)
                ct.add("TS")
                t.matmul(psF[:], pf[0:1, F_ON:F_ON + 128],
                         pf[0:1, F_B + 1024 + cb * 512:
                            F_B + 1024 + (cb + 1) * 512],
                         start=True, stop=True).then_inc(TS, 1)
                ct.add("TS")
                ct.c["TS_bc_%d" % cb] = ct.v("TS")

        @block.vector
        def _(v):
            if D_PHASES <= 2.3:
                return
            V = mk_auto(ct, "VS", VS, v)
            for cb in range(2):
                v.wait_ge(TS, ct.v("TS_bc_%d" % cb))
                V(lambda cb=cb: v.tensor_copy(
                    pf[:, F_B + cb * 512:F_B + (cb + 1) * 512], psE[:]))
                V(lambda cb=cb: v.tensor_copy(
                    pf[:, F_B + 1024 + cb * 512:F_B + 1024 + (cb + 1) * 512],
                    psF[:]))
        VS_bc = ct.v("VS")

        VS_p3base = None

        LD_p3b = ct.v("LD")
        VS_p3base = ct.v("VS")

        @block.tensor
        def _(t):
            t.wait_ge(VS, VS_bc)
            for tt in range(16):
                for cb in range(2):
                    j = tt * 2 + cb
                    ps = psE if j % 2 == 0 else psF
                    if j >= 2:
                        j2 = j - 2
                        t.wait_ge(VS, VS_p3base + (j2 // 2) * (9 if DEBUG_B
                                  else 8) + (j2 % 2) + 1)
                    for ec in range(8):
                        inst = t.matmul(
                            ps[:],
                            pb[:, R_AT + ec * 2048 + tt * 128:
                               R_AT + ec * 2048 + tt * 128 + 128],
                            pb[:, R_WO + ec * 1024 + cb * 512:
                               R_WO + ec * 1024 + (cb + 1) * 512],
                            start=(ec == 0), stop=(ec == 7))
                    inst.then_inc(TS, 1)
                    ct.add("TS")
                    ct.c["TS_p3_%d" % j] = ct.v("TS")

        for tt in range(16):
            @block.vector
            def _(v, tt=tt):
                V = mk_auto(ct, "VS", VS, v)
                v.wait_ge(LD, LD_p3b + 32 * (tt + 1))
                if tt >= 1:
                    v.wait_ge(OS, 16 * tt)
                for cb in range(2):
                    j = tt * 2 + cb
                    ps = psE if j % 2 == 0 else psF
                    v.wait_ge(TS, ct.v("TS_p3_%d" % j))
                    V(lambda ps=ps, cb=cb: v.scalar_tensor_tensor(
                        pf[:, F_HP + cb * 512:F_HP + (cb + 1) * 512],
                        ps[:], 1.0, pf[:, F_B + cb * 512:F_B + (cb + 1) * 512],
                        ALU.mult, ALU.add))
                    ct.c["VS_p3ev_%d" % j] = ct.v("VS")
                V(lambda tt=tt: v.tensor_tensor(
                    pf[:, F_HP:F_HP + 1024], pf[:, F_HP:F_HP + 1024],
                    pb[:, R_XT + (tt % 2) * 1024:R_XT + (tt % 2) * 1024 + 1024],
                    ALU.add))
                if DEBUG_B and tt == 0:
                    sy0 = None
                ct.c["VS_hp_%d" % tt] = ct.v("VS")

            @block.scalar
            def _(s, tt=tt):
                Sc = mk_auto(ct, "SS", SS, s)
                s.wait_ge(VS, ct.v("VS_hp_%d" % tt))
                Sc(lambda: s.activation(pb[:, R_JK:R_JK + 1024],
                                        pf[:, F_HP:F_HP + 1024], AF.Square,
                                        bias=0.0, scale=1.0,
                                        accum_out=pf[:, F_W:F_W + 1]))
            ct.c["SS_sq_%d" % tt] = ct.v("SS")

            @block.vector
            def _(v, tt=tt):
                V = mk_auto(ct, "VS", VS, v)
                v.wait_ge(SS, ct.v("SS_sq_%d" % tt))
                V(lambda: v.tensor_scalar(pf[:, F_W + 1:F_W + 2],
                                          pf[:, F_W:F_W + 1], 1.0 / DM, None,
                                          ALU.mult))
            ct.c["VS_var_%d" % tt] = ct.v("VS")

            @block.scalar
            def _(s, tt=tt):
                Sc = mk_auto(ct, "SS", SS, s)
                s.wait_ge(VS, ct.v("VS_var_%d" % tt))
                Sc(lambda: s.activation(pf[:, F_W + 2:F_W + 3],
                                        pf[:, F_W + 1:F_W + 2], AF.Sqrt,
                                        bias=1e-6, scale=1.0))
            ct.c["SS_sd_%d" % tt] = ct.v("SS")

            @block.vector
            def _(v, tt=tt):
                V = mk_auto(ct, "VS", VS, v)
                v.wait_ge(SS, ct.v("SS_sd_%d" % tt))
                if DEBUG_B and tt == 0:
                    v.tensor_copy(pf[:, 0:1024],
                                  pf[:, F_HP:F_HP + 1024]).then_inc(SHS, 1)
                    v.wait_ge(SHS, 129)
                V(lambda: v.reciprocal(pf[:, F_W + 3:F_W + 4],
                                       pf[:, F_W + 2:F_W + 3]))
                if DEBUG_B:
                    V(lambda tt=tt: v.tensor_copy(
                        pf[:, F_W + 16 + tt:F_W + 17 + tt],
                        pf[:, F_W + 3:F_W + 4]))
                V(lambda: v.tensor_scalar(pf[:, F_HP:F_HP + 1024],
                                          pf[:, F_HP:F_HP + 1024],
                                          pf[:, F_W + 3:F_W + 4], None,
                                          ALU.mult))
                V(lambda: v.tensor_tensor(pf[:, F_HP:F_HP + 1024],
                                          pf[:, F_HP:F_HP + 1024],
                                          pf[:, F_B + 1024:F_B + 2048],
                                          ALU.mult))
                V(lambda tt=tt: v.tensor_tensor(
                    pf[:, F_HP:F_HP + 1024], pf[:, F_HP:F_HP + 1024],
                    pf[:, F_AC + (tt % 2) * 1024:F_AC + (tt % 2) * 1024 + 1024],
                    ALU.add))
            ct.c["VS_y_%d" % tt] = ct.v("VS")

        @block.sync
        def _(sy):
            NOP3 = 9 if DEBUG_B else 8

            def outdma(k):
                sy.wait_ge(VS, VS_p3base + NOP3 * (k + 1))
                sy.dma_start(out=outf[k * 128:(k + 1) * 128, :],
                             in_=pf[:, F_HP:F_HP + 1024]).then_inc(OS, 16)
                ct.add("OS", 16)
            for tt in range(16):
                if tt >= 2:
                    outdma(tt - 2)
                sy.dma_start(out=pb[:, R_XT + (tt % 2) * 1024:
                               R_XT + (tt % 2) * 1024 + 1024],
                             in_=xntm1[tt * 128:(tt + 1) * 128, :]
                             ).then_inc(LD, 16)
                sy.dma_start(out=pf[:, F_AC + (tt % 2) * 1024:
                                F_AC + (tt % 2) * 1024 + 1024],
                             in_=accDi[tt * 128:(tt + 1) * 128, :]
                             ).then_inc(LD, 16)
                ct.add("LD", 32)
            outdma(14)
            outdma(15)
            sy.wait_ge(OS, ct.v("OS"))
            if DEBUG_B:
                sy.dma_start(out=ddat, in_=pb[:, R_AT:R_AT + 16384]).then_inc(OS, 16)
                sy.dma_start(out=drst, in_=pf[:, F_W + 16:F_W + 32]).then_inc(OS, 16)
                sy.dma_start(out=dhp0, in_=pf[:, 0:1024]).then_inc(OS, 16)
                sy.wait_ge(OS, ct.v("OS") + 48)
    return nc



def _c(a):
    return np.ascontiguousarray(a, dtype=np.float32)


def prep_a_inputs(inp):
    x = np.asarray(inp["L0_attn_x"], np.float32)
    wq = np.asarray(inp["L0_attn_wq"], np.float32)
    wk = np.asarray(inp["L0_attn_wk"], np.float32)
    wv = np.asarray(inp["L0_attn_wv"], np.float32)
    wo = np.asarray(inp["L0_attn_wo"], np.float32)
    bq = np.asarray(inp["L0_attn_bq"], np.float32)
    bk = np.asarray(inp["L0_attn_bk"], np.float32)
    bv = np.asarray(inp["L0_attn_bv"], np.float32)
    bo = np.asarray(inp["L0_attn_bo"], np.float32)
    wqf = np.asarray(inp["L0_ffn_Wq"], np.float32)
    bqf = np.asarray(inp["L0_ffn_bq"], np.float32)

    wqT = _c(wq.T[:, PERM1024])
    wkT = _c(wk.T)
    wvT = _c(wv.T)
    woT = _c(wo.T[PERM1024, :])
    bqp = bq[PERM1024]
    bq_l = _c(bqp.reshape(8, 128).T)
    bk_l = _c(bk.reshape(2, 128).T)
    bqf_l = _c(bqf.reshape(16, 128).T)
    identb = np.eye(128, dtype=bfd)
    shiftb = np.zeros((64, 128), dtype=bfd)
    for i in range(64):
        shiftb[i, i + 64] = 1
    maps = []
    for cid in range(8):
        b, p = cid // 4, cid % 4
        maps.append({
            "xb": _c(x[b]),
            "xloc": _c(x[b, p * 2048:(p + 1) * 2048]),
            "wqT": wqT, "wkT": wkT, "wvT": wvT, "woT": woT,
            "bq_l": bq_l, "bk_l": bk_l,
            "bv_r": _c(bv[None, :]), "bo_r": _c(bo[None, :]),
            "wqf": _c(wqf), "bqf_l": bqf_l,
            "identb": identb, "shiftb": shiftb,
        })
    return maps




def prep_b_consts(inp):
    wqf = _c(inp["L0_ffn_Wq"])
    bqf = np.asarray(inp["L0_ffn_bq"], np.float32)
    gam = np.asarray(inp["L0_ffn_bn_gamma"], np.float32)
    bet = np.asarray(inp["L0_ffn_bn_beta"], np.float32)
    tab = np.ascontiguousarray(np.concatenate(
        [np.asarray(inp["L0_ffn_expert_down"], np.float32),
         np.asarray(inp["L0_ffn_expert_up"], np.float32)], axis=1))
    d = {
        "wqf": wqf,
        "bqf_l": _c(bqf.reshape(16, 128).T),
        "gam_l": _c(gam.reshape(2, 128).T),
        "bet_l": _c(bet.reshape(2, 128).T),
        "keys0": _c(inp["L0_ffn_sub_keys0"]),
        "keys1": _c(inp["L0_ffn_sub_keys1"]),
        "tab": tab,
        "identb": np.eye(128, dtype=bfd),
        "iotad": _c(np.tile(127.0 - np.arange(128, dtype=np.float32),
                            (128, 1))),
        "negc": _c(np.tile(255.0 - np.arange(256, dtype=np.float32),
                           (128, 1))),
        "iota16": _c(np.tile(np.arange(16, dtype=np.float32), (128, 1))),
    }
    return d






def prep_c1_consts(inp):
    we1 = np.ascontiguousarray(np.concatenate(
        [np.asarray(inp["L1_ffn_We1"], np.float32),
         np.asarray(inp["L1_ffn_Ws1"], np.float32)[None]], axis=0))
    we2 = np.ascontiguousarray(np.concatenate(
        [np.asarray(inp["L1_ffn_We2"], np.float32),
         np.asarray(inp["L1_ffn_Ws2"], np.float32)[None]], axis=0))
    be1 = np.concatenate([np.asarray(inp["L1_ffn_be1"], np.float32),
                          np.asarray(inp["L1_ffn_bs1"], np.float32)[None]], 0)
    be2 = np.concatenate([np.asarray(inp["L1_ffn_be2"], np.float32),
                          np.asarray(inp["L1_ffn_bs2"], np.float32)[None]], 0)
    oneh8 = np.zeros((8, 1024), dtype=bfd)
    for e in range(8):
        oneh8[e, e * 128:(e + 1) * 128] = 1
    return {
        "wr1": _c(inp["L1_ffn_Wr1"]),
        "br1_l": _c(np.asarray(inp["L1_ffn_br1"], np.float32)
                    .reshape(2, 128).T),
        "wr2": _c(inp["L1_ffn_Wr2"]),
        "br2_r": _c(np.asarray(inp["L1_ffn_br2"], np.float32)[None, :]),
        "we1": we1,
        "be1_l": _c(be1.reshape(9, 32, 128).transpose(2, 0, 1)
                    .reshape(128, 288)),
        "we2": we2,
        "be2_9": _c(be2),
        "oneh8": oneh8,
        "identb": np.eye(128, dtype=bfd),
    }




def _rope_consts(pos0):
    invf = 1.0 / 10000.0 ** (np.arange(0, 64, 2, dtype=np.float32) / 64.0)
    t = np.arange(pos0, pos0 + 2048, dtype=np.float32)
    fr = t[:, None] * invf[None, :]
    cos64 = np.cos(np.concatenate([fr, fr], axis=1))
    sin64 = np.sin(np.concatenate([fr, fr], axis=1))
    cosf = np.zeros((128, 2048), np.float32)
    rsf = np.zeros((128, 2048), np.float32)
    for r in range(128):
        d = r % 64
        cosf[r] = cos64[:, d]
        rsf[r] = (-1.0 if d < 32 else 1.0) * sin64[:, d]
    return cosf, rsf


def prep_c2_consts(inp):
    swapb = np.zeros((128, 128), dtype=bfd)
    for m in range(128):
        blk, mm = (m // 64) * 64, m % 64
        p = blk + (mm + 32 if mm < 32 else mm - 32)
        swapb[p, m] = 1
    selb = np.zeros((16, 1024), dtype=bfd)
    for qc in range(8):
        for p in range(128):
            selb[qc * 2 + p // 64, qc * 128 + p] = 1
    oneh16 = np.zeros((16, 1024), np.float32)
    for h in range(16):
        oneh16[h, h * 64:(h + 1) * 64] = 1
    return {
        "wqT": _c(np.asarray(inp["L1_attn_Wq"], np.float32).T),
        "wkT": _c(np.asarray(inp["L1_attn_Wk"], np.float32).T),
        "wvT": _c(np.asarray(inp["L1_attn_Wv"], np.float32).T),
        "wbT": _c(np.asarray(inp["L1_attn_Wb"], np.float32).T),
        "bb_c": _c(np.asarray(inp["L1_attn_bb"], np.float32)[:, None]),
        "swapb": swapb, "selb": selb, "oneh16": oneh16,
        "identb": np.eye(128, dtype=bfd),
    }


def prep_d_consts(inp):
    causal = np.tril(np.ones((128, 128), np.float32)).T.astype(bfd)
    shiftb = np.zeros((64, 128), dtype=bfd)
    for i in range(64):
        shiftb[i, i + 64] = 1
    oneh16 = np.zeros((16, 1024), np.float32)
    for h in range(16):
        oneh16[h, h * 64:(h + 1) * 64] = 1
    return {
        "wo1T": _c(np.asarray(inp["L1_attn_Wo"], np.float32).T),
        "bo1_r": _c(np.asarray(inp["L1_attn_bo"], np.float32)[None, :]),
        "sc1_r": _c(np.asarray(inp["L1_attn_scale"], np.float32)[None, :]),
        "causal": causal, "shiftb": shiftb, "oneh16": oneh16,
    }


def build_fused():
    nc = bass.Bass("TRN2", target_bir_lowering=False)
    io = {}

    def ein(name, shape, dt):
        io[name] = nc.dram_tensor(name, shape, dt, kind="ExternalInput").ap()

    def inl(name, shape, dt):
        io[name] = nc.dram_tensor(name, shape, dt, kind="Internal").ap()

    ein("xb", [S, DM], F32); ein("xloc", [TLOC, DM], F32)
    ein("wqT", [DM, DM], F32); ein("wkT", [DM, 256], F32)
    ein("wvT", [DM, 256], F32); ein("woT", [DM, DM], F32)
    ein("bq_l", [128, 8], F32); ein("bk_l", [128, 2], F32)
    ein("bv_r", [1, 256], F32); ein("bo_r", [1, DM], F32)
    ein("wqf", [DM, 2048], F32); ein("bqf_l", [128, 16], F32)
    ein("identb", [128, 128], BF16); ein("shiftb", [64, 128], BF16)
    ein("gam_l", [128, 2], F32); ein("bet_l", [128, 2], F32)
    ein("keys0", [128, 128], F32); ein("keys1", [128, 128], F32)
    ein("tab", [16384, 2048], F32)
    ein("iotad", [128, 128], F32); ein("negc", [128, 256], F32)
    ein("iota16", [128, 16], F32)
    ein("wr1", [DM, 256], F32); ein("br1_l", [128, 2], F32)
    ein("wr2", [256, 8], F32); ein("br2_r", [1, 8], F32)
    ein("we1", [9, DM, 4096], F32); ein("be1_l", [128, 9 * 32], F32)
    ein("we2", [9, 4096, DM], F32); ein("be2_9", [9, DM], F32)
    ein("oneh8", [8, 1024], BF16)
    ein("q1T", [DM, DM], F32); ein("k1T", [DM, DM], F32)
    ein("v1T", [DM, DM], F32); ein("wbT", [DM, 16], F32)
    ein("bb_c", [16, 1], F32)
    ein("cosf", [128, 2048], F32); ein("rsf", [128, 2048], F32)
    ein("swapb", [128, 128], BF16); ein("selb", [16, 1024], BF16)
    ein("oneh16", [16, 1024], F32)
    ein("wo1T", [DM, DM], F32); ein("bo1_r", [1, DM], F32)
    ein("sc1_r", [1, DM], F32); ein("causal", [128, 128], BF16)
    ein("selw", [64, 24], F32)
    inl("a0", [TLOC, DM], F32); inl("bst", [128, 4], F32)
    inl("bstG", [1024, 4], F32)
    inl("xnfm", [128, 16384], BF16); inl("xntm", [TLOC, DM], BF16)
    inl("x1", [TLOC, DM], F32); inl("tabb", [16384, 2048], BF16)
    inl("accD", [TLOC, DM], F32)
    inl("xnfm1", [128, 16384], BF16); inl("xntm1", [TLOC, DM], BF16)
    inl("qh", [128, 16384], BF16); inl("kh", [128, 16384], BF16)
    inl("khtm", [128, 16384], BF16); inl("vsb", [128, 16640], BF16)
    inl("aalld", [16, 2048], F32)
    inl("sumcat", [16, 4224], F32); inl("sumcatG", [128, 4224], F32)
    io["outf"] = nc.dram_tensor("outf", [TLOC, DM], F32,
                                kind="ExternalOutput").ap()

    def allgather(tag, src, dst):
        with nc.Block() as block, nc.semaphore("ccs" + tag) as ccs:
            @block.gpsimd
            def _(g):
                g.collective_compute(
                    "AllGather", ALU.bypass,
                    replica_groups=[list(range(8))],
                    ins=[src.opt()], outs=[dst.opt()]).then_inc(ccs, 1)
                g.wait_ge(ccs, 1)

    with nc.reset_on_exit():
        phase_a(nc, io)
    with nc.reset_on_exit():
        allgather("b", io["bst"], io["bstG"])
    with nc.reset_on_exit():
        phase_b(nc, io)
    with nc.reset_on_exit():
        phase_c1(nc, io)
    with nc.reset_on_exit():
        phase_c2(nc, io)
    with nc.reset_on_exit():
        allgather("s", io["sumcat"], io["sumcatG"])
    phase_d(nc, io)
    return nc


_NC_CACHE = {}


def _get_nc(name, builder):
    if name not in _NC_CACHE:
        _NC_CACHE[name] = builder()
    return _NC_CACHE[name]


def prep_fused_inputs(inp):
    maps_a = prep_a_inputs(inp)
    bcon = prep_b_consts(inp)
    c1c = prep_c1_consts(inp)
    c2c = dict(prep_c2_consts(inp))
    c2c["q1T"] = c2c.pop("wqT")
    c2c["k1T"] = c2c.pop("wkT")
    c2c["v1T"] = c2c.pop("wvT")
    dc = prep_d_consts(inp)
    maps = []
    for cid in range(8):
        b, p = cid // 4, cid % 4
        m = {}
        m.update(maps_a[cid])
        m.update(bcon)
        m.update(c1c)
        m.update(c2c)
        cf, rf = _rope_consts(p * 2048)
        m["cosf"], m["rsf"] = cf, rf
        m.update(dc)
        selw = np.zeros((64, 24), np.float32)
        for k in (1, 2, 3):
            if p - k >= 0:
                selw[:, (k - 1) * 8 + (b * 4 + p - k)] = 1.0
        m["selw"] = _c(selw)
        maps.append(m)
    return maps


def kernel(**inputs):
    nc = _get_nc("fused", build_fused)
    maps = prep_fused_inputs(inputs)
    res = run_bass_kernel_spmd(nc, maps, core_ids=list(range(8))).results
    out = np.zeros((2, 8192, 1024), np.float32)
    for cid in range(8):
        b, p = cid // 4, cid % 4
        out[b, p * 2048:(p + 1) * 2048] = res[cid]["outf"]
    return out


# version 07
# speedup vs baseline: 1.4549x; 1.0231x over previous
import numpy as np
import ml_dtypes
import concourse.bass as bass
import concourse.mybir as mybir
from concourse.bass_utils import run_bass_kernel_spmd
from concourse.bass_core_rust import IndirectOffsetOnAxis

F32 = mybir.dt.float32
BF16 = mybir.dt.bfloat16
U32 = mybir.dt.uint32
AF = mybir.ActivationFunctionType
ALU = mybir.AluOpType
AX = mybir.AxisListType.X

S, DM = 8192, 1024
DEBUG_A = False
DEBUG_B = False
C2_PHASES = 5
C1_PHASES = 2
D_PHASES = 3
TLOC, NTT, NKT = 2048, 16, 64
bfd = ml_dtypes.bfloat16

PH = [0, 4, 1, 5, 2, 6, 3, 7, 8, 12, 9, 13, 10, 14, 11, 15]
PERM1024 = np.concatenate([np.arange(h * 64, h * 64 + 64) for h in PH])


class Ct:
    def __init__(self):
        self.c = {}

    def add(self, n, k=1):
        self.c[n] = self.c.get(n, 0) + k
        return self.c[n]

    def v(self, n):
        return self.c.get(n, 0)


def mk_auto(ct, key, sem, eng):
    def run(f):
        inst = f()
        inst.then_inc(sem, 1)
        eng.wait_ge(sem, ct.add(key))
        return inst
    return run


def phase_a(nc, io):
    xb, xloc = io["xb"], io["xloc"]
    wqT, wkT, wvT, woT = io["wqT"], io["wkT"], io["wvT"], io["woT"]
    bq_l, bk_l, bv_r, bo_r = io["bq_l"], io["bk_l"], io["bv_r"], io["bo_r"]
    wqf, bqf_l = io["wqf"], io["bqf_l"]
    identb, shiftb = io["identb"], io["shiftb"]
    a0, bst, xnfm, xntm = io["a0"], io["bst"], io["xnfm"], io["xntm"]
    tab, tabb = io["tab"], io["tabb"]
    if DEBUG_A:
        dxn = dk = dv = dat = None

    R_XN = 0
    R_K = 16384
    R_V = 32768
    R_AT = 49408
    R_WQ = 65792
    R_W2 = 73984
    R_SC = 82176
    R_QC = 85248
    R_MS = 87296
    PBN = 90752
    F_A = 0
    F_ST = 128
    F_BI = 292
    F_RD = 320
    F_BO = 832
    F_ON = 1856
    PFN = 1984

    ct = Ct()
    import contextlib
    with contextlib.ExitStack() as _st:
        E = _st.enter_context
        pb = E(nc.sbuf_tensor("Apb", [128, PBN], BF16))
        cst = E(nc.sbuf_tensor("Acst", [128, 8], F32))
        for _i, _v in enumerate([1e-5, -15.0, 0.125]):
            nc.const_aps.register(F32, _v, cst[:, _i:_i + 1])
        pf = E(nc.sbuf_tensor("Apf", [128, PFN], F32))
        rb64 = E(nc.sbuf_tensor("Arb64", [64, 512], F32))
        idb = E(nc.sbuf_tensor("Aidb", [128, 128], BF16))
        shb = E(nc.sbuf_tensor("Ashb", [64, 128], BF16))
        psA = E(nc.psum_tensor("ApsA", [128, 512], F32))
        psB = E(nc.psum_tensor("ApsB", [128, 512], F32))
        psC = E(nc.psum_tensor("ApsC", [128, 512], F32))
        psD = E(nc.psum_tensor("ApsD", [128, 512], F32))
        psE = E(nc.psum_tensor("ApsE", [128, 512], F32))
        psF = E(nc.psum_tensor("ApsF", [128, 512], F32))
        psT0 = E(nc.psum_tensor("ApsT0", [128, 128], BF16))
        psT1 = E(nc.psum_tensor("ApsT1", [128, 128], BF16))
        LD = E(nc.semaphore("ALD"))
        GS = E(nc.semaphore("AGS"))
        VS = E(nc.semaphore("AVS"))
        SS = E(nc.semaphore("ASS"))
        TS = E(nc.semaphore("ATS"))
        SQS = E(nc.semaphore("ASQS"))
        EXS = E(nc.semaphore("AEXS"))
        AVS = E(nc.semaphore("AAVS"))
        VP5 = E(nc.semaphore("AVP5"))
        QPS = E(nc.semaphore("AQPS"))
        QES = E(nc.semaphore("AQES"))
        SHS = E(nc.semaphore("ASHS"))
        OS = E(nc.semaphore("AOS"))
        ATB = E(nc.semaphore("ATB"))
        block = E(nc.Block())
        PT0 = R_SC + 2048
        PT1 = R_SC + 2560

        @block.sync
        def _(sy):
            sy.dma_start(out=idb[:], in_=identb).then_inc(LD, 16)
            sy.dma_start(out=pf[:, 292:300], in_=bq_l).then_inc(LD, 16)
            sy.dma_start(out=pf[:, 300:302], in_=bk_l).then_inc(LD, 16)
            sy.dma_start(out=pf[:, 304:320], in_=bqf_l).then_inc(LD, 16)
            sy.dma_start(out=pf[0:1, F_BO:F_BO + DM], in_=bo_r).then_inc(LD, 16)
            sy.dma_start(out=shb[:], in_=shiftb).then_inc(LD, 16)
            ct.add("LD", 6 * 16)
        LD_c = ct.v("LD")


        @block.gpsimd
        def _(g):
            g.dma_start(out=pb[:, R_AT:R_AT + 16384]
                        .rearrange("p (i c) -> p i c", c=DM),
                        in_=xloc.rearrange("(i p) c -> p i c", p=128)).then_inc(GS, 16)
            ct.add("GS", 16)
        GS_xloc = ct.v("GS")

        @block.vector
        def _(v):
            V = mk_auto(ct, "VS", VS, v)
            for _i, _v in enumerate([1e-5, -15.0, 0.125]):
                V(lambda _i=_i, _v=_v: v.memset(cst[:, _i:_i + 1], _v))
            V(lambda: v.memset(pb[0:1, R_MS + 256:R_MS + 384], 1.0))
            V(lambda: v.memset(pf[0:1, F_ON:F_ON + 128], 1.0))
            V(lambda: v.memset(pf[64:65, F_ON:F_ON + 128], 1.0))
            V(lambda: v.memset(
                pb[:, R_V:R_V + 16640].rearrange("p (kt h e) -> p kt h e", h=4, e=65)
                [:, :, :, 64:65], 1.0))
        VS_init = ct.v("VS")

        @block.vector
        def _(v):
            V = mk_auto(ct, "VS", VS, v)
            v.wait_ge(GS, GS_xloc)
            V(lambda: v.tensor_reduce(
                pf[:, 0:16],
                pb[:, R_AT:R_AT + 16384].rearrange("p (i c) -> p i c", c=DM),
                axis=AX, op=ALU.add))
            V(lambda: v.tensor_scalar(pf[:, 16:32], pf[:, 0:16], -1.0 / DM, None,
                                      ALU.mult))
        VS_p1a = ct.v("VS")

        @block.scalar
        def _(s):
            Sc = mk_auto(ct, "SS", SS, s)
            s.wait_ge(GS, GS_xloc)
            for i in range(16):
                Sc(lambda i=i: s.activation(
                    pb[:, R_SC + 2048:R_SC + 3072],
                    pb[:, R_AT + i * 1024:R_AT + (i + 1) * 1024], AF.Square,
                    bias=0.0, scale=1.0, accum_out=pf[:, 32 + i:33 + i]))
        SS_p1sq = ct.v("SS")

        @block.vector
        def _(v):
            V = mk_auto(ct, "VS", VS, v)
            v.wait_ge(SS, SS_p1sq)
            V(lambda: v.tensor_scalar(pf[:, 32:48], pf[:, 32:48], 1.0 / DM, None,
                                      ALU.mult))
            V(lambda: v.tensor_tensor(pf[:, 48:64], pf[:, 16:32], pf[:, 16:32],
                                      ALU.mult))
            V(lambda: v.tensor_sub(pf[:, 32:48], pf[:, 32:48], pf[:, 48:64]))
        VS_p1b = ct.v("VS")

        @block.scalar
        def _(s):
            Sc = mk_auto(ct, "SS", SS, s)
            s.wait_ge(VS, VS_p1b)
            Sc(lambda: s.activation(pf[:, 48:64], pf[:, 32:48], AF.Sqrt,
                                    bias=1e-5, scale=1.0))
        SS_p1sd = ct.v("SS")

        @block.vector
        def _(v):
            V = mk_auto(ct, "VS", VS, v)
            v.wait_ge(SS, SS_p1sd)
            V(lambda: v.reciprocal(pf[:, 64:80], pf[:, 48:64]))
        VS_p1c = ct.v("VS")

        SS_e1base = ct.v("SS")
        for i in range(16):
            xnt = R_SC + (i % 2) * 1024

            @block.vector
            def _(v, i=i, xnt=xnt):
                V = mk_auto(ct, "VS", VS, v)
                if i >= 2:
                    v.wait_ge(TS, ct.v("TS_p1_%d" % (i - 2)))
                    v.wait_ge(OS, 16 + 16 * (i - 1))
                V(lambda: v.tensor_scalar(
                    pb[:, xnt:xnt + 1024],
                    pb[:, R_AT + i * 1024:R_AT + (i + 1) * 1024],
                    pf[:, 16 + i:17 + i], pf[:, 64 + i:65 + i],
                    ALU.add, ALU.mult))
                ct.c["VS_p1n_%d" % i] = ct.v("VS")

            @block.sync
            def _(sy, i=i, xnt=xnt):
                sy.wait_ge(VS, ct.v("VS_p1n_%d" % i))
                sy.dma_start(out=xntm[i * 128:(i + 1) * 128, :],
                             in_=pb[:, xnt:xnt + 1024]).then_inc(OS, 16)
                ct.add("OS", 16)

            @block.tensor
            def _(t, i=i, xnt=xnt):
                t.wait_ge(VS, ct.v("VS_p1n_%d" % i))
                for fc in range(8):
                    pt = psT0 if fc % 2 == 0 else psT1
                    if i * 8 + fc >= 2:
                        t.wait_ge(SS, SS_e1base + i * 8 + fc - 1)
                    t.transpose(pt[:], pb[:, xnt + fc * 128:xnt + (fc + 1) * 128],
                                idb[:]).then_inc(TS, 1)
                    ct.add("TS")
                    ct.c["TS_p1t_%d" % (i * 8 + fc)] = ct.v("TS")
                ct.c["TS_p1_%d" % i] = ct.v("TS")

            @block.scalar
            def _(s, i=i):
                Sc = mk_auto(ct, "SS", SS, s)
                for fc in range(8):
                    pt = psT0 if fc % 2 == 0 else psT1
                    s.wait_ge(TS, ct.v("TS_p1t_%d" % (i * 8 + fc)))
                    Sc(lambda pt=pt, fc=fc: s.activation(
                        pb[:, R_XN + fc * 2048 + i * 128:
                           R_XN + fc * 2048 + i * 128 + 128],
                        pt[:], AF.Copy, bias=0.0, scale=1.0))
                    ct.c["SS_p1e_%d" % (i * 8 + fc)] = ct.v("SS")
        SS_xn = ct.v("SS")
        VS_p1done = ct.v("VS")

        @block.gpsimd
        def _(g):
            g.wait_ge(VS, ct.v("VS_p1n_15"))
            g.wait_ge(SS, SS_xn)
            for fc in range(8):
                g.dma_start(out=pb[:, R_AT + fc * 2048:R_AT + (fc + 1) * 2048],
                            in_=wqf[fc * 128:(fc + 1) * 128, :]).then_inc(GS, 16)
            ct.add("GS", 8 * 16)
        GS_wqf = ct.v("GS")

        SS_p2base = ct.v("SS")

        @block.tensor
        def _(t):
            t.wait_ge(GS, GS_wqf)
            t.wait_ge(LD, LD_c)
            for j in range(64):
                ofc2, nb = j // 4, j % 4
                ps = psE if j % 2 == 0 else psF
                if j >= 2:
                    t.wait_ge(SS, SS_p2base + (j - 1) * 2)
                for fc in range(8):
                    inst = t.matmul(
                        ps[:],
                        pb[:, R_AT + fc * 2048 + ofc2 * 128:
                           R_AT + fc * 2048 + ofc2 * 128 + 128],
                        pb[:, R_XN + fc * 2048 + nb * 512:
                           R_XN + fc * 2048 + nb * 512 + 512],
                        start=(fc == 0), stop=(fc == 7))
                inst.then_inc(TS, 1)
                ct.add("TS")
        TS_p2base = ct.v("TS") - 64

        @block.scalar
        def _(s):
            Sc = mk_auto(ct, "SS", SS, s)
            for j in range(64):
                ofc2 = j // 4
                ps = psE if j % 2 == 0 else psF
                s.wait_ge(TS, TS_p2base + j + 1)
                q2b = pb[:, R_SC + 2048:R_SC + 2560]
                Sc(lambda ps=ps, q2b=q2b, ofc2=ofc2, j=j: s.activation(
                    q2b, ps[:], AF.Identity,
                    bias=pf[:, 304 + ofc2:305 + ofc2], scale=1.0,
                    accum_out=pf[:, F_ST + j:F_ST + j + 1]))
                Sc(lambda q2b=q2b, j=j: s.activation(
                    pb[:, R_SC + 2560:R_SC + 3072], q2b, AF.Square,
                    bias=0.0, scale=1.0,
                    accum_out=pf[:, F_ST + 64 + j:F_ST + 64 + j + 1]))
        SS_p2 = ct.v("SS")

        @block.vector
        def _(v):
            V = mk_auto(ct, "VS", VS, v)
            v.wait_ge(SS, SS_p2)
            V(lambda: v.tensor_reduce(
                pf[:, 256:272],
                pf[:, F_ST:F_ST + 64].rearrange("p (a b) -> p a b", b=4),
                axis=AX, op=ALU.add))
            V(lambda: v.tensor_reduce(
                pf[:, 272:288],
                pf[:, F_ST + 64:F_ST + 128].rearrange("p (a b) -> p a b", b=4),
                axis=AX, op=ALU.add))
            V(lambda: v.tensor_reduce(
                pf[:, 288:290],
                pf[:, 256:272].rearrange("p (a b) -> p b a", b=2),
                axis=AX, op=ALU.add))
            V(lambda: v.tensor_reduce(
                pf[:, 290:292],
                pf[:, 272:288].rearrange("p (a b) -> p b a", b=2),
                axis=AX, op=ALU.add))
        VS_st = ct.v("VS")

        @block.sync
        def _(sy):
            sy.wait_ge(VS, VS_st)
            sy.dma_start(out=bst, in_=pf[:, 288:292]).then_inc(OS, 16)
            ct.add("OS", 16)

        @block.gpsimd
        def _(g):
            for fc in range(8):
                g.dma_start(out=pb[:, R_WQ + fc * 1024:R_WQ + (fc + 1) * 1024],
                            in_=wqT[fc * 128:(fc + 1) * 128, :]).then_inc(GS, 16)
            for fc in range(8):
                g.dma_start(out=pb[:, R_W2 + fc * 256:R_W2 + (fc + 1) * 256],
                            in_=wkT[fc * 128:(fc + 1) * 128, :]).then_inc(GS, 16)
                g.dma_start(out=pb[:, R_W2 + 2048 + fc * 256:
                               R_W2 + 2048 + (fc + 1) * 256],
                            in_=wvT[fc * 128:(fc + 1) * 128, :]).then_inc(GS, 16)
            g.dma_start(out=pb[0:1, R_MS:R_MS + 256], in_=bv_r).then_inc(GS, 16)
            ct.add("GS", 25 * 16)
        GS_w = ct.v("GS")

        @block.tensor
        def _(t):
            t.wait_ge(GS, GS_w)
            t.wait_ge(VS, VS_init)
            t.wait_ge(LD, LD_c)
            t.matmul(psC[:, 0:256], pb[0:1, R_MS + 256:R_MS + 384],
                     pb[0:1, R_MS:R_MS + 256], start=True, stop=True).then_inc(TS, 1)
            ct.add("TS")
            ct.c["TS_bvb"] = ct.v("TS")
            for cb in range(2):
                ps = psD if cb == 0 else psE
                t.matmul(ps[:], pf[0:1, F_ON:F_ON + 128],
                         pf[0:1, F_BO + cb * 512:F_BO + (cb + 1) * 512],
                         start=True, stop=True).then_inc(TS, 1)
                ct.add("TS")
            ct.c["TS_bob"] = ct.v("TS")

        @block.vector
        def _(v):
            V = mk_auto(ct, "VS", VS, v)
            v.wait_ge(TS, ct.v("TS_bvb"))
            V(lambda: v.tensor_copy(pb[:, R_MS:R_MS + 256], psC[:, 0:256]))
            v.wait_ge(TS, ct.v("TS_bob"))
            V(lambda: v.tensor_copy(pb[:, R_MS + 384:R_MS + 896], psD[:]))
            V(lambda: v.tensor_copy(pb[:, R_MS + 896:R_MS + 1408], psE[:]))
        VS_bcast = ct.v("VS")

        SS_p4base = ct.v("SS")

        def ss_of_evac(e):
            return SS_p4base + 9 * (e // 64 + 1) + e + 1
        for gk in range(8):
            @block.gpsimd
            def _(g, gk=gk):
                if gk == 0:
                    g.wait_ge(SS, SS_p2)
                else:
                    g.wait_ge(SS, ct.v("SS_p4sq_%d" % (gk - 1)))
                    g.wait_ge(VS, ct.v("VS_p4n_%d" % (gk * 8 - 1)))
                g.dma_start(out=pb[:, R_AT:R_AT + 8192]
                            .rearrange("p (i c) -> p i c", c=DM),
                            in_=xb[gk * 1024:(gk + 1) * 1024, :]
                            .rearrange("(i p) c -> p i c", p=128)).then_inc(GS, 16)
                ct.add("GS", 16)
                ct.c["GS_p4_%d" % gk] = ct.v("GS")

            @block.vector
            def _(v, gk=gk):
                V = mk_auto(ct, "VS", VS, v)
                v.wait_ge(GS, ct.v("GS_p4_%d" % gk))
                V(lambda: v.tensor_reduce(
                    pf[:, 0:8],
                    pb[:, R_AT:R_AT + 8192].rearrange("p (i c) -> p i c", c=DM),
                    axis=AX, op=ALU.add))
                V(lambda: v.tensor_scalar(pf[:, 16:24], pf[:, 0:8], -1.0 / DM,
                                          None, ALU.mult))

            @block.scalar
            def _(s, gk=gk):
                Sc = mk_auto(ct, "SS", SS, s)
                s.wait_ge(GS, ct.v("GS_p4_%d" % gk))
                for i in range(8):
                    Sc(lambda i=i: s.activation(
                        pb[:, R_SC + 2048:R_SC + 3072],
                        pb[:, R_AT + i * 1024:R_AT + (i + 1) * 1024], AF.Square,
                        bias=0.0, scale=1.0, accum_out=pf[:, 32 + i:33 + i]))
                ct.c["SS_p4sq_%d" % gk] = ct.v("SS")

            @block.vector
            def _(v, gk=gk):
                V = mk_auto(ct, "VS", VS, v)
                v.wait_ge(SS, ct.v("SS_p4sq_%d" % gk))
                V(lambda: v.tensor_scalar(pf[:, 32:40], pf[:, 32:40], 1.0 / DM,
                                          None, ALU.mult))
                V(lambda: v.tensor_tensor(pf[:, 48:56], pf[:, 16:24],
                                          pf[:, 16:24], ALU.mult))
                V(lambda: v.tensor_sub(pf[:, 32:40], pf[:, 32:40], pf[:, 48:56]))
                ct.c["VS_p4b_%d" % gk] = ct.v("VS")

            @block.scalar
            def _(s, gk=gk):
                Sc = mk_auto(ct, "SS", SS, s)
                s.wait_ge(VS, ct.v("VS_p4b_%d" % gk))
                Sc(lambda: s.activation(pf[:, 48:56], pf[:, 32:40], AF.Sqrt,
                                        bias=1e-5, scale=1.0))
                ct.c["SS_p4sd_%d" % gk] = ct.v("SS")

            @block.vector
            def _(v, gk=gk):
                V = mk_auto(ct, "VS", VS, v)
                v.wait_ge(SS, ct.v("SS_p4sd_%d" % gk))
                V(lambda: v.reciprocal(pf[:, 64:72], pf[:, 48:56]))
                ct.c["VS_p4c_%d" % gk] = ct.v("VS")

            for i in range(8):
                kt = gk * 8 + i
                xnt = R_SC + (i % 2) * 1024

                @block.vector
                def _(v, gk=gk, i=i, kt=kt, xnt=xnt):
                    V = mk_auto(ct, "VS", VS, v)
                    if kt >= 2:
                        v.wait_ge(TS, ct.v("TS_p4t8_%d" % (kt - 2)))
                    V(lambda: v.tensor_scalar(
                        pb[:, xnt:xnt + 1024],
                        pb[:, R_AT + i * 1024:R_AT + (i + 1) * 1024],
                        pf[:, 16 + i:17 + i], pf[:, 64 + i:65 + i],
                        ALU.add, ALU.mult))
                    ct.c["VS_p4n_%d" % kt] = ct.v("VS")

                @block.tensor
                def _(t, kt=kt, xnt=xnt):
                    t.wait_ge(VS, ct.v("VS_p4n_%d" % kt))
                    for fc in range(8):
                        pt = psT0 if fc % 2 == 0 else psT1
                        if kt * 8 + fc >= 2:
                            t.wait_ge(SS, ss_of_evac(kt * 8 + fc - 2))
                        t.transpose(pt[:], pb[:, xnt + fc * 128:xnt + (fc + 1) * 128],
                                    idb[:]).then_inc(TS, 1)
                        ct.add("TS")
                        ct.c["TS_p4t_%d" % (kt * 8 + fc)] = ct.v("TS")
                    ct.c["TS_p4t8_%d" % kt] = ct.v("TS")

                @block.scalar
                def _(s, kt=kt):
                    Sc = mk_auto(ct, "SS", SS, s)
                    for fc in range(8):
                        pt = psT0 if fc % 2 == 0 else psT1
                        s.wait_ge(TS, ct.v("TS_p4t_%d" % (kt * 8 + fc)))
                        Sc(lambda pt=pt, fc=fc: s.activation(
                            pb[:, R_QC + fc * 128:R_QC + (fc + 1) * 128],
                            pt[:], AF.Copy, bias=0.0, scale=1.0))
                        ct.c["SS_p4t_%d" % (kt * 8 + fc)] = ct.v("SS")
                    ct.c["SS_p4fm_%d" % kt] = ct.v("SS")

                @block.tensor
                def _(t, kt=kt):
                    t.wait_ge(SS, ss_of_evac(kt * 8 + 7))
                    if kt == 0:
                        t.wait_ge(GS, GS_w)
                        t.wait_ge(VS, VS_bcast)
                    else:
                        t.wait_ge(VS, ct.v("VS_p4kv_%d" % (kt - 1)))
                    for nch in range(2):
                        ps = psA if nch == 0 else psB
                        for fc in range(8):
                            inst = t.matmul(
                                ps[:, 0:128],
                                pb[:, R_W2 + fc * 256 + nch * 128:
                                   R_W2 + fc * 256 + nch * 128 + 128],
                                pb[:, R_QC + fc * 128:R_QC + (fc + 1) * 128],
                                start=(fc == 0), stop=(fc == 7))
                        inst.then_inc(TS, 1)
                        ct.add("TS")
                    for fc in range(8):
                        inst = t.matmul(
                            psC[:, 0:256],
                            pb[:, R_QC + fc * 128:R_QC + (fc + 1) * 128],
                            pb[:, R_W2 + 2048 + fc * 256:
                               R_W2 + 2048 + (fc + 1) * 256],
                            start=(fc == 0), stop=(fc == 7))
                    inst.then_inc(TS, 1)
                    ct.add("TS")
                    ct.c["TS_p4kv_%d" % kt] = ct.v("TS")

                @block.vector
                def _(v, kt=kt):
                    V = mk_auto(ct, "VS", VS, v)
                    v.wait_ge(TS, ct.v("TS_p4kv_%d" % kt))
                    for nch in range(2):
                        ps = psA if nch == 0 else psB
                        V(lambda ps=ps, nch=nch: v.tensor_scalar(
                            pb[:, R_K + nch * 8192 + kt * 128:
                               R_K + nch * 8192 + kt * 128 + 128],
                            ps[:, 0:128], pf[:, 300 + nch:301 + nch], None,
                            ALU.add))
                    V(lambda: v.tensor_tensor(
                        pb[:, R_V + kt * 260:R_V + kt * 260 + 260]
                        .rearrange("p (h e) -> p h e", e=65)[:, :, 0:64],
                        psC[:, 0:256].rearrange("p (h e) -> p h e", e=64),
                        pb[:, R_MS:R_MS + 256].rearrange("p (h e) -> p h e", e=64),
                        ALU.add))
                    ct.c["VS_p4kv_%d" % kt] = ct.v("VS")
        SS_p4 = ct.v("SS")
        VS_p4 = ct.v("VS")

        def kvslice(h):
            kvh = h // 4
            return kvh // 2, (kvh % 2) * 64

        TS_p5base = ct.v("TS")

        @block.tensor
        def _(t):
            t.wait_ge(VS, VS_p4)
            t.wait_ge(SS, SS_p4)
            nS = 0
            for qc in range(8):
                for nb in range(4):
                    j = qc * 4 + nb
                    ps = psE if j % 2 == 0 else psF
                    if j >= 2:
                        t.wait_ge(QES, j - 1)
                    for fc in range(8):
                        inst = t.matmul(
                            ps[:],
                            pb[:, R_WQ + fc * 1024 + qc * 128:
                               R_WQ + fc * 1024 + qc * 128 + 128],
                            pb[:, R_XN + fc * 2048 + nb * 512:
                               R_XN + fc * 2048 + nb * 512 + 512],
                            start=(fc == 0), stop=(fc == 7))
                    inst.then_inc(QPS, 1)
                t.wait_ge(QES, (qc + 1) * 4)
                for hh in range(2):
                    h = PH[qc * 2 + hh]
                    doff = hh * 64
                    nch, koff = kvslice(h)
                    assert koff == doff, (h, koff, doff)
                    for qb in range(4):
                        m = (qc * 2 + hh) * 4 + qb
                        for kc in range(64):
                            n = nS
                            nS += 1
                            ps = psA if n % 2 == 0 else psB
                            if n >= 2:
                                t.wait_ge(EXS, n - 1)
                            t.matmul(
                                ps[:],
                                pb[koff:koff + 64,
                                   R_K + nch * 8192 + kc * 128:
                                   R_K + nch * 8192 + kc * 128 + 128],
                                pb[doff:doff + 64,
                                   R_QC + qb * 512:R_QC + qb * 512 + 512],
                                start=True, stop=True).then_inc(SQS, 1)
                            if kc == 0:
                                t.wait_ge(VP5, 4 * m)
                            if kc >= 1:
                                t.wait_ge(EXS, n)
                                t.matmul(
                                    psC[0:65, :],
                                    pb[:, R_V + (kc - 1) * 260 + (h // 4) * 65:
                                       R_V + (kc - 1) * 260 + (h // 4) * 65 + 65],
                                    pb[:, (PT0 if (n - 1) % 2 == 0 else PT1):
                                       (PT0 if (n - 1) % 2 == 0 else PT1) + 512],
                                    start=(kc == 1), stop=False)
                        t.wait_ge(EXS, nS)
                        t.matmul(
                            psC[0:65, :],
                            pb[:, R_V + 63 * 260 + (h // 4) * 65:
                               R_V + 63 * 260 + (h // 4) * 65 + 65],
                            pb[:, (PT0 if (nS - 1) % 2 == 0 else PT1):
                               (PT0 if (nS - 1) % 2 == 0 else PT1) + 512],
                            start=False, stop=True).then_inc(AVS, 1)
                        t.wait_ge(VP5, 4 * m + 1)
                        t.matmul(psD[0:64, :], pf[64:65, F_ON:F_ON + 64],
                                 pf[64:65, F_RD:F_RD + 512],
                                 start=True, stop=True).then_inc(AVS, 1)
                        if hh == 1:
                            t.wait_ge(VP5, 4 * m + 3)
                            t.matmul(psD[:], shb[:],
                                     pb[0:64, R_SC:R_SC + 512],
                                     start=True, stop=True).then_inc(SHS, 1)

        @block.scalar
        def _(s):
            s.wait_ge(LD, LD_c)
            s.wait_ge(VS, VS_init)
            for qc in range(8):
                for nb in range(4):
                    j = qc * 4 + nb
                    ps = psE if j % 2 == 0 else psF
                    s.wait_ge(QPS, j + 1)
                    s.activation(pb[:, R_QC + nb * 512:R_QC + (nb + 1) * 512],
                                 ps[:], AF.Identity,
                                 bias=pf[:, 292 + qc:293 + qc],
                                 scale=1.0).then_inc(QES, 1)
                for hh in range(2):
                    for qb in range(4):
                        m = (qc * 2 + hh) * 4 + qb
                        for kc in range(64):
                            n = m * 64 + kc
                            ps = psA if n % 2 == 0 else psB
                            s.wait_ge(SQS, n + 1)
                            s.activation(
                                pb[:, (PT0 if n % 2 == 0 else PT1):
                                   (PT0 if n % 2 == 0 else PT1) + 512],
                                ps[:], AF.Exp, bias=-15.0,
                                scale=0.125).then_inc(EXS, 1)

        @block.vector
        def _(v):
            shs_i = 0
            for qc in range(8):
                for hh in range(2):
                    for qb in range(4):
                        m = (qc * 2 + hh) * 4 + qb
                        v.wait_ge(AVS, 2 * m + 1)
                        v.reciprocal(pf[64:65, F_RD:F_RD + 512],
                                     psC[64:65, :]).then_inc(VP5, 1)
                        v.wait_ge(VP5, 4 * m + 1)
                        v.wait_ge(AVS, 2 * m + 2)
                        v.tensor_copy(rb64[:], psD[0:64, :]).then_inc(VP5, 1)
                        v.wait_ge(VP5, 4 * m + 2)
                        if hh == 0:
                            v.scalar_tensor_tensor(
                                pb[0:64,
                                   R_AT + qc * 2048 + qb * 512:
                                   R_AT + qc * 2048 + qb * 512 + 512],
                                psC[0:64, :], 1.0, rb64[:],
                                ALU.mult, ALU.mult).then_inc(VP5, 1)
                            v.wait_ge(VP5, 4 * m + 3)
                            v.engine_nop().then_inc(VP5, 1)
                            v.wait_ge(VP5, 4 * m + 4)
                        else:
                            v.scalar_tensor_tensor(
                                pb[0:64, R_SC:R_SC + 512],
                                psC[0:64, :], 1.0, rb64[:],
                                ALU.mult, ALU.mult).then_inc(VP5, 1)
                            v.wait_ge(VP5, 4 * m + 3)
                            shs_i += 1
                            v.wait_ge(SHS, shs_i)
                            v.tensor_copy(
                                pb[64:128,
                                   R_AT + qc * 2048 + qb * 512:
                                   R_AT + qc * 2048 + qb * 512 + 512],
                                psD[64:128, :]).then_inc(VP5, 1)
                            v.wait_ge(VP5, 4 * m + 4)
        VS_after_p5 = ct.v("VS")

        @block.gpsimd
        def _(g):
            g.wait_ge(TS, ct.v("TS_p4kv_63"))
            for fc in range(8):
                g.dma_start(out=pb[:, R_W2 + fc * 1024:R_W2 + (fc + 1) * 1024],
                            in_=woT[fc * 128:(fc + 1) * 128, :]).then_inc(GS, 16)
            ct.add("GS", 8 * 16)
        GS_wo = ct.v("GS")

        @block.tensor
        def _(t):
            t.wait_ge(GS, GS_wo)
            t.wait_ge(VP5, 4 * 64)
            for tt in range(16):
                for cb in range(2):
                    j = tt * 2 + cb
                    ps = psE if j % 2 == 0 else psF
                    if j >= 2:
                        t.wait_ge(VS, VS_p4 + (j - 1))
                    for ec in range(8):
                        inst = t.matmul(
                            ps[:],
                            pb[:, R_AT + ec * 2048 + tt * 128:
                               R_AT + ec * 2048 + tt * 128 + 128],
                            pb[:, R_W2 + ec * 1024 + cb * 512:
                               R_W2 + ec * 1024 + (cb + 1) * 512],
                            start=(ec == 0), stop=(ec == 7))
                    inst.then_inc(TS, 1)
                    ct.add("TS")
                    ct.c["TS_p6_%d" % j] = ct.v("TS")

        @block.vector
        def _(v):
            for tt in range(16):
                for cb in range(2):
                    j = tt * 2 + cb
                    ps = psE if j % 2 == 0 else psF
                    v.wait_ge(TS, ct.v("TS_p6_%d" % j))
                    if j >= 2:
                        v.wait_ge(GS, GS_wo + ((j - 1) // 2) * 16)
                    v.scalar_tensor_tensor(
                        pb[:, R_MS + 1408 + (tt % 2) * 1024 + cb * 512:
                           R_MS + 1408 + (tt % 2) * 1024 + (cb + 1) * 512],
                        ps[:], 1.0,
                        pb[:, R_MS + 384 + cb * 512:R_MS + 384 + (cb + 1) * 512],
                        ALU.mult, ALU.add).then_inc(VS, 1)
                    ct.add("VS")
                    ct.c["VS_p6_%d" % j] = ct.v("VS")

        @block.gpsimd
        def _(g):
            for tt in range(16):
                g.wait_ge(VS, ct.v("VS_p6_%d" % (tt * 2 + 1)))
                g.dma_start(out=a0[tt * 128:(tt + 1) * 128, :],
                            in_=pb[:, R_MS + 1408 + (tt % 2) * 1024:
                                   R_MS + 1408 + (tt % 2) * 1024 + 1024]
                            ).then_inc(GS, 16)
                ct.add("GS", 16)
        GS_end = ct.v("GS")

        if DEBUG_A:
            @block.gpsimd
            def _(g):
                g.wait_ge(GS, GS_end)
                g.dma_start(out=dxn, in_=pb[:, R_XN:R_XN + 16384]).then_inc(GS, 16)
                g.dma_start(out=dk, in_=pb[:, R_K:R_K + 16384]).then_inc(GS, 16)
                g.dma_start(out=dv, in_=pb[:, R_V:R_V + 16640]).then_inc(GS, 16)
                g.dma_start(out=dat, in_=pb[:, R_AT:R_AT + 16384]).then_inc(GS, 16)
                ct.add("GS", 64)
            GS_end2 = ct.v("GS")
        else:
            GS_end2 = GS_end

        @block.sync
        def _(sy):
            sy.wait_ge(GS, GS_end2)
            sy.wait_ge(OS, ct.v("OS"))
            sy.dma_start(out=xnfm, in_=pb[:, R_XN:R_XN + 16384]).then_inc(OS, 16)
            sy.wait_ge(OS, ct.v("OS") + 16)



MAGIC = 12582912.0


def phase_b(nc, io):
    xloc, a0, xnfm, xntm = io["xloc"], io["a0"], io["xnfm"], io["xntm"]
    bsta = io["bstG"].rearrange("(c r) j -> r c j", c=8)
    wqf, bqf_l = io["wqf"], io["bqf_l"]
    gam_l, bet_l = io["gam_l"], io["bet_l"]
    keys0, keys1, tab, identb = io["keys0"], io["keys1"], io["tab"], io["identb"]
    iotad, negc, iota16 = io["iotad"], io["negc"], io["iota16"]
    x1, tabb = io["x1"], io["tabb"]
    if DEBUG_B:
        didx = dwt = dq = drn = None

    R_XN = 0
    R_WQF = 16384
    R_Q = 32768
    R_Q2 = 36864
    R_KH = 38912
    R_GT = 39424
    R_JK = 43520
    R_XT = 44544
    R_ON = 46592
    R_GT2 = 46624
    PBN = 58912

    def gbuf(u):
        m = u % 8
        return (R_GT + m * 2048) if m < 2 else (R_GT2 + (m - 2) * 2048)
    F_S0 = 0
    F_E = 256
    F_CE = 512
    F_D = 768
    F_OH = 1024
    F_W = 1536
    F_ST = 1600
    F_NR = 1728
    F_RN = 3776
    F_KEY = 3808
    F_EI = 3936
    F_ON1 = 3944
    F_YA = 3968
    F_XA = 6016
    F_WT = 10112
    PFN = 12160

    ct = Ct()
    import contextlib
    with contextlib.ExitStack() as _st:
        E = _st.enter_context
        pb = E(nc.sbuf_tensor("Bpb", [128, PBN], BF16))
        pf = E(nc.sbuf_tensor("Bpf", [128, PFN], F32))
        cst = E(nc.sbuf_tensor("Bcst", [128, 8], F32))
        for _i, _v in enumerate([1e-5, 1e-12]):
            nc.const_aps.register(F32, _v, cst[:, _i:_i + 1])
        cts = E(nc.sbuf_tensor("Bcts", [128, 416], F32))
        idxu = E(nc.sbuf_tensor("Bidxu", [128, 2048], U32))
        idb = E(nc.sbuf_tensor("Bidb", [128, 128], BF16))
        psA = E(nc.psum_tensor("BpsA", [128, 512], F32))
        psB = E(nc.psum_tensor("BpsB", [128, 512], F32))
        psC = E(nc.psum_tensor("BpsC", [128, 512], F32))
        psE = E(nc.psum_tensor("BpsE", [128, 512], F32))
        psT0 = E(nc.psum_tensor("BpsT0", [128, 128], BF16))
        LD = E(nc.semaphore("BLD"))
        GS = E(nc.semaphore("BGS"))
        VS = E(nc.semaphore("BVS"))
        SS = E(nc.semaphore("BSS"))
        TS = E(nc.semaphore("BTS"))
        BQP = E(nc.semaphore("BBQP"))
        BQE = E(nc.semaphore("BBQE"))
        GT = E(nc.semaphore("BGT"))
        DSS = E(nc.semaphore("BDSS"))
        GLS = E(nc.semaphore("BGLS"))
        UPS = E(nc.semaphore("BUPS"))
        OS = E(nc.semaphore("BOS"))
        block = E(nc.Block())

        def VA(v, inst):
            inst.then_inc(VS, 1)
            v.wait_ge(VS, ct.add("VS"))
            return inst

        @block.sync
        def _(sy):
            sy.dma_start(out=idb[:], in_=identb).then_inc(LD, 16)
            sy.dma_start(out=pf[:, 1600:1632].rearrange("r (c j) -> r c j", j=4),
                         in_=bsta).then_inc(LD, 16)
            sy.dma_start(out=pf[:, 1660:1676], in_=bqf_l).then_inc(LD, 16)
            sy.dma_start(out=pf[:, 1676:1678], in_=gam_l).then_inc(LD, 16)
            sy.dma_start(out=pf[:, 1678:1680], in_=bet_l).then_inc(LD, 16)
            sy.dma_start(out=pf[:, F_KEY:F_KEY + 128], in_=keys0).then_inc(LD, 16)
            sy.dma_start(out=cts[:, 0:128], in_=iotad).then_inc(LD, 16)
            sy.dma_start(out=cts[:, 128:384], in_=negc).then_inc(LD, 16)
            sy.dma_start(out=cts[:, 384:400], in_=iota16).then_inc(LD, 16)
            ct.add("LD", 9 * 16)
        LD_c = ct.v("LD")

        @block.gpsimd
        def _(g):
            g.dma_start(out=pb[:, R_XN:R_XN + 16384], in_=xnfm).then_inc(GS, 16)
            for fc in range(8):
                g.dma_start(out=pb[:, R_WQF + fc * 2048:R_WQF + (fc + 1) * 2048],
                            in_=wqf[fc * 128:(fc + 1) * 128, :]).then_inc(GS, 16)
            ct.add("GS", 9 * 16)
        GS_w = ct.v("GS")

        @block.gpsimd
        def _(g):
            for i in range(16):
                g.dma_start(out=tabb[i * 1024:(i + 1) * 1024, :],
                            in_=tab[i * 1024:(i + 1) * 1024, :]
                            ).then_inc(GS, 16)
                ct.add("GS", 16)
            g.wait_ge(GS, ct.v("GS"))
        GS_tab = ct.v("GS")

        @block.vector
        def _(v):
            V = mk_auto(ct, "VS", VS, v)
            v.wait_ge(LD, LD_c)
            for _i, _v2 in enumerate([1e-5, 1e-12]):
                V(lambda _i=_i, _v2=_v2: v.memset(cst[:, _i:_i + 1], _v2))
            V(lambda: v.memset(pb[:, R_ON:R_ON + 1], 1.0))
            V(lambda: v.memset(pf[:, F_ON1:F_ON1 + 1], 1.0))
            V(lambda: v.tensor_reduce(
                pf[:, 1632:1636],
                pf[:, 1600:1632].rearrange("p (c s) -> p s c", s=4),
                axis=AX, op=ALU.add))
            NINV = 1.0 / 131072.0
            V(lambda: v.tensor_scalar(pf[:, 1636:1638], pf[:, 1632:1634], -NINV,
                                      None, ALU.mult))
            V(lambda: v.tensor_scalar(pf[:, 1638:1640], pf[:, 1634:1636], NINV,
                                      None, ALU.mult))
            V(lambda: v.tensor_tensor(pf[:, 1640:1642], pf[:, 1636:1638],
                                      pf[:, 1636:1638], ALU.mult))
            V(lambda: v.tensor_sub(pf[:, 1638:1640], pf[:, 1638:1640],
                                   pf[:, 1640:1642]))
        VS_bn1 = ct.v("VS")

        @block.scalar
        def _(s):
            Sc = mk_auto(ct, "SS", SS, s)
            s.wait_ge(VS, VS_bn1)
            Sc(lambda: s.activation(pf[:, 1640:1642], pf[:, 1638:1640], AF.Sqrt,
                                    bias=1e-5, scale=1.0))
        SS_bn = ct.v("SS")

        @block.vector
        def _(v):
            V = mk_auto(ct, "VS", VS, v)
            v.wait_ge(SS, SS_bn)
            V(lambda: v.reciprocal(pf[:, 1642:1644], pf[:, 1640:1642]))
            V(lambda: v.tensor_tensor(pf[:, 1700:1702], pf[:, 1642:1644],
                                      pf[:, 1676:1678], ALU.mult))
            for half in range(2):
                V(lambda half=half: v.tensor_scalar(
                    pf[:, 1704 + half:1720:2], pf[:, 1660 + half:1676:2],
                    pf[:, 1636 + half:1637 + half],
                    pf[:, 1700 + half:1701 + half], ALU.add, ALU.mult))
                V(lambda half=half: v.tensor_scalar(
                    pf[:, 1704 + half:1720:2], pf[:, 1704 + half:1720:2],
                    pf[:, 1678 + half:1679 + half], None, ALU.add))
        VS_bn2 = ct.v("VS")

        for ki in range(2):
            if ki == 1:
                @block.sync
                def _(sy):
                    sy.wait_ge(VS, ct.v("VS_key_0"))
                    sy.dma_start(out=pf[:, F_KEY:F_KEY + 128],
                                 in_=keys1).then_inc(LD, 16)
                    ct.add("LD", 16)
            LD_key = ct.v("LD")

            @block.scalar
            def _(s, ki=ki, LD_key=LD_key):
                Sc = mk_auto(ct, "SS", SS, s)
                s.wait_ge(LD, LD_key)
                if ki == 0:
                    s.wait_ge(VS, VS_bn2)
                Sc(lambda: s.activation(pb[:, R_JK:R_JK + 128],
                                        pf[:, F_KEY:F_KEY + 128], AF.Square,
                                        bias=0.0, scale=1.0,
                                        accum_out=pf[:, 1644:1645]))
                Sc(lambda: s.activation(pf[:, 1645:1646], pf[:, 1644:1645],
                                        AF.Sqrt, bias=0.0, scale=1.0))
            ct.c["SS_key_%d" % ki] = ct.v("SS")

            @block.vector
            def _(v, ki=ki):
                V = mk_auto(ct, "VS", VS, v)
                v.wait_ge(SS, ct.v("SS_key_%d" % ki))
                V(lambda: v.tensor_scalar(pf[:, 1645:1646], pf[:, 1645:1646],
                                          1e-12, None, ALU.max))
                V(lambda: v.reciprocal(pf[:, 1646:1647], pf[:, 1645:1646]))
                V(lambda: v.tensor_scalar(
                    pb[:, R_KH + 256:R_KH + 384], pf[:, F_KEY:F_KEY + 128],
                    pf[:, 1646:1647], None, ALU.mult))
                ct.c["VS_key_%d" % ki] = ct.v("VS")

            @block.tensor
            def _(t, ki=ki):
                t.wait_ge(VS, ct.v("VS_key_%d" % ki))
                t.wait_ge(LD, LD_c)
                t.transpose(psT0[:], pb[:, R_KH + 256:R_KH + 384],
                            idb[:]).then_inc(TS, 1)
                ct.add("TS")
                ct.c["TS_key_%d" % ki] = ct.v("TS")

            @block.scalar
            def _(s, ki=ki):
                Sc = mk_auto(ct, "SS", SS, s)
                s.wait_ge(TS, ct.v("TS_key_%d" % ki))
                Sc(lambda ki=ki: s.activation(
                    pb[:, R_KH + ki * 128:R_KH + (ki + 1) * 128], psT0[:],
                    AF.Copy, bias=0.0, scale=1.0))
        SS_keys = ct.v("SS")

        nbank = [psA, psB, psC, psE]
        for h in range(8):
            for half in range(2):
                ofc2 = h * 2 + half

                @block.tensor
                def _(t, h=h, half=half, ofc2=ofc2):
                    g0 = h * 8 + half * 4
                    if h == 0 and half == 0:
                        t.wait_ge(GS, GS_w)
                        t.wait_ge(SS, SS_keys)
                    for nb in range(4):
                        j = g0 + nb
                        ps = psA if j % 2 == 0 else psB
                        if j >= 2:
                            t.wait_ge(BQE, j - 1)
                        for fc in range(8):
                            inst = t.matmul(
                                ps[:],
                                pb[:, R_WQF + fc * 2048 + ofc2 * 128:
                                   R_WQF + fc * 2048 + ofc2 * 128 + 128],
                                pb[:, R_XN + fc * 2048 + nb * 512:
                                   R_XN + fc * 2048 + nb * 512 + 512],
                                start=(fc == 0), stop=(fc == 7))
                        inst.then_inc(BQP, 1)

                @block.vector
                def _(v, h=h, half=half, ofc2=ofc2):
                    g0 = h * 8 + half * 4
                    for nb in range(4):
                        j = g0 + nb
                        ps = psA if j % 2 == 0 else psB
                        v.wait_ge(BQP, j + 1)
                        v.tensor_scalar(
                            pb[:, R_Q + half * 2048 + nb * 512:
                               R_Q + half * 2048 + (nb + 1) * 512],
                            ps[:], pf[:, 1700 + half:1701 + half],
                            pf[:, 1704 + ofc2:1705 + ofc2],
                            ALU.mult, ALU.add).then_inc(BQE, 1)

                @block.scalar
                def _(s, h=h, half=half):
                    Sc = mk_auto(ct, "SS", SS, s)
                    s.wait_ge(BQE, h * 8 + half * 4 + 4)
                    Sc(lambda half=half: s.activation(
                        pb[:, R_Q2:R_Q2 + 2048],
                        pb[:, R_Q + half * 2048:R_Q + half * 2048 + 2048],
                        AF.Square, bias=0.0, scale=1.0))
                ct.c["SS_sq_%d_%d" % (h, half)] = ct.v("SS")

                @block.tensor
                def _(t, h=h, half=half):
                    t.wait_ge(SS, ct.v("SS_sq_%d_%d" % (h, half)))
                    if h * 2 + half > 0:
                        t.wait_ge(VS, ct.v("VS_rn_%d" % (h * 2 + half - 1)))
                    for nb in range(4):
                        t.matmul(nbank[nb][0:1, :], pb[:, R_ON:R_ON + 1],
                                 pb[:, R_Q2 + nb * 512:R_Q2 + (nb + 1) * 512],
                                 start=True, stop=True).then_inc(TS, 1)
                        ct.add("TS")
                        ct.c["TS_nr_%d" % nb] = ct.v("TS")

                @block.vector
                def _(v, h=h, half=half):
                    V = mk_auto(ct, "VS", VS, v)
                    for nb in range(4):
                        v.wait_ge(TS, ct.v("TS_nr_%d" % nb))
                        V(lambda nb=nb: v.tensor_copy(
                            pf[0:1, F_NR + nb * 512:F_NR + (nb + 1) * 512],
                            nbank[nb][0:1, :]))
                    ct.c["VS_nrr_%d" % (h * 2 + half)] = ct.v("VS")

                @block.tensor
                def _(t, h=h, half=half):
                    t.wait_ge(VS, ct.v("VS_nrr_%d" % (h * 2 + half)))
                    for tt in range(16):
                        inst = t.matmul(psE[:, 256 + tt:257 + tt],
                                        pf[0:1, F_NR + tt * 128:
                                           F_NR + (tt + 1) * 128],
                                        pf[0:1, F_ON1:F_ON1 + 1],
                                        start=True, stop=True)
                    inst.then_inc(TS, 1)
                    ct.add("TS")
                    ct.c["TS_nc_%d" % (h * 2 + half)] = ct.v("TS")

                @block.vector
                def _(v, h=h, half=half):
                    V = mk_auto(ct, "VS", VS, v)
                    v.wait_ge(TS, ct.v("TS_nc_%d" % (h * 2 + half)))
                    V(lambda half=half: v.tensor_copy(
                        pf[:, F_RN + half * 16:F_RN + half * 16 + 16],
                        psE[:, 256:272]))
                    ct.c["VS_rn_%d" % (h * 2 + half)] = ct.v("VS")

            @block.scalar
            def _(s, h=h):
                Sc = mk_auto(ct, "SS", SS, s)
                s.wait_ge(VS, ct.v("VS_rn_%d" % (h * 2 + 1)))
                Sc(lambda: s.activation(pf[:, F_RN:F_RN + 32],
                                        pf[:, F_RN:F_RN + 32], AF.Sqrt,
                                        bias=0.0, scale=1.0))
            ct.c["SS_rsq_%d" % h] = ct.v("SS")

            @block.vector
            def _(v, h=h):
                V = mk_auto(ct, "VS", VS, v)
                v.wait_ge(SS, ct.v("SS_rsq_%d" % h))
                V(lambda: v.tensor_scalar(pf[:, F_RN:F_RN + 32],
                                          pf[:, F_RN:F_RN + 32], 1e-12, None,
                                          ALU.max))
                V(lambda: v.reciprocal(pf[:, F_RN:F_RN + 32],
                                       pf[:, F_RN:F_RN + 32]))
            ct.c["VS_rsq_%d" % h] = ct.v("VS")

            for tt in range(16):
                @block.tensor
                def _(t, h=h, tt=tt):
                    t.wait_ge(VS, ct.v("VS_rsq_%d" % h))
                    if tt > 0:
                        t.wait_ge(VS, ct.v("VS_sc_%d_%d" % (h, tt - 1)))
                    for half in range(2):
                        ps = psC if half == 0 else psE
                        t.matmul(
                            ps[:, 0:128],
                            pb[:, R_Q + half * 2048 + tt * 128:
                               R_Q + half * 2048 + tt * 128 + 128],
                            pb[:, R_KH + half * 128:R_KH + (half + 1) * 128],
                            start=True, stop=True).then_inc(TS, 1)
                        ct.add("TS")
                        ct.c["TS_sc_%d" % half] = ct.v("TS")

                @block.vector
                def _(v, h=h, tt=tt):
                    V = mk_auto(ct, "VS", VS, v)
                    D = F_D
                    for half in range(2):
                        ps = psC if half == 0 else psE
                        v.wait_ge(TS, ct.v("TS_sc_%d" % half))
                        V(lambda ps=ps, half=half: v.tensor_scalar(
                            pf[:, F_S0 + half * 128:F_S0 + (half + 1) * 128],
                            ps[:, 0:128],
                            pf[:, F_RN + half * 16 + tt:
                               F_RN + half * 16 + tt + 1],
                            None, ALU.mult))
                    ct.c["VS_sc_%d_%d" % (h, tt)] = ct.v("VS")
                    for half in range(2):
                        s0 = pf[:, F_S0 + half * 128:F_S0 + (half + 1) * 128]
                        e0 = pf[:, F_E:F_E + 128]
                        ev = pf[:, D + half * 16:D + half * 16 + 16]
                        V(lambda s0=s0, e0=e0: v.tensor_scalar(
                            e0, s0, 32768.0, MAGIC, ALU.mult, ALU.add))
                        V(lambda e0=e0: v.tensor_scalar(
                            e0, e0, MAGIC, 256.0, ALU.subtract, ALU.mult))
                        V(lambda e0=e0: v.tensor_add(e0, e0, cts[:, 0:128]))
                        V(lambda e0=e0, ev=ev: v.max(out=ev[:, 0:8], in_=e0))
                        V(lambda e0=e0, ev=ev: v.match_replace(
                            out=e0, in_to_replace=ev[:, 0:8], in_values=e0,
                            imm_value=-1e9))
                        V(lambda e0=e0, ev=ev: v.max(out=ev[:, 8:16], in_=e0))
                        u = pf[:, D + 48:D + 64]
                        Q = pf[:, D + 64:D + 80]
                        r = pf[:, D + 80:D + 96]
                        V(lambda ev=ev, u=u: v.tensor_scalar(
                            u, ev, 1.0 / 256.0, MAGIC, ALU.mult, ALU.add))
                        V(lambda u=u, Q=Q: v.tensor_scalar(
                            Q, u, MAGIC, None, ALU.subtract))
                        V(lambda Q=Q, ev=ev, r=r: v.scalar_tensor_tensor(
                            r, Q, -256.0, ev, ALU.mult, ALU.add))
                        V(lambda r=r, half=half: v.tensor_scalar(
                            pf[:, D + 96 + half * 16:D + 112 + half * 16],
                            r, -1.0, 127.0, ALU.mult, ALU.add))
                        V(lambda Q=Q, half=half: v.tensor_scalar(
                            pf[:, D + 128 + half * 16:D + 144 + half * 16],
                            Q, 2.0 ** -15, None, ALU.mult))
                    ce = pf[:, F_CE:F_CE + 256]
                    v0 = pf[:, D + 128:D + 144]
                    v1 = pf[:, D + 144:D + 160]
                    V(lambda: v.tensor_tensor(
                        ce.rearrange("p (a b) -> p a b", b=16),
                        v0[:, :].unsqueeze(2).to_broadcast([128, 16, 16]),
                        v1[:, :].unsqueeze(1).to_broadcast([128, 16, 16]),
                        ALU.add))
                    V(lambda: v.tensor_scalar(ce, ce, 16384.0, MAGIC,
                                              ALU.mult, ALU.add))
                    V(lambda: v.tensor_scalar(ce, ce, MAGIC, 512.0,
                                              ALU.subtract, ALU.mult))
                    V(lambda: v.tensor_add(ce, ce, cts[:, 128:384]))
                    cv = pf[:, D + 160:D + 176]
                    V(lambda: v.max(out=cv[:, 0:8], in_=ce))
                    V(lambda: v.match_replace(out=ce, in_to_replace=cv[:, 0:8],
                                              in_values=ce, imm_value=-1e9))
                    V(lambda: v.max(out=cv[:, 8:16], in_=ce))
                    u2 = pf[:, D + 176:D + 192]
                    Q2 = pf[:, D + 192:D + 208]
                    r2 = pf[:, D + 208:D + 224]
                    V(lambda: v.tensor_scalar(u2, cv, 1.0 / 512.0, MAGIC,
                                              ALU.mult, ALU.add))
                    V(lambda: v.tensor_scalar(Q2, u2, MAGIC, None, ALU.subtract))
                    V(lambda: v.scalar_tensor_tensor(r2, Q2, -512.0, cv,
                                                     ALU.mult, ALU.add))
                    code = pf[:, D + 224:D + 240]
                    V(lambda: v.tensor_scalar(code, r2, -1.0, 255.0,
                                              ALU.mult, ALU.add))
                    vc = pf[:, F_W + 32:F_W + 48]
                    V(lambda: v.tensor_scalar(vc, Q2, 2.0 ** -14, None,
                                              ALU.mult))
                    aa = pf[:, D + 240:D + 256]
                    V(lambda: v.tensor_scalar(aa, code, 0.0625, -0.46875,
                                              ALU.mult, ALU.add))
                    V(lambda: v.tensor_scalar(aa, aa, MAGIC, MAGIC,
                                              ALU.add, ALU.subtract))
                    bb2 = pf[:, F_E:F_E + 16]
                    V(lambda: v.scalar_tensor_tensor(bb2, aa, -16.0, code,
                                                     ALU.mult, ALU.add))
                    oh = pf[:, F_OH:F_OH + 256]
                    t4 = pf[:, F_OH + 256:F_OH + 512]
                    isel = pf[:, F_E + 16:F_E + 32]
                    jsel = pf[:, F_E + 32:F_E + 48]
                    i0l = pf[:, D + 96:D + 112]
                    j1l = pf[:, D + 112:D + 128]
                    V(lambda: v.tensor_tensor(
                        oh.rearrange("p (s w) -> p s w", w=16),
                        aa[:, :].unsqueeze(2).to_broadcast([128, 16, 16]),
                        cts[:, 384:400].unsqueeze(1).to_broadcast([128, 16, 16]),
                        ALU.is_equal))
                    V(lambda: v.tensor_tensor(
                        t4.rearrange("p (s w) -> p s w", w=16),
                        oh.rearrange("p (s w) -> p s w", w=16),
                        i0l[:, :].unsqueeze(1).to_broadcast([128, 16, 16]),
                        ALU.mult))
                    V(lambda: v.tensor_reduce(
                        isel, t4.rearrange("p (s w) -> p s w", w=16),
                        axis=AX, op=ALU.add))
                    V(lambda: v.tensor_tensor(
                        oh.rearrange("p (s w) -> p s w", w=16),
                        bb2[:, :].unsqueeze(2).to_broadcast([128, 16, 16]),
                        cts[:, 384:400].unsqueeze(1).to_broadcast([128, 16, 16]),
                        ALU.is_equal))
                    V(lambda: v.tensor_tensor(
                        t4.rearrange("p (s w) -> p s w", w=16),
                        oh.rearrange("p (s w) -> p s w", w=16),
                        j1l[:, :].unsqueeze(1).to_broadcast([128, 16, 16]),
                        ALU.mult))
                    V(lambda: v.tensor_reduce(
                        jsel, t4.rearrange("p (s w) -> p s w", w=16),
                        axis=AX, op=ALU.add))
                    nsel = pf[:, F_E + 48:F_E + 64]
                    V(lambda: v.scalar_tensor_tensor(nsel, isel, 128.0, jsel,
                                                     ALU.mult, ALU.add))
                    ui = (tt * 8 + h) * 16
                    V(lambda ui=ui: v.tensor_copy(idxu[:, ui:ui + 16], nsel))
                    V(lambda: v.tensor_scalar(pf[:, F_W + 16:F_W + 17],
                                              vc[:, 0:1], -1.0, None,
                                              ALU.mult))
                    ct.c["VS_tk_%d_%d" % (h, tt)] = ct.v("VS")

                @block.scalar
                def _(s, h=h, tt=tt):
                    Sc = mk_auto(ct, "SS", SS, s)
                    s.wait_ge(VS, ct.v("VS_tk_%d_%d" % (h, tt)))
                    Sc(lambda: s.activation(
                        pf[:, F_W:F_W + 16], pf[:, F_W + 32:F_W + 48], AF.Exp,
                        bias=pf[:, F_W + 16:F_W + 17], scale=1.0,
                        accum_out=pf[:, F_W + 17:F_W + 18]))
                ct.c["SS_w_%d_%d" % (h, tt)] = ct.v("SS")

                @block.vector
                def _(v, h=h, tt=tt):
                    V = mk_auto(ct, "VS", VS, v)
                    v.wait_ge(SS, ct.v("SS_w_%d_%d" % (h, tt)))
                    V(lambda: v.reciprocal(pf[:, F_W + 18:F_W + 19],
                                           pf[:, F_W + 17:F_W + 18]))
                    ui = (tt * 8 + h) * 16
                    V(lambda ui=ui: v.tensor_scalar(
                        pf[:, F_WT + ui:F_WT + ui + 16], pf[:, F_W:F_W + 16],
                        pf[:, F_W + 18:F_W + 19], None, ALU.mult))
                    ct.c["VS_w_%d_%d" % (h, tt)] = ct.v("VS")
        VS_route = ct.v("VS")

        NU = 2048

        @block.gpsimd
        def _(g):
            g.wait_ge(VS, VS_route)
            for u in range(NU):
                if u >= 8:
                    g.wait_ge(UPS, u - 7)
                buf = gbuf(u)
                g.indirect_dma_start(
                    out=pb[:, buf:buf + 2048], out_offset=None,
                    in_=tabb,
                    in_offset=IndirectOffsetOnAxis(ap=idxu[:, u:u + 1], axis=0),
                ).then_inc(GT, 16)

        LD_b2base = ct.v("LD")

        @block.scalar
        def _(s):
            s.wait_ge(VS, VS_route)
            for u in range(NU):
                s.wait_ge(DSS, u + 1)
                s.activation(pf[:, F_EI + 2 + u % 2:F_EI + 3 + u % 2],
                             pf[:, F_EI + u % 2:F_EI + u % 2 + 1], AF.Gelu,
                             bias=0.0, scale=1.0).then_inc(GLS, 1)

        def emit_cmul_up(v, up):
            utt = up // 128
            uhk = up % 128
            uh, uk = uhk // 16, uhk % 16
            wcol = F_WT + (utt * 8 + uh) * 16 + uk
            ub = gbuf(up)
            ya = F_YA + (utt % 2) * 1024
            v.wait_ge(GLS, up + 1)
            v.tensor_tensor(
                pf[:, F_EI + 4 + up % 2:F_EI + 5 + up % 2],
                pf[:, F_EI + 2 + up % 2:F_EI + 3 + up % 2],
                pf[:, wcol:wcol + 1], ALU.mult).then_inc(VS, 1)
            v.wait_ge(VS, ct.add("VS"))
            v.scalar_tensor_tensor(
                pf[:, ya:ya + 1024], pb[:, ub + 1024:ub + 2048],
                pf[:, F_EI + 4 + up % 2:F_EI + 5 + up % 2],
                pf[:, ya:ya + 1024], ALU.mult, ALU.add).then_inc(UPS, 1)

        def emit_epilogue(v, et):
            ya = F_YA + (et % 2) * 1024
            xa = F_XA + (et % 2) * 1024
            aa2 = F_XA + 2048 + (et % 2) * 1024
            v.tensor_add(pf[:, ya:ya + 1024], pf[:, ya:ya + 1024],
                         pf[:, xa:xa + 1024]).then_inc(VS, 1)
            v.wait_ge(VS, ct.add("VS"))
            v.tensor_add(pf[:, ya:ya + 1024], pf[:, ya:ya + 1024],
                         pf[:, aa2:aa2 + 1024]).then_inc(VS, 1)
            v.wait_ge(VS, ct.add("VS"))
            ct.c["VS_x1_%d" % et] = ct.v("VS")

        @block.vector
        def _(v):
            for tt in range(16):
                v.wait_ge(LD, LD_b2base + 48 * (tt + 1))
                if tt >= 2:
                    v.wait_ge(OS, (tt - 1) * 16)
                v.memset(pf[:, F_YA + (tt % 2) * 1024:
                            F_YA + (tt % 2) * 1024 + 1024], 0.0).then_inc(VS, 1)
                v.wait_ge(VS, ct.add("VS"))
                for hk in range(128):
                    u = tt * 128 + hk
                    buf = gbuf(u)
                    v.wait_ge(GT, 16 * (u + 1))
                    v.scalar_tensor_tensor(
                        pb[:, R_JK:R_JK + 1024], pb[:, buf:buf + 1024], 1.0,
                        pb[:, R_XT + (tt % 2) * 1024:
                           R_XT + (tt % 2) * 1024 + 1024],
                        ALU.mult, ALU.mult,
                        accum_out=pf[:, F_EI + u % 2:F_EI + u % 2 + 1]
                    ).then_inc(DSS, 1)
                    if u >= 1:
                        emit_cmul_up(v, u - 1)
                    if hk == 0 and tt > 0:
                        emit_epilogue(v, tt - 1)
            emit_cmul_up(v, NU - 1)
            emit_epilogue(v, 15)

        @block.sync
        def _(sy):
            def x1dma(k):
                sy.wait_ge(VS, ct.v("VS_x1_%d" % k))
                sy.dma_start(out=x1[k * 128:(k + 1) * 128, :],
                             in_=pf[:, F_YA + (k % 2) * 1024:
                                    F_YA + (k % 2) * 1024 + 1024]
                             ).then_inc(OS, 16)
            for tt in range(16):
                if tt >= 2:
                    x1dma(tt - 2)
                    sy.wait_ge(VS, ct.v("VS_x1_%d" % (tt - 2)))
                    sy.wait_ge(DSS, (tt - 1) * 128)
                sy.dma_start(out=pf[:, F_XA + (tt % 2) * 1024:
                                   F_XA + (tt % 2) * 1024 + 1024],
                             in_=xloc[tt * 128:(tt + 1) * 128, :]
                             ).then_inc(LD, 16)
                sy.dma_start(out=pf[:, F_XA + 2048 + (tt % 2) * 1024:
                                   F_XA + 2048 + (tt % 2) * 1024 + 1024],
                             in_=a0[tt * 128:(tt + 1) * 128, :]).then_inc(LD, 16)
                sy.dma_start(out=pb[:, R_XT + (tt % 2) * 1024:
                                   R_XT + (tt % 2) * 1024 + 1024],
                             in_=xntm[tt * 128:(tt + 1) * 128, :]
                             ).then_inc(LD, 16)
                ct.add("LD", 48)
            x1dma(14)
            x1dma(15)
            sy.wait_ge(OS, 16 * 16)
            if DEBUG_B:
                sy.dma_start(out=didx, in_=idxu[:]).then_inc(OS, 16)
                sy.dma_start(out=dwt, in_=pf[:, F_WT:F_WT + 2048]).then_inc(OS, 16)
                sy.dma_start(out=dq, in_=pb[:, R_Q:R_Q + 4096]).then_inc(OS, 16)
                sy.dma_start(out=drn, in_=pf[:, F_RN:F_RN + 32]).then_inc(OS, 16)
                sy.wait_ge(OS, 16 * 16 + 64)


def phase_c1(nc, io):
    x1 = io["x1"]
    wr1, br1_l, wr2, br2_r = io["wr1"], io["br1_l"], io["wr2"], io["br2_r"]
    we1, be1_l, we2, be2_9 = io["we1"], io["be1_l"], io["we2"], io["be2_9"]
    oneh8, identb = io["oneh8"], io["identb"]
    accD, xnfm1, xntm1 = io["accD"], io["xnfm1"], io["xntm1"]

    R_XN = 0
    R_W1 = 16384
    R_H = 49152
    R_W2 = 65536
    R_WB = 69632
    R_R1 = 70656
    R_WR1 = 74752
    R_WR2 = 76800
    R_W8 = 76816
    R_WRO = 76824
    R_SC = 76832
    R_Y = 78880
    R_ON = 80928
    R_OH8 = 81064
    R_BE2 = 82088
    R_BE2B = 83112
    PBN = 84136
    F_LN = 0
    F_BR1 = 128
    F_LG = 136
    F_X1 = 192
    F_BE = 2240
    F_BE2 = 2528
    F_ONR = 2600
    PFN = 2752

    ct = Ct()
    import contextlib
    with contextlib.ExitStack() as _st:
        E = _st.enter_context
        pb = E(nc.sbuf_tensor("Cpb", [128, PBN], BF16))
        pf = E(nc.sbuf_tensor("Cpf", [128, PFN], F32))
        cst = E(nc.sbuf_tensor("Ccst", [128, 8], F32))
        for _i, _v in enumerate([1e-5]):
            nc.const_aps.register(F32, _v, cst[:, _i:_i + 1])
        wrows = E(nc.sbuf_tensor("Cwrows", [8, 2048], BF16))
        idb = E(nc.sbuf_tensor("Cidb", [128, 128], BF16))
        LD = E(nc.semaphore("CLD"))
        GS = E(nc.semaphore("CGS"))
        VS = E(nc.semaphore("CVS"))
        SS = E(nc.semaphore("CSS"))
        TS = E(nc.semaphore("CTS"))
        HTS = E(nc.semaphore("CHTS"))
        HES = E(nc.semaphore("CHES"))
        HGS = E(nc.semaphore("CHGS"))
        YTS = E(nc.semaphore("CYTS"))
        YES = E(nc.semaphore("CYES"))
        W2S = E(nc.semaphore("CW2S"))
        WBS = E(nc.semaphore("CWBS"))
        OS = E(nc.semaphore("COS"))
        INITS = E(nc.semaphore("CINITS"))
        block = E(nc.Block())

        with (nc.psum_tensor("CpsT0", [128, 128], BF16) as psT0,
              nc.psum_tensor("CpsT1", [128, 128], BF16) as psT1,
              nc.psum_tensor("CpsR0", [128, 512], F32) as psR0,
              nc.psum_tensor("CpsR1", [128, 512], F32) as psR1):

            @block.sync
            def _(sy):
                sy.dma_start(out=idb[:], in_=identb).then_inc(LD, 16)
                sy.dma_start(out=pf[:, 2240:2528], in_=be1_l).then_inc(LD, 16)
                sy.dma_start(out=pf[0:1, 2528:2536], in_=br2_r).then_inc(LD, 16)
                sy.dma_start(out=pf[:, 128:130], in_=br1_l).then_inc(LD, 16)
                sy.dma_start(out=pb[0:8, R_OH8:R_OH8 + 1024],
                             in_=oneh8).then_inc(LD, 16)
                ct.add("LD", 5 * 16)
            LD_c = ct.v("LD")

            @block.gpsimd
            def _(g):
                g.dma_start(out=accD, in_=x1).then_inc(INITS, 16)
                g.dma_start(out=pb[0:8, R_BE2:R_BE2 + 1024],
                            in_=be2_9[0:8, :]).then_inc(GS, 16)
                g.dma_start(out=pb[0:1, R_BE2B:R_BE2B + 1024],
                            in_=be2_9[8:9, :]).then_inc(GS, 16)
                ct.add("GS", 32)
                g.dma_start(out=pb[:, R_W1:R_W1 + 16384]
                            .rearrange("p (i c) -> p i c", c=DM),
                            in_=x1.rearrange("(i p) c -> p i c", p=128)
                            ).then_inc(GS, 16)
                ct.add("GS", 16)
            GS_x1 = ct.v("GS")

            @block.vector
            def _(v):
                V = mk_auto(ct, "VS", VS, v)
                V(lambda: v.memset(cst[:, 0:1], 1e-5))
                V(lambda: v.memset(pb[:, R_ON:R_ON + 1], 1.0))
                V(lambda: v.memset(pb[0:1, R_ON + 2:R_ON + 130], 1.0))
                V(lambda: v.memset(pf[0:1, F_ONR:F_ONR + 128], 1.0))
            VS_init = ct.v("VS")

            @block.vector
            def _(v):
                V = mk_auto(ct, "VS", VS, v)
                v.wait_ge(GS, GS_x1)
                V(lambda: v.tensor_reduce(
                    pf[:, 0:16],
                    pb[:, R_W1:R_W1 + 16384].rearrange("p (i c) -> p i c", c=DM),
                    axis=AX, op=ALU.add))
                V(lambda: v.tensor_scalar(pf[:, 16:32], pf[:, 0:16], -1.0 / DM,
                                          None, ALU.mult))

            @block.scalar
            def _(s):
                Sc = mk_auto(ct, "SS", SS, s)
                s.wait_ge(GS, GS_x1)
                for i in range(16):
                    Sc(lambda i=i: s.activation(
                        pb[:, R_H:R_H + 1024],
                        pb[:, R_W1 + i * 1024:R_W1 + (i + 1) * 1024], AF.Square,
                        bias=0.0, scale=1.0, accum_out=pf[:, 32 + i:33 + i]))
            SS_p1sq = ct.v("SS")

            @block.vector
            def _(v):
                V = mk_auto(ct, "VS", VS, v)
                v.wait_ge(SS, SS_p1sq)
                V(lambda: v.tensor_scalar(pf[:, 32:48], pf[:, 32:48], 1.0 / DM,
                                          None, ALU.mult))
                V(lambda: v.tensor_tensor(pf[:, 48:64], pf[:, 16:32],
                                          pf[:, 16:32], ALU.mult))
                V(lambda: v.tensor_sub(pf[:, 32:48], pf[:, 32:48],
                                       pf[:, 48:64]))
            VS_p1b = ct.v("VS")

            @block.scalar
            def _(s):
                Sc = mk_auto(ct, "SS", SS, s)
                s.wait_ge(VS, VS_p1b)
                Sc(lambda: s.activation(pf[:, 48:64], pf[:, 32:48], AF.Sqrt,
                                        bias=1e-5, scale=1.0))
            SS_p1sd = ct.v("SS")

            @block.vector
            def _(v):
                V = mk_auto(ct, "VS", VS, v)
                v.wait_ge(SS, SS_p1sd)
                V(lambda: v.reciprocal(pf[:, 64:80], pf[:, 48:64]))

            if C1_PHASES <= 0.3:
                @block.sync
                def _(sy):
                    sy.wait_ge(OS, ct.v("OS"))
                    sy.wait_ge(INITS, 16)
                return nc
            SS_e1base = ct.v("SS")
            for i in range(16):
                xnt = R_SC + (i % 2) * 1024

                @block.vector
                def _(v, i=i, xnt=xnt):
                    V = mk_auto(ct, "VS", VS, v)
                    if i >= 2:
                        v.wait_ge(TS, ct.v("TS_p1_%d" % (i - 2)))
                        v.wait_ge(OS, 16 * (i - 1))
                    V(lambda: v.tensor_scalar(
                        pb[:, xnt:xnt + 1024],
                        pb[:, R_W1 + i * 1024:R_W1 + (i + 1) * 1024],
                        pf[:, 16 + i:17 + i], pf[:, 64 + i:65 + i],
                        ALU.add, ALU.mult))
                    ct.c["VS_p1n_%d" % i] = ct.v("VS")

                @block.sync
                def _(sy, i=i, xnt=xnt):
                    sy.wait_ge(VS, ct.v("VS_p1n_%d" % i))
                    sy.dma_start(out=xntm1[i * 128:(i + 1) * 128, :],
                                 in_=pb[:, xnt:xnt + 1024]).then_inc(OS, 16)
                    ct.add("OS", 16)

                @block.tensor
                def _(t, i=i, xnt=xnt):
                    t.wait_ge(VS, ct.v("VS_p1n_%d" % i))
                    t.wait_ge(LD, LD_c)
                    for fc in range(8):
                        pt = psT0 if fc % 2 == 0 else psT1
                        if i * 8 + fc >= 2:
                            t.wait_ge(SS, SS_e1base + i * 8 + fc - 1)
                        t.transpose(pt[:],
                                    pb[:, xnt + fc * 128:xnt + (fc + 1) * 128],
                                    idb[:]).then_inc(TS, 1)
                        ct.add("TS")
                        ct.c["TS_p1t_%d" % (i * 8 + fc)] = ct.v("TS")
                    ct.c["TS_p1_%d" % i] = ct.v("TS")

                @block.scalar
                def _(s, i=i):
                    Sc = mk_auto(ct, "SS", SS, s)
                    for fc in range(8):
                        pt = psT0 if fc % 2 == 0 else psT1
                        s.wait_ge(TS, ct.v("TS_p1t_%d" % (i * 8 + fc)))
                        Sc(lambda pt=pt, fc=fc: s.activation(
                            pb[:, R_XN + fc * 2048 + i * 128:
                               R_XN + fc * 2048 + i * 128 + 128],
                            pt[:], AF.Copy, bias=0.0, scale=1.0))
            SS_xn = ct.v("SS")
            VS_xn = ct.v("VS_p1n_15")

            @block.sync
            def _(sy):
                sy.wait_ge(SS, SS_xn)
                sy.dma_start(out=xnfm1,
                             in_=pb[:, R_XN:R_XN + 16384]).then_inc(OS, 16)
                ct.add("OS", 16)

            if C1_PHASES <= 0.5:
                @block.sync
                def _(sy):
                    sy.wait_ge(OS, ct.v("OS"))
                    sy.wait_ge(INITS, 16)
                return nc
            @block.gpsimd
            def _(g):
                for fc in range(8):
                    g.dma_start(out=pb[:, R_WR1 + fc * 256:
                                   R_WR1 + (fc + 1) * 256],
                                in_=wr1[fc * 128:(fc + 1) * 128, :]
                                ).then_inc(GS, 16)
                g.dma_start(out=pb[:, R_WR2:R_WR2 + 16]
                            .rearrange("p (a b) -> p a b", b=8),
                            in_=wr2.rearrange("(a p) b -> p a b", p=128)
                            ).then_inc(GS, 16)
                g.dma_start(out=pf[0:1, 130:138][0:1, 0:8]
                            if False else pf[0:1, 2728:2736],
                            in_=br2_r).then_inc(GS, 16)
                ct.add("GS", 10 * 16)
            GS_r = ct.v("GS")

            @block.tensor
            def _(t):
                t.wait_ge(GS, GS_r)
                t.wait_ge(SS, SS_xn)
                ssb = ct.v("SS")
                for rc in range(2):
                    for nb in range(4):
                        j = rc * 4 + nb
                        ps = psR0 if j % 2 == 0 else psR1
                        if j >= 2:
                            t.wait_ge(SS, ssb + j - 1)
                        for fc in range(8):
                            inst = t.matmul(
                                ps[:],
                                pb[:, R_WR1 + fc * 256 + rc * 128:
                                   R_WR1 + fc * 256 + rc * 128 + 128],
                                pb[:, R_XN + fc * 2048 + nb * 512:
                                   R_XN + fc * 2048 + nb * 512 + 512],
                                start=(fc == 0), stop=(fc == 7))
                        inst.then_inc(TS, 1)
                        ct.add("TS")
                        ct.c["TS_r1_%d" % j] = ct.v("TS")
            ct.c["SS_r1base"] = ct.v("SS")

            @block.scalar
            def _(s):
                Sc = mk_auto(ct, "SS", SS, s)
                for rc in range(2):
                    for nb in range(4):
                        j = rc * 4 + nb
                        ps = psR0 if j % 2 == 0 else psR1
                        s.wait_ge(TS, ct.v("TS_r1_%d" % j))
                        Sc(lambda ps=ps, rc=rc, nb=nb: s.activation(
                            pb[:, R_R1 + rc * 2048 + nb * 512:
                               R_R1 + rc * 2048 + (nb + 1) * 512],
                            ps[:], AF.Relu, bias=pf[:, 128 + rc:129 + rc],
                            scale=1.0))
            SS_r1 = ct.v("SS")
            if C1_PHASES <= 0.7:
                @block.sync
                def _(sy):
                    sy.wait_ge(OS, ct.v("OS"))
                    sy.wait_ge(SS, SS_r1)
                    sy.wait_ge(INITS, 16)
                return nc

            @block.tensor
            def _(t):
                t.wait_ge(VS, VS_init)
                t.wait_ge(GS, GS_r)
                t.matmul(psR0[:, 0:8], pf[0:1, F_ONR:F_ONR + 128],
                         pf[0:1, 2728:2736], start=True,
                         stop=True).then_inc(TS, 1)
                ct.add("TS")
                ct.c["TS_br2"] = ct.v("TS")

            @block.vector
            def _(v):
                V = mk_auto(ct, "VS", VS, v)
                v.wait_ge(TS, ct.v("TS_br2"))
                V(lambda: v.tensor_copy(pf[:, 2736:2744], psR0[:, 0:8]))
            VS_br2 = ct.v("VS")

            @block.tensor
            def _(t):
                t.wait_ge(VS, VS_br2)
                t.wait_ge(GS, GS_x1)
                for cb in range(2):
                    ps = psR0 if cb == 0 else psR1
                    t.matmul(ps[:], pb[0:1, R_ON + 2:R_ON + 130],
                             pb[0:1, R_BE2B + cb * 512:R_BE2B + (cb + 1) * 512],
                             start=True, stop=True).then_inc(TS, 1)
                    ct.add("TS")
                ct.c["TS_be2b"] = ct.v("TS")

            @block.vector
            def _(v):
                V = mk_auto(ct, "VS", VS, v)
                v.wait_ge(TS, ct.v("TS_be2b"))
                V(lambda: v.tensor_copy(pb[:, R_BE2B:R_BE2B + 512],
                                        psR0[:]))
                V(lambda: v.tensor_copy(pb[:, R_BE2B + 512:R_BE2B + 1024],
                                        psR1[:]))
            VS_be2b = ct.v("VS")

            for tt in range(16):
                @block.tensor
                def _(t, tt=tt):
                    t.wait_ge(SS, SS_r1)
                    t.wait_ge(VS, VS_be2b)
                    if tt >= 1:
                        t.wait_ge(VS, ct.v("VS_lga_%d" % (tt - 1)))
                    for rc in range(2):
                        inst = t.matmul(
                            psR0[:, 0:8],
                            pb[:, R_R1 + rc * 2048 + tt * 128:
                               R_R1 + rc * 2048 + tt * 128 + 128],
                            pb[:, R_WR2 + rc * 8:R_WR2 + (rc + 1) * 8],
                            start=(rc == 0), stop=(rc == 1))
                    inst.then_inc(TS, 1)
                    ct.add("TS")
                    ct.c["TS_lg_%d" % tt] = ct.v("TS")

                @block.vector
                def _(v, tt=tt):
                    V = mk_auto(ct, "VS", VS, v)
                    v.wait_ge(TS, ct.v("TS_lg_%d" % tt))
                    lg = pf[:, 136:144]
                    V(lambda: v.tensor_tensor(lg, psR0[:, 0:8],
                                              pf[:, 2736:2744], ALU.add))
                    ct.c["VS_lga_%d" % tt] = ct.v("VS")
                    V(lambda: v.tensor_reduce(pf[:, 144:145], lg, axis=AX,
                                              op=ALU.max))
                    V(lambda: v.tensor_scalar(pf[:, 152:160], lg,
                                              pf[:, 144:145], None,
                                              ALU.is_equal))
                    V(lambda: v.scalar_tensor_tensor(
                        pf[:, 160:168], pf[:, 152:160], -1e30, lg,
                        ALU.mult, ALU.add))
                    V(lambda: v.tensor_reduce(pf[:, 145:146], pf[:, 160:168],
                                              axis=AX, op=ALU.max))
                    V(lambda: v.tensor_scalar(pf[:, 168:176], pf[:, 160:168],
                                              pf[:, 145:146], None,
                                              ALU.is_equal))
                    V(lambda: v.tensor_scalar(pf[:, 146:147], pf[:, 144:145],
                                              -1.0, None, ALU.mult))
                    ct.c["VS_lg1_%d" % tt] = ct.v("VS")

                @block.scalar
                def _(s, tt=tt):
                    Sc = mk_auto(ct, "SS", SS, s)
                    s.wait_ge(VS, ct.v("VS_lg1_%d" % tt))
                    Sc(lambda: s.activation(pf[:, 184:192], pf[:, 136:144],
                                            AF.Exp, bias=pf[:, 146:147],
                                            scale=1.0,
                                            accum_out=pf[:, 147:148]))
                    Sc(lambda: s.activation(pf[:, 148:149], pf[:, 145:146],
                                            AF.Exp, bias=pf[:, 146:147],
                                            scale=1.0))
                ct.c["SS_lg_%d" % tt] = ct.v("SS")

                @block.vector
                def _(v, tt=tt):
                    V = mk_auto(ct, "VS", VS, v)
                    v.wait_ge(SS, ct.v("SS_lg_%d" % tt))
                    V(lambda: v.reciprocal(pf[:, 149:150], pf[:, 147:148]))
                    V(lambda: v.tensor_tensor(pf[:, 150:151], pf[:, 148:149],
                                              pf[:, 149:150], ALU.mult))
                    ct.c["VS_lg2_%d" % tt] = ct.v("VS")

                @block.scalar
                def _(s, tt=tt):
                    Sc = mk_auto(ct, "SS", SS, s)
                    s.wait_ge(VS, ct.v("VS_lg2_%d" % tt))
                    Sc(lambda: s.activation(pf[:, 151:152], pf[:, 149:150],
                                            AF.Exp, bias=0.0, scale=0.5))
                    Sc(lambda: s.activation(pf[:, 184:185], pf[:, 150:151],
                                            AF.Exp, bias=0.0, scale=0.5))
                ct.c["SS_lg2_%d" % tt] = ct.v("SS")

                @block.vector
                def _(v, tt=tt):
                    V = mk_auto(ct, "VS", VS, v)
                    v.wait_ge(SS, ct.v("SS_lg2_%d" % tt))
                    V(lambda: v.tensor_add(pf[:, 185:186], pf[:, 151:152],
                                           pf[:, 184:185]))
                    V(lambda: v.reciprocal(pf[:, 186:187], pf[:, 185:186]))
                    V(lambda: v.tensor_tensor(pf[:, 187:188], pf[:, 151:152],
                                              pf[:, 186:187], ALU.mult))
                    V(lambda: v.tensor_tensor(pf[:, 188:189], pf[:, 184:185],
                                              pf[:, 186:187], ALU.mult))
                    V(lambda: v.tensor_scalar(pf[:, 160:168], pf[:, 152:160],
                                              pf[:, 187:188], None, ALU.mult))
                    V(lambda: v.scalar_tensor_tensor(
                        pf[:, 160:168], pf[:, 168:176], pf[:, 188:189],
                        pf[:, 160:168], ALU.mult, ALU.add))
                    V(lambda: v.tensor_copy(pb[:, R_W8:R_W8 + 8],
                                            pf[:, 160:168]))
                    ct.c["VS_lg_%d" % tt] = ct.v("VS")

                @block.tensor
                def _(t, tt=tt):
                    t.wait_ge(VS, ct.v("VS_lg_%d" % tt))
                    if tt >= 2:
                        t.wait_ge(SS, ct.v("SS_w8_%d" % (tt - 2)))
                    t.transpose(psT0[0:8, :] if tt % 2 == 0 else psT1[0:8, :],
                                pb[:, R_W8:R_W8 + 8], idb[:]).then_inc(TS, 1)
                    ct.add("TS")
                    ct.c["TS_w8_%d" % tt] = ct.v("TS")

                @block.scalar
                def _(s, tt=tt):
                    Sc = mk_auto(ct, "SS", SS, s)
                    s.wait_ge(TS, ct.v("TS_w8_%d" % tt))
                    Sc(lambda tt=tt: s.activation(
                        wrows[:, tt * 128:(tt + 1) * 128],
                        (psT0[0:8, :] if tt % 2 == 0 else psT1[0:8, :]),
                        AF.Copy, bias=0.0, scale=1.0))
                ct.c["SS_w8_%d" % tt] = ct.v("SS")
            SS_router = ct.v("SS")
        if C1_PHASES <= 1:
            @block.sync
            def _(sy):
                sy.wait_ge(OS, ct.v("OS"))
                sy.wait_ge(INITS, 16)
            return nc
        with (nc.psum_tensor("CpsY0", [128, 1024], F32) as psY0,
              nc.psum_tensor("CpsY1", [128, 1024], F32) as psY1,
              nc.psum_tensor("CpsY2", [128, 1024], F32) as psY2,
              nc.psum_tensor("CpsY3", [128, 1024], F32) as psY3):
            psY = [psY0, psY1, psY2, psY3]
            with nc.semaphore("CWBE2") as WBE2, nc.semaphore("CCCS2") as CCS2:

                @block.gpsimd
                def _(g):
                    for e in range(9):
                        if e == 0:
                            g.wait_ge(SS, SS_xn)
                            g.wait_ge(VS, VS_xn)
                        else:
                            g.wait_ge(HTS, e * 128)
                        for fc in range(8):
                            g.dma_start(out=pb[:, R_W1 + fc * 4096:
                                           R_W1 + (fc + 1) * 4096],
                                        in_=we1[e, fc * 128:(fc + 1) * 128, :]
                                        ).then_inc(GS, 16)
                        ct.add("GS", 8 * 16)
                        ct.c["GS_we1_%d" % e] = ct.v("GS")
                        for nb in range(4):
                            m = e * 4 + nb
                            for ii in range(32):
                                k = m * 32 + ii
                                if k >= 4:
                                    g.wait_ge(YTS, k - 3)
                                g.dma_start(out=pb[:, R_W2 + (k % 4) * 1024:
                                               R_W2 + (k % 4) * 1024 + 1024],
                                            in_=we2[e, ii * 128:
                                                    (ii + 1) * 128, :]
                                            ).then_inc(W2S, 16)
                                ct.add("W2S", 16)
                            for tt2 in range(4):
                                j = m * 4 + tt2
                                if j == 0:
                                    g.wait_ge(INITS, 16)
                                g.wait_ge(YES, j + 1)
                                g.dma_start(
                                    out=accD[nb * 512 + tt2 * 128:
                                             nb * 512 + (tt2 + 1) * 128, :],
                                    in_=pb[:, R_Y + (j % 2) * 1024:
                                           R_Y + (j % 2) * 1024 + 1024],
                                    accum_op=ALU.add).then_inc(CCS2, 16)
                                ct.add("CCS2", 16)
                GS_exp = ct.v("GS")

                @block.tensor
                def _(t):
                    t.wait_ge(SS, SS_router)
                    for e in range(9):
                        t.wait_ge(GS, ct.v("GS_we1_%d" % e))
                        for nb in range(4):
                            m = e * 4 + nb
                            if e < 8:
                                if m >= 1:
                                    t.wait_ge(YES, m * 4)
                                t.matmul(psY2[:, 0:512],
                                         pb[0:8, R_OH8 + e * 128:
                                            R_OH8 + (e + 1) * 128],
                                         wrows[0:8, nb * 512:(nb + 1) * 512],
                                         start=True, stop=True).then_inc(WBS, 1)
                                ct.add("WBS")
                            for ii in range(32):
                                hg = m * 32 + ii
                                ph = psY3[:, (hg % 2) * 512:
                                          (hg % 2) * 512 + 512]
                                if ii < 2 and m >= 1:
                                    t.wait_ge(YES, m * 4)
                                if hg >= 2:
                                    t.wait_ge(HES, hg - 1)
                                for fc in range(8):
                                    inst = t.matmul(
                                        ph,
                                        pb[:, R_W1 + fc * 4096 + ii * 128:
                                           R_W1 + fc * 4096 + ii * 128 + 128],
                                        pb[:, R_XN + fc * 2048 + nb * 512:
                                           R_XN + fc * 2048 + nb * 512 + 512],
                                        start=(fc == 0), stop=(fc == 7))
                                inst.then_inc(HTS, 1)
                            for ii in range(32):
                                k = m * 32 + ii
                                t.wait_ge(W2S, 16 * (k + 1))
                                if e < 8:
                                    t.wait_ge(HGS, k + 1)
                                else:
                                    t.wait_ge(HES, k + 1)
                                if ii == 0:
                                    t.wait_ge(HES, m * 32 + 32)
                                    if e < 8:
                                        t.wait_ge(WBE2, m + 1)
                                    if m >= 1:
                                        t.wait_ge(YES, m * 4)
                                for tc in range(8):
                                    tt2, cb = tc // 2, tc % 2
                                    inst = t.matmul(
                                        psY[tt2][:, cb * 512:(cb + 1) * 512],
                                        pb[:, R_H + ii * 512 + tt2 * 128:
                                           R_H + ii * 512 + tt2 * 128 + 128],
                                        pb[:, R_W2 + (k % 4) * 1024 + cb * 512:
                                           R_W2 + (k % 4) * 1024 +
                                           (cb + 1) * 512],
                                        start=(ii == 0), stop=(ii == 31))
                                inst.then_inc(YTS, 1)
                    t.wait_ge(YES, 36 * 4)
                    for tt in range(16):
                        if tt >= 1:
                            t.wait_ge(YES, 144 + tt)
                        for cb in range(2):
                            inst = t.matmul(
                                psY0[:, cb * 512:(cb + 1) * 512],
                                wrows[0:8, tt * 128:(tt + 1) * 128],
                                pb[0:8, R_BE2 + cb * 512:R_BE2 +
                                   (cb + 1) * 512],
                                start=True, stop=True)
                        inst.then_inc(YTS, 1)

                @block.scalar
                def _(s):
                    for e in range(9):
                        for nb in range(4):
                            m = e * 4 + nb
                            for ii in range(32):
                                hg = m * 32 + ii
                                ph = psY3[:, (hg % 2) * 512:
                                          (hg % 2) * 512 + 512]
                                if ii == 0 and m >= 1:
                                    s.wait_ge(YTS, m * 32)
                                s.wait_ge(HTS, hg + 1)
                                s.activation(
                                    pb[:, R_H + ii * 512:R_H + (ii + 1) * 512],
                                    ph, AF.Silu,
                                    bias=pf[:, 2240 + e * 32 + ii:
                                            2241 + e * 32 + ii],
                                    scale=1.0).then_inc(HES, 1)

                @block.vector
                def _(v):
                    for e in range(9):
                        for nb in range(4):
                            m = e * 4 + nb
                            if e < 8:
                                v.wait_ge(WBS, m + 1)
                                v.tensor_copy(pb[:, R_WB + (m % 2) * 512:
                                                 R_WB + (m % 2) * 512 + 512],
                                              psY2[:, 0:512]).then_inc(WBE2, 1)
                                v.wait_ge(WBE2, m + 1)
                                for ii in range(32):
                                    hg = m * 32 + ii
                                    v.wait_ge(HES, hg + 1)
                                    v.tensor_tensor(
                                        pb[:, R_H + ii * 512:
                                           R_H + (ii + 1) * 512],
                                        pb[:, R_H + ii * 512:
                                           R_H + (ii + 1) * 512],
                                        pb[:, R_WB + (m % 2) * 512:
                                           R_WB + (m % 2) * 512 + 512],
                                        ALU.mult).then_inc(HGS, 1)
                            for tt2 in range(4):
                                j = m * 4 + tt2
                                v.wait_ge(YTS, (m + 1) * 32)
                                if j >= 2:
                                    v.wait_ge(CCS2, 16 * (j - 1))
                                if e == 8:
                                    v.scalar_tensor_tensor(
                                        pb[:, R_Y + (j % 2) * 1024:
                                           R_Y + (j % 2) * 1024 + 1024],
                                        psY[tt2][:, 0:1024], 1.0,
                                        pb[:, R_BE2B:R_BE2B + 1024],
                                        ALU.mult, ALU.add).then_inc(YES, 1)
                                else:
                                    v.tensor_copy(
                                        pb[:, R_Y + (j % 2) * 1024:
                                           R_Y + (j % 2) * 1024 + 1024],
                                        psY[tt2][:, 0:1024]).then_inc(YES, 1)
                    for tt in range(16):
                        j = 144 + tt
                        v.wait_ge(YTS, 36 * 32 + tt + 1)
                        if j >= 2:
                            v.wait_ge(CCS2, 16 * (j - 1))
                        v.tensor_copy(pb[:, R_Y + (j % 2) * 1024:
                                         R_Y + (j % 2) * 1024 + 1024],
                                      psY0[:, 0:1024]).then_inc(YES, 1)

                @block.gpsimd
                def _(g):
                    for tt in range(16):
                        j = 144 + tt
                        g.wait_ge(YES, j + 1)
                        g.dma_start(out=accD[tt * 128:(tt + 1) * 128, :],
                                    in_=pb[:, R_Y + (j % 2) * 1024:
                                           R_Y + (j % 2) * 1024 + 1024],
                                    accum_op=ALU.add).then_inc(CCS2, 16)
                        ct.add("CCS2", 16)

                @block.sync
                def _(sy):
                    sy.wait_ge(CCS2, ct.v("CCS2"))
                    sy.wait_ge(OS, ct.v("OS"))

def phase_c2(nc, io):
    xnfm1 = io["xnfm1"]
    wqT, wkT, wvT = io["q1T"], io["k1T"], io["v1T"]
    wbT, bb_c = io["wbT"], io["bb_c"]
    cosf, rsf = io["cosf"], io["rsf"]
    swapb, selb, oneh16, identb = (io["swapb"], io["selb"], io["oneh16"],
                                   io["identb"])
    qh, kh, khtm, vsb, aalld = (io["qh"], io["kh"], io["khtm"], io["vsb"],
                                io["aalld"])
    summ = io["sumcat"][:, 0:4160]
    sump = io["sumcat"][:, 4160:4161]

    R_XN = 0
    R_KT = 16384
    R_V = 32768
    R_W = 49408
    R_PB = 65792
    R_RT = 67840
    R_HB = 69376
    R_SW = 71424
    R_SEL = 71552
    R_OH = 72576
    R_AB = 73600
    R_SC2 = 77696
    PBN = 79744
    F_BT = 0
    F_AA = 2048
    F_RI = 4096
    F_AE = 6144
    F_S = 6416
    F_CS = 6560
    PFN = 10656

    ct = Ct()
    import contextlib
    with contextlib.ExitStack() as _st:
        E = _st.enter_context
        pb = E(nc.sbuf_tensor("Epb", [128, PBN], BF16))
        pf = E(nc.sbuf_tensor("Epf", [128, PFN], F32))
        pfoh = E(nc.sbuf_tensor("Epfoh", [16, 1024], F32))
        idb = E(nc.sbuf_tensor("Eidb", [128, 128], BF16))
        psA = E(nc.psum_tensor("EpsA", [128, 512], F32))
        psB = E(nc.psum_tensor("EpsB", [128, 512], F32))
        psC = E(nc.psum_tensor("EpsC", [128, 512], F32))
        psD = E(nc.psum_tensor("EpsD", [128, 512], F32))
        psS = E(nc.psum_tensor("EpsS", [64, 512], F32))
        psT0 = E(nc.psum_tensor("EpsT0", [128, 128], BF16))
        psT1 = E(nc.psum_tensor("EpsT1", [128, 128], BF16))
        LD = E(nc.semaphore("ELD"))
        GS = E(nc.semaphore("EGS"))
        VS = E(nc.semaphore("EVS"))
        SS = E(nc.semaphore("ESS"))
        TS = E(nc.semaphore("ETS"))
        OS = E(nc.semaphore("EOS"))
        block = E(nc.Block())

        @block.sync
        def _(sy):
            sy.dma_start(out=idb[:], in_=identb).then_inc(LD, 16)
            sy.dma_start(out=pf[:, 6560:8608], in_=cosf).then_inc(LD, 16)
            sy.dma_start(out=pf[:, 8608:10656], in_=rsf).then_inc(LD, 16)
            sy.dma_start(out=pf[0:16, 6547:6548], in_=bb_c).then_inc(LD, 16)
            sy.dma_start(out=pfoh[0:16, :], in_=oneh16).then_inc(LD, 16)
            ct.add("LD", 80)
        LD_c = ct.v("LD")

        @block.gpsimd
        def _(g):
            g.dma_start(out=pb[:, R_XN:R_XN + 16384], in_=xnfm1).then_inc(GS, 16)
            g.dma_start(out=pb[:, R_SW:R_SW + 128], in_=swapb).then_inc(GS, 16)
            g.dma_start(out=pb[0:16, R_SEL:R_SEL + 1024], in_=selb).then_inc(GS, 16)
            for fc in range(8):
                g.dma_start(out=pb[:, R_RT + fc * 16:R_RT + (fc + 1) * 16],
                            in_=wbT[fc * 128:(fc + 1) * 128, :]).then_inc(GS, 16)
            ct.add("GS", 11 * 16)
        GS_b = ct.v("GS")

        @block.tensor
        def _(t):
            t.wait_ge(GS, GS_b)
            t.wait_ge(LD, LD_c)
            ssb = ct.v("SS")
            for nb in range(4):
                for fc in range(8):
                    inst = t.matmul(
                        psS[0:16, :],
                        pb[:, R_RT + fc * 16:R_RT + (fc + 1) * 16],
                        pb[:, R_XN + fc * 2048 + nb * 512:
                           R_XN + fc * 2048 + nb * 512 + 512],
                        start=(fc == 0), stop=(fc == 7))
                inst.then_inc(TS, 1)
                ct.add("TS")
                ct.c["TS_bt_%d" % nb] = ct.v("TS")
                if nb < 3:
                    t.wait_ge(SS, ssb + nb + 1)
        ct.c["SS_bt0"] = ct.v("SS")

        @block.scalar
        def _(s):
            Sc = mk_auto(ct, "SS", SS, s)
            for nb in range(4):
                s.wait_ge(TS, ct.v("TS_bt_%d" % nb))
                Sc(lambda nb=nb: s.activation(
                    pf[0:16, F_BT + nb * 512:F_BT + (nb + 1) * 512],
                    psS[0:16, :], AF.Sigmoid, bias=pf[0:16, 6547:6548],
                    scale=1.0))
        SS_bt = ct.v("SS")

        @block.vector
        def _(v):
            V = mk_auto(ct, "VS", VS, v)
            v.wait_ge(SS, SS_bt)
            V(lambda: v.tensor_scalar(pf[0:16, F_BT:F_BT + 2048],
                                      pf[0:16, F_BT:F_BT + 2048], 0.999, None,
                                      ALU.min))
            V(lambda: v.tensor_scalar(pf[0:16, F_BT:F_BT + 2048],
                                      pf[0:16, F_BT:F_BT + 2048], 0.8, None,
                                      ALU.max))
            for c in range(16):
                V(lambda c=c: v.tensor_tensor_scan(
                    pf[0:16, F_AA + c * 128:F_AA + (c + 1) * 128],
                    pf[0:16, F_BT + c * 128:F_BT + (c + 1) * 128],
                    pf[0:16, F_BT + c * 128:F_BT + (c + 1) * 128],
                    1.0, ALU.mult, ALU.bypass))
            V(lambda: v.tensor_copy(pf[0:16, 6144:6160],
                                    pf[0:16, F_AA:F_AA + 2048]
                                    .rearrange("p (c u) -> p c u", u=128)
                                    [:, :, 127:128].rearrange("p c u -> p (c u)")))
            V(lambda: v.reciprocal(pf[0:16, F_RI:F_RI + 2048],
                                   pf[0:16, F_AA:F_AA + 2048]))
            V(lambda: v.tensor_copy(pb[0:16, R_AB:R_AB + 2048],
                                    pf[0:16, F_AA:F_AA + 2048]))
            V(lambda: v.tensor_copy(pb[0:16, R_AB + 2048:R_AB + 4096],
                                    pf[0:16, F_RI:F_RI + 2048]))
        VS_a = ct.v("VS")

        @block.sync
        def _(sy):
            sy.wait_ge(VS, VS_a)
            sy.dma_start(out=aalld, in_=pf[0:16, F_AA:F_AA + 2048]).then_inc(OS, 16)
            ct.add("OS", 16)

        if C2_PHASES <= 1:
            @block.sync
            def _(sy):
                sy.wait_ge(OS, ct.v("OS"))
                sy.wait_ge(GS, ct.v("GS"))
            return nc

        VS_acbbase = ct.v("VS")

        @block.tensor
        def _(t):
            t.wait_ge(VS, VS_a)
            for h in range(16):
                if h >= 1:
                    t.wait_ge(VS, VS_acbbase + h)
                t.matmul(psS[0:64, 0:16],
                         pfoh[0:16, h * 64:(h + 1) * 64],
                         pf[0:16, 6144:6160], start=True,
                         stop=True).then_inc(TS, 1)
                ct.add("TS")
                ct.c["TS_acb_%d" % h] = ct.v("TS")

        @block.vector
        def _(v):
            V = mk_auto(ct, "VS", VS, v)
            for h in range(16):
                v.wait_ge(TS, ct.v("TS_acb_%d" % h))
                V(lambda h=h: v.tensor_copy(pf[0:64, 6160 + h * 16:
                                               6160 + (h + 1) * 16],
                                            psS[0:64, 0:16]))
        VS_acb = ct.v("VS")

        if C2_PHASES <= 2:
            @block.sync
            def _(sy):
                sy.wait_ge(OS, ct.v("OS"))
                sy.wait_ge(GS, ct.v("GS"))
            return nc

        @block.vector
        def _(v):
            V = mk_auto(ct, "VS", VS, v)
            V(lambda: v.memset(
                pb[:, R_V:R_V + 16640].rearrange("p (h c e) -> p h c e",
                                                 c=16, e=65)
                [:, :, :, 64:65], 1.0))
        VS_vones = ct.v("VS")

        @block.gpsimd
        def _(g):
            for fc in range(8):
                g.dma_start(out=pb[:, R_W + 8192 + fc * 1024:
                               R_W + 8192 + (fc + 1) * 1024],
                            in_=wvT[fc * 128:(fc + 1) * 128, :]).then_inc(GS, 16)
            ct.add("GS", 8 * 16)
        GS_wv = ct.v("GS")

        for c in range(16):
            @block.tensor
            def _(t, c=c):
                t.wait_ge(GS, GS_wv)
                if c == 0:
                    t.wait_ge(VS, VS_vones)
                else:
                    t.wait_ge(VS, ct.v("VS_v_%d" % (c - 1)))
                for cb in range(2):
                    ps = psA if cb == 0 else psB
                    for fc in range(8):
                        inst = t.matmul(
                            ps[:],
                            pb[:, R_XN + fc * 2048 + c * 128:
                               R_XN + fc * 2048 + c * 128 + 128],
                            pb[:, R_W + 8192 + fc * 1024 + cb * 512:
                               R_W + 8192 + fc * 1024 + (cb + 1) * 512],
                            start=(fc == 0), stop=(fc == 7))
                    inst.then_inc(TS, 1)
                    ct.add("TS")
                    ct.c["TS_v_%d_%d" % (c, cb)] = ct.v("TS")

            @block.vector
            def _(v, c=c):
                V = mk_auto(ct, "VS", VS, v)
                for cb in range(2):
                    ps = psA if cb == 0 else psB
                    v.wait_ge(TS, ct.v("TS_v_%d_%d" % (c, cb)))
                    V(lambda ps=ps, cb=cb, c=c: v.tensor_copy(
                        pb[:, R_V:R_V + 16640]
                        .rearrange("p (h ce) -> p h ce", ce=1040)
                        [:, cb * 8:(cb + 1) * 8, c * 65:c * 65 + 64],
                        ps[:].rearrange("p (h e) -> p h e", e=64)))
                ct.c["VS_v_%d" % c] = ct.v("VS")
        VS_v = ct.v("VS")

        @block.sync
        def _(sy):
            sy.wait_ge(VS, VS_v)
            sy.dma_start(out=vsb, in_=pb[:, R_V:R_V + 16640]).then_inc(OS, 16)
            ct.add("OS", 16)

        if C2_PHASES <= 3:
            @block.sync
            def _(sy):
                sy.wait_ge(OS, ct.v("OS"))
                sy.wait_ge(GS, ct.v("GS"))
            return nc

        @block.gpsimd
        def _(g):
            for fc in range(8):
                g.dma_start(out=pb[:, R_W + fc * 1024:R_W + (fc + 1) * 1024],
                            in_=wqT[fc * 128:(fc + 1) * 128, :]).then_inc(GS, 16)
            ct.add("GS", 8 * 16)
            ct.c["GS_wq"] = ct.v("GS")
            g.wait_ge(TS, ct.v("TS_v_15_1"))
            for fc in range(8):
                g.dma_start(out=pb[:, R_W + 8192 + fc * 1024:
                               R_W + 8192 + (fc + 1) * 1024],
                            in_=wkT[fc * 128:(fc + 1) * 128, :]).then_inc(GS, 16)
            ct.add("GS", 8 * 16)
            ct.c["GS_wk"] = ct.v("GS")

        OS_qk0 = ct.v("OS")
        ct.c["SS_pj0"] = ct.v("SS")
        for qk in range(2):
            wbase = R_W if qk == 0 else R_W + 8192
            scbase = R_AB if qk == 0 else R_AB + 2048
            dout = qh if qk == 0 else kh
            for qc in range(8):
                @block.tensor
                def _(t, qk=qk, qc=qc, wbase=wbase):
                    t.wait_ge(GS, ct.v("GS_wq" if qk == 0 else "GS_wk"))
                    t.wait_ge(VS, VS_acb)
                    if qk == 0 and qc == 0:
                        t.wait_ge(VS, VS_v)
                    for nb in range(4):
                        j = (qk * 8 + qc) * 4 + nb
                        ps = psA if j % 2 == 0 else psB
                        if j >= 2:
                            jj = j - 2
                            if jj < 32:
                                tgt = (jj // 4) * 8 + (jj % 4) + 1
                            else:
                                tgt = 64 + ((jj - 32) // 4) * 40 + (jj % 4) + 1
                            t.wait_ge(SS, ct.v("SS_pj0") + tgt)
                        for fc in range(8):
                            inst = t.matmul(
                                ps[:],
                                pb[:, wbase + fc * 1024 + qc * 128:
                                   wbase + fc * 1024 + qc * 128 + 128],
                                pb[:, R_XN + fc * 2048 + nb * 512:
                                   R_XN + fc * 2048 + nb * 512 + 512],
                                start=(fc == 0), stop=(fc == 7))
                        inst.then_inc(TS, 1)
                        ct.add("TS")
                        ct.c["TS_pj_%d" % j] = ct.v("TS")

                @block.scalar
                def _(s, qk=qk, qc=qc):
                    Sc = mk_auto(ct, "SS", SS, s)
                    for nb in range(4):
                        j = (qk * 8 + qc) * 4 + nb
                        ps = psA if j % 2 == 0 else psB
                        s.wait_ge(TS, ct.v("TS_pj_%d" % j))
                        if j >= 4:
                            s.wait_ge(VS, ct.v("VS_rp_%d" % (j - 4)))
                        Sc(lambda ps=ps, nb=nb: s.activation(
                            pb[:, R_PB + nb * 512:R_PB + (nb + 1) * 512],
                            ps[:], AF.Copy, bias=0.0, scale=1.0))
                        ct.c["SS_pj_%d" % j] = ct.v("SS")

                for nb in range(4):
                    j = (qk * 8 + qc) * 4 + nb

                    @block.tensor
                    def _(t, j=j, nb=nb, qc=qc, scbase=scbase):
                        t.wait_ge(SS, ct.v("SS_pj_%d" % j))
                        if j >= 1:
                            t.wait_ge(VS, ct.v("VS_rot_%d" % (j - 1)))
                        t.matmul(psC[:], pb[:, R_SW:R_SW + 128],
                                 pb[:, R_PB + nb * 512:R_PB + (nb + 1) * 512],
                                 start=True, stop=True).then_inc(TS, 1)
                        ct.add("TS")
                        ct.c["TS_rot_%d" % j] = ct.v("TS")
                        if j >= 1:
                            t.wait_ge(VS, ct.v("VS_ab_%d" % (j - 1)))
                        t.matmul(psD[:],
                                 pb[0:16, R_SEL + qc * 128:
                                    R_SEL + (qc + 1) * 128],
                                 pb[0:16, scbase + nb * 512:
                                    scbase + (nb + 1) * 512],
                                 start=True, stop=True).then_inc(TS, 1)
                        ct.add("TS")
                        ct.c["TS_abc_%d" % j] = ct.v("TS")

                    @block.vector
                    def _(v, j=j, nb=nb, qk=qk, qc=qc):
                        V = mk_auto(ct, "VS", VS, v)
                        RT0, T1, RP = R_RT, R_RT + 512, R_RT + 1024
                        MN, MX, EE, AB = (R_SC2, R_SC2 + 512, R_SC2 + 1024,
                                          R_SC2 + 1536)
                        v.wait_ge(TS, ct.v("TS_rot_%d" % j))
                        V(lambda: v.tensor_copy(pb[:, RT0:RT0 + 512], psC[:]))
                        ct.c["VS_rot_%d" % j] = ct.v("VS")
                        V(lambda: v.tensor_tensor(
                            pb[:, T1:T1 + 512],
                            pb[:, R_PB + nb * 512:R_PB + (nb + 1) * 512],
                            pf[:, 6560 + nb * 512:6560 + (nb + 1) * 512],
                            ALU.mult))
                        V(lambda: v.tensor_tensor(
                            pb[:, RP:RP + 512], pb[:, RT0:RT0 + 512],
                            pf[:, 8608 + nb * 512:8608 + (nb + 1) * 512],
                            ALU.mult))
                        V(lambda: v.tensor_add(pb[:, RP:RP + 512],
                                               pb[:, RP:RP + 512],
                                               pb[:, T1:T1 + 512]))
                        V(lambda: v.tensor_scalar(pb[:, MN:MN + 512],
                                                  pb[:, RP:RP + 512], 0.0,
                                                  None, ALU.min))
                        V(lambda: v.tensor_scalar(pb[:, MX:MX + 512],
                                                  pb[:, RP:RP + 512], 0.0,
                                                  None, ALU.max))
                        ct.c["VS_mn_%d" % j] = ct.v("VS")

                    @block.scalar
                    def _(s, j=j):
                        Sc = mk_auto(ct, "SS", SS, s)
                        MN, EE = R_SC2, R_SC2 + 1024
                        s.wait_ge(VS, ct.v("VS_mn_%d" % j))
                        Sc(lambda: s.activation(pb[:, EE:EE + 512],
                                                pb[:, MN:MN + 512], AF.Exp,
                                                bias=0.0, scale=1.0))
                    ct.c["SS_ee_%d" % j] = ct.v("SS")

                    @block.vector
                    def _(v, j=j, nb=nb, qk=qk, qc=qc):
                        V = mk_auto(ct, "VS", VS, v)
                        MX, EE, AB = R_SC2 + 512, R_SC2 + 1024, R_SC2 + 1536
                        v.wait_ge(SS, ct.v("SS_ee_%d" % j))
                        V(lambda: v.tensor_add(pb[:, EE:EE + 512],
                                               pb[:, EE:EE + 512],
                                               pb[:, MX:MX + 512]))
                        v.wait_ge(TS, ct.v("TS_abc_%d" % j))
                        V(lambda: v.tensor_copy(pb[:, AB:AB + 512], psD[:]))
                        ct.c["VS_ab_%d" % j] = ct.v("VS")
                        if qk == 1 and nb == 0 and qc > 0:
                            v.wait_ge(TS, ct.v("TS_kt_%d" % (qc - 1)))
                        if nb == 0:
                            v.wait_ge(OS, OS_qk0 + 16 * (qk * 8 + qc))
                        V(lambda: v.tensor_tensor(
                            pb[:, R_HB + nb * 512:R_HB + (nb + 1) * 512],
                            pb[:, EE:EE + 512], pb[:, AB:AB + 512], ALU.mult))
                        ct.c["VS_rp_%d" % j] = ct.v("VS")

                @block.sync
                def _(sy, qk=qk, qc=qc, dout=dout):
                    sy.wait_ge(VS, ct.v("VS_rp_%d" % ((qk * 8 + qc) * 4 + 3)))
                    sy.dma_start(out=dout[:, qc * 2048:(qc + 1) * 2048],
                                 in_=pb[:, R_HB:R_HB + 2048]).then_inc(OS, 16)
                    ct.add("OS", 16)

                if qk == 1:
                    if qc == 0:
                        ct.c["SS_qk1base"] = ct.v("SS") - 8

                    @block.tensor
                    def _(t, qc=qc):
                        t.wait_ge(VS, ct.v("VS_rp_%d" % ((8 + qc) * 4 + 3)))
                        for c in range(16):
                            pt = psT0 if c % 2 == 0 else psT1
                            e = qc * 16 + c
                            if e >= 2:
                                e2 = e - 2
                                t.wait_ge(SS, ct.v("SS_qk1base") +
                                          (e2 // 16) * 40 + 8 +
                                          2 * ((e2 % 16) + 1))
                            t.transpose(pt[:],
                                        pb[:, R_HB + c * 128:R_HB + (c + 1) * 128],
                                        idb[:]).then_inc(TS, 1)
                            ct.add("TS")
                            ct.c["TS_ktt_%d" % (qc * 16 + c)] = ct.v("TS")
                        ct.c["TS_kt_%d" % qc] = ct.v("TS")

                    @block.scalar
                    def _(s, qc=qc):
                        Sc = mk_auto(ct, "SS", SS, s)
                        for c in range(16):
                            pt = psT0 if c % 2 == 0 else psT1
                            s.wait_ge(TS, ct.v("TS_ktt_%d" % (qc * 16 + c)))
                            Sc(lambda pt=pt, c=c, qc=qc: s.activation(
                                pb[:, R_KT + (2 * qc) * 1024 + c * 64:
                                   R_KT + (2 * qc) * 1024 + c * 64 + 64],
                                pt[:, 0:64], AF.Copy, bias=0.0, scale=1.0))
                            Sc(lambda pt=pt, c=c, qc=qc: s.activation(
                                pb[:, R_KT + (2 * qc + 1) * 1024 + c * 64:
                                   R_KT + (2 * qc + 1) * 1024 + c * 64 + 64],
                                pt[:, 64:128], AF.Copy, bias=0.0, scale=1.0))
        SS_kt = ct.v("SS")

        @block.sync
        def _(sy):
            sy.wait_ge(SS, SS_kt)
            sy.dma_start(out=khtm, in_=pb[:, R_KT:R_KT + 16384]).then_inc(OS, 16)
            ct.add("OS", 16)

        if C2_PHASES <= 4:
            @block.sync
            def _(sy):
                sy.wait_ge(OS, ct.v("OS"))
                sy.wait_ge(GS, ct.v("GS"))
            return nc

        @block.vector
        def _(v):
            V = mk_auto(ct, "VS", VS, v)
            V(lambda: v.memset(pf[0:16, 6546:6547], 1.0))
        VS_pr = ct.v("VS")

        VS_subase = ct.v("VS")

        def vs_su(m):
            return VS_subase + (m // 16) * 33 + 1 + ((m % 16) + 1) * 2

        @block.tensor
        def _(t):
            t.wait_ge(SS, SS_kt)
            t.wait_ge(VS, VS_v)
            for h in range(16):
                qc, dd = h // 2, (h % 2) * 64
                for c in range(16):
                    if h * 16 + c >= 1:
                        t.wait_ge(VS, vs_su(h * 16 + c - 1))
                    t.matmul(psS[0:64, 0:65],
                             pb[:, R_KT + h * 1024 + c * 64:
                                R_KT + h * 1024 + c * 64 + 64],
                             pb[:, R_V + h * 1040 + c * 65:
                                R_V + h * 1040 + (c + 1) * 65],
                             start=True, stop=True).then_inc(TS, 1)
                    ct.add("TS")
                    ct.c["TS_su_%d" % (h * 16 + c)] = ct.v("TS")

        OS_su0 = ct.v("OS")

        @block.vector
        def _(v):
            V = mk_auto(ct, "VS", VS, v)
            for h in range(16):
                if h >= 1:
                    v.wait_ge(OS, OS_su0 + 16 * h)
                V(lambda: v.memset(pf[0:64, 6416:6481], 0.0))
                for c in range(16):
                    v.wait_ge(TS, ct.v("TS_su_%d" % (h * 16 + c)))
                    V(lambda h=h, c=c: v.tensor_scalar(
                        pf[0:64, 6481:6546], pf[0:64, 6416:6481],
                        pf[0:64, 6160 + h * 16 + c:6160 + h * 16 + c + 1],
                        None, ALU.mult))
                    V(lambda h=h, c=c: v.scalar_tensor_tensor(
                        pf[0:64, 6416:6481], psS[0:64, 0:65],
                        pf[0:64, 6160 + h * 16 + c:6160 + h * 16 + c + 1],
                        pf[0:64, 6481:6546], ALU.mult, ALU.add))
                    ct.c["VS_su_%d" % (h * 16 + c)] = ct.v("VS")
                ct.c["VS_sum_%d" % h] = ct.v("VS")
            V(lambda: v.tensor_copy(pf[0:16, 6546:6547],
                                    pf[0:16, 6144:6145]))
            for c in range(1, 16):
                V(lambda c=c: v.tensor_tensor(pf[0:16, 6546:6547],
                                              pf[0:16, 6546:6547],
                                              pf[0:16, 6144 + c:6145 + c],
                                              ALU.mult))
        VS_sum = ct.v("VS")

        @block.sync
        def _(sy):
            for h in range(16):
                sy.wait_ge(VS, ct.v("VS_sum_%d" % h))
                sy.dma_start(out=summ[h:h + 1, :]
                             .rearrange("o (d e) -> (o d) e", e=65),
                             in_=pf[0:64, 6416:6481]).then_inc(OS, 16)
                ct.add("OS", 16)
            sy.wait_ge(VS, VS_sum)
            sy.dma_start(out=sump, in_=pf[0:16, 6546:6547],
                         allow_slow_non_contiguous=True).then_inc(OS, 16)
            ct.add("OS", 16)
            sy.wait_ge(OS, ct.v("OS"))




def phase_d(nc, io):
    accDi, xntm1 = io["accD"], io["xntm1"]
    qh, kh, khtm, vsb, aalld = (io["qh"], io["kh"], io["khtm"], io["vsb"],
                                io["aalld"])
    sumcatG, selw = io["sumcatG"], io["selw"]
    wo1T, bo1_r, sc1_r = io["wo1T"], io["bo1_r"], io["sc1_r"]
    causal, shiftb, oneh16 = io["causal"], io["shiftb"], io["oneh16"]
    outf = io["outf"]
    if DEBUG_B:
        ddat = dhp0 = drst = None

    R_AT = 0
    R_WO = 16384
    R_CA = 24576
    R_SW = 24704
    R_SB = 24832
    R_TMP = 24960
    R_XT = 25216
    R_JK = 27264
    PBN = 28288
    F_SL = 0
    F_SI = 3120
    F_AA = 4160
    F_AE = 6208
    F_W = 6528
    F_B = 6864
    F_HP = 8912
    F_AC = 9936
    F_ON = 11984
    PFN = 12112

    ct = Ct()
    import contextlib
    with contextlib.ExitStack() as _st:
        E = _st.enter_context
        pb = E(nc.sbuf_tensor("Dpb", [128, PBN], BF16))
        pf = E(nc.sbuf_tensor("Dpf", [128, PFN], F32))
        cst = E(nc.sbuf_tensor("Dcst", [128, 8], F32))
        nc.const_aps.register(F32, 1e-6, cst[:, 0:1])
        pfoh = E(nc.sbuf_tensor("Dpfoh", [16, 1024], F32))
        qhb = E(nc.sbuf_tensor("Dqhb", [64, 2 * 2048], BF16))
        khb = E(nc.sbuf_tensor("Dkhb", [64, 2 * 2048], BF16))
        ktb = E(nc.sbuf_tensor("Dktb", [128, 2 * 1024], BF16))
        vhb = E(nc.sbuf_tensor("Dvhb", [128, 2 * 1040], BF16))
        rb64 = E(nc.sbuf_tensor("Drb64", [64, 128], F32))
        sgth = E(nc.sbuf_tensor("Dsgth", [64, 8320], F32))
        sgtp = E(nc.sbuf_tensor("Dsgtp", [16, 8], F32))
        selwb = E(nc.sbuf_tensor("Dselwb", [64, 24], F32))
        psA = E(nc.psum_tensor("DpsA", [128, 512], F32))
        psC = E(nc.psum_tensor("DpsC", [128, 512], F32))
        psD = E(nc.psum_tensor("DpsD", [128, 512], F32))
        psS = E(nc.psum_tensor("DpsS", [64, 512], F32))
        psE = E(nc.psum_tensor("DpsE", [128, 512], F32))
        psF = E(nc.psum_tensor("DpsF", [128, 512], F32))
        LD = E(nc.semaphore("DLD"))
        GS = E(nc.semaphore("DGS"))
        VS = E(nc.semaphore("DVS"))
        SS = E(nc.semaphore("DSS"))
        TS = E(nc.semaphore("DTS"))
        HLD = E(nc.semaphore("DHLD"))
        OS = E(nc.semaphore("DOS"))
        block = E(nc.Block())

        @block.sync
        def _(sy):
            sy.dma_start(out=sgth[:].rearrange("d (m e) -> d m e", e=65),
                         in_=sumcatG[:, 0:4160]
                         .rearrange("m (d e) -> d m e", e=65)).then_inc(LD, 16)
            sy.dma_start(out=sgtp[:],
                         in_=sumcatG[:, 4160:4161]
                         .rearrange("(c h) o -> h (c o)", h=16),
                         allow_slow_non_contiguous=True).then_inc(LD, 16)
            sy.dma_start(out=selwb[:], in_=selw).then_inc(LD, 16)
            sy.dma_start(out=pf[0:16, F_AA:F_AA + 2048][0:16, :],
                         in_=aalld).then_inc(LD, 16)
            sy.dma_start(out=pfoh[:], in_=oneh16).then_inc(LD, 16)
            sy.dma_start(out=pb[:, R_CA:R_CA + 128], in_=causal).then_inc(LD, 16)
            sy.dma_start(out=pb[0:64, R_SW:R_SW + 128], in_=shiftb).then_inc(LD, 16)
            sy.dma_start(out=pf[0:1, F_B:F_B + 1024], in_=bo1_r).then_inc(LD, 16)
            sy.dma_start(out=pf[0:1, F_B + 1024:F_B + 2048],
                         in_=sc1_r).then_inc(LD, 16)
            ct.add("LD", 9 * 16)
        LD_c = ct.v("LD")

        @block.gpsimd
        def _(g):
            for fc in range(8):
                g.dma_start(out=pb[:, R_WO + fc * 1024:R_WO + (fc + 1) * 1024],
                            in_=wo1T[fc * 128:(fc + 1) * 128, :]).then_inc(GS, 16)
            ct.add("GS", 8 * 16)
        GS_wo = ct.v("GS")

        @block.vector
        def _(v):
            V = mk_auto(ct, "VS", VS, v)
            v.wait_ge(LD, LD_c)
            V(lambda: v.memset(cst[:, 0:1], 1e-6))
            V(lambda: v.memset(pf[0:1, F_ON:F_ON + 128], 1.0))
            V(lambda: v.memset(pf[64:65, F_ON + 64:F_ON + 128], 1.0))
            V(lambda: v.tensor_copy(pf[0:16, 6208:6224],
                                    pf[0:16, F_AA:F_AA + 2048]
                                    .rearrange("p (c u) -> p c u", u=128)
                                    [:, :, 127:128]
                                    .rearrange("p c u -> p (c u)")))
            for k in (1, 2, 3):
                base = F_SL + (k - 1) * 1040
                pcol = 6208 + 320 + (k - 1)
                for c in range(8):
                    idx = (k - 1) * 8 + c
                    if c == 0:
                        V(lambda base=base, c=c, idx=idx: v.tensor_scalar(
                            pf[0:64, base:base + 1040],
                            sgth[0:64, c * 1040:(c + 1) * 1040],
                            selwb[0:64, idx:idx + 1], None, ALU.mult))
                        V(lambda pcol=pcol, c=c, idx=idx: v.tensor_scalar(
                            pf[0:16, pcol:pcol + 1], sgtp[0:16, c:c + 1],
                            selwb[0:16, idx:idx + 1], None, ALU.mult))
                    else:
                        V(lambda base=base, c=c, idx=idx:
                          v.scalar_tensor_tensor(
                              pf[0:64, base:base + 1040],
                              sgth[0:64, c * 1040:(c + 1) * 1040],
                              selwb[0:64, idx:idx + 1],
                              pf[0:64, base:base + 1040],
                              ALU.mult, ALU.add))
                        V(lambda pcol=pcol, c=c, idx=idx:
                          v.scalar_tensor_tensor(
                              pf[0:16, pcol:pcol + 1], sgtp[0:16, c:c + 1],
                              selwb[0:16, idx:idx + 1],
                              pf[0:16, pcol:pcol + 1],
                              ALU.mult, ALU.add))
        VS_i = ct.v("VS")

        VS_acbbase = ct.v("VS")

        @block.tensor
        def _(t):
            t.wait_ge(VS, VS_i)
            for h in range(16):
                if h >= 1:
                    t.wait_ge(VS, VS_acbbase + 2 * h)
                t.matmul(psS[0:64, 0:16], pfoh[0:16, h * 64:(h + 1) * 64],
                         pf[0:16, 6208:6224], start=True,
                         stop=True).then_inc(TS, 1)
                ct.add("TS")
                t.matmul(psS[0:64, 16:19], pfoh[0:16, h * 64:(h + 1) * 64],
                         pf[0:16, 6528:6531] if False
                         else pf[0:16, 6208 + 320:6208 + 323],
                         start=True, stop=True).then_inc(TS, 1)
                ct.add("TS")
                ct.c["TS_acb_%d" % h] = ct.v("TS")

        @block.vector
        def _(v):
            V = mk_auto(ct, "VS", VS, v)
            for h in range(16):
                v.wait_ge(TS, ct.v("TS_acb_%d" % h))
                V(lambda h=h: v.tensor_copy(
                    pf[0:64, 6224 + h * 16:6224 + (h + 1) * 16],
                    psS[0:64, 0:16]))
                V(lambda h=h: v.tensor_copy(
                    pf[0:64, 6480 + h * 3:6480 + (h + 1) * 3],
                    psS[0:64, 16:19]))
                ct.c["VS_acb_%d" % h] = ct.v("VS")
            for h in range(16):
                si = pf[0:64, F_SI + h * 65:F_SI + (h + 1) * 65]
                L = [pf[0:64, F_SL + (k * 16 + h) * 65:
                        F_SL + (k * 16 + h) * 65 + 65] for k in range(3)]
                V(lambda si=si, L=L: v.tensor_copy(si, L[2]))
                for k in (2, 1):
                    V(lambda si=si, L=L, k=k, h=h: v.scalar_tensor_tensor(
                        si, si, pf[0:64, 6480 + h * 3 + k - 1:
                                   6480 + h * 3 + k], L[k - 1],
                        ALU.mult, ALU.add))
        VS_sin = ct.v("VS")

        if D_PHASES <= 1:
            @block.sync
            def _(sy):
                sy.wait_ge(OS, ct.v("OS"))
                sy.wait_ge(GS, ct.v("GS"))
            return nc

        def VP(m, k):
            return (m // 16) * 145 + 1 + (m % 16) * 9 + k

        ATS = E(nc.semaphore("DATS"))
        OTS = E(nc.semaphore("DOTS"))
        BTS = E(nc.semaphore("DBTS"))
        SHS = E(nc.semaphore("DSHS"))
        ADS = E(nc.semaphore("DADS"))
        DVS = E(nc.semaphore("DDVS"))

        @block.sync
        def _(sy):
            for h in range(16):
                qc, dd = h // 2, (h % 2) * 64
                if h >= 2:
                    sy.wait_ge(ADS, (h - 1) * 16)
                sy.dma_start(out=qhb[:, (h % 2) * 2048:(h % 2) * 2048 + 2048],
                             in_=qh[dd:dd + 64, qc * 2048:(qc + 1) * 2048]
                             ).then_inc(HLD, 16)
                sy.dma_start(out=khb[:, (h % 2) * 2048:(h % 2) * 2048 + 2048],
                             in_=kh[dd:dd + 64, qc * 2048:(qc + 1) * 2048]
                             ).then_inc(HLD, 16)
                sy.dma_start(out=ktb[:, (h % 2) * 1024:(h % 2) * 1024 + 1024],
                             in_=khtm[:, h * 1024:(h + 1) * 1024]
                             ).then_inc(HLD, 16)
                sy.dma_start(out=vhb[:, (h % 2) * 1040:(h % 2) * 1040 + 1040],
                             in_=vsb[:, h * 1040:(h + 1) * 1040]
                             ).then_inc(HLD, 16)
                ct.add("HLD", 64)

        @block.tensor
        def _(t):
            t.wait_ge(VS, VS_sin)
            t.wait_ge(LD, LD_c)
            shs = 0
            for h in range(16):
                hb = h % 2
                t.wait_ge(HLD, 64 * (h + 1))
                for c in range(16):
                    m = h * 16 + c
                    if m >= 1:
                        t.wait_ge(DVS, VP(m - 1, 1))
                    t.matmul(psA[:, 0:128],
                             khb[:, hb * 2048 + c * 128:
                                 hb * 2048 + c * 128 + 128],
                             qhb[:, hb * 2048 + c * 128:
                                 hb * 2048 + c * 128 + 128],
                             start=True, stop=True).then_inc(ATS, 1)
                    if m >= 1:
                        t.wait_ge(DVS, VP(m - 1, 5))
                        t.wait_ge(DVS, (h * 145 + 1) if c == 0
                                  else VP(m - 1, 9))
                    else:
                        t.wait_ge(DVS, 1)
                    t.matmul(psC[0:65, 0:128], pb[0:64, R_SB:R_SB + 65],
                             qhb[:, hb * 2048 + c * 128:
                                 hb * 2048 + c * 128 + 128],
                             start=True, stop=False)
                    t.wait_ge(DVS, VP(m, 1))
                    t.matmul(psC[0:65, 0:128],
                             vhb[:, hb * 1040 + c * 65:hb * 1040 + (c + 1) * 65],
                             pb[:, R_TMP + 128:R_TMP + 256],
                             start=False, stop=True).then_inc(OTS, 1)
                    t.wait_ge(DVS, VP(m, 3))
                    if m >= 1:
                        t.wait_ge(DVS, VP(m - 1, 6))
                    t.matmul(psD[0:64, 0:128], pf[64:65, F_ON + 64:F_ON + 128],
                             pf[64:65, F_W + 128:F_W + 256],
                             start=True, stop=True).then_inc(BTS, 1)
                    if h % 2 == 1:
                        t.wait_ge(DVS, VP(m, 5))
                        t.matmul(psD[:, 0:128], pb[0:64, R_SW:R_SW + 128],
                                 pb[0:64, R_TMP:R_TMP + 128],
                                 start=True, stop=True).then_inc(SHS, 1)
                        shs += 1
                    if m >= 1:
                        t.wait_ge(DVS, VP(m - 1, 8))
                    t.matmul(psS[0:64, 0:65],
                             ktb[:, hb * 1024 + c * 64:hb * 1024 + c * 64 + 64],
                             vhb[:, hb * 1040 + c * 65:hb * 1040 + (c + 1) * 65],
                             start=True, stop=True).then_inc(ADS, 1)

        @block.vector
        def _(v):
            shs = 0
            dvc = [0]

            def DV(f):
                inst = f()
                inst.then_inc(DVS, 1)
                dvc[0] += 1
                v.wait_ge(DVS, dvc[0])
            for h in range(16):
                hb = h % 2
                qc = h // 2
                v.wait_ge(VS, VS_sin)
                DV(lambda h=h: v.tensor_copy(
                    pb[0:64, R_SB:R_SB + 65],
                    pf[0:64, F_SI + h * 65:F_SI + (h + 1) * 65]))
                for c in range(16):
                    m = h * 16 + c
                    v.wait_ge(ATS, m + 1)
                    DV(lambda: v.scalar_tensor_tensor(
                        pb[:, R_TMP + 128:R_TMP + 256], psA[:, 0:128], 1.0,
                        pb[:, R_CA:R_CA + 128], ALU.mult, ALU.mult))
                    v.wait_ge(OTS, m + 1)
                    DV(lambda: v.tensor_scalar(pf[64:65, F_W:F_W + 128],
                                               psC[64:65, 0:128], 1e-6, None,
                                               ALU.add))
                    DV(lambda: v.reciprocal(pf[64:65, F_W + 128:F_W + 256],
                                            pf[64:65, F_W:F_W + 128]))
                    v.wait_ge(BTS, m + 1)
                    DV(lambda: v.tensor_copy(rb64[:], psD[0:64, 0:128]))
                    if h % 2 == 0:
                        DV(lambda qc=qc, c=c: v.scalar_tensor_tensor(
                            pb[0:64, R_AT + qc * 2048 + c * 128:
                               R_AT + qc * 2048 + c * 128 + 128],
                            psC[0:64, 0:128], 1.0, rb64[:],
                            ALU.mult, ALU.mult))
                        DV(lambda: v.engine_nop())
                    else:
                        DV(lambda: v.scalar_tensor_tensor(
                            pb[0:64, R_TMP:R_TMP + 128],
                            psC[0:64, 0:128], 1.0, rb64[:],
                            ALU.mult, ALU.mult))
                        shs += 1
                        v.wait_ge(SHS, shs)
                        DV(lambda qc=qc, c=c: v.tensor_copy(
                            pb[64:128, R_AT + qc * 2048 + c * 128:
                               R_AT + qc * 2048 + c * 128 + 128],
                            psD[64:128, 0:128]))
                    DV(lambda h=h, c=c: v.tensor_scalar(
                        pf[0:64, F_W + 256:F_W + 321],
                        pf[0:64, F_SI + h * 65:F_SI + (h + 1) * 65],
                        pf[0:64, 6224 + h * 16 + c:6224 + h * 16 + c + 1],
                        None, ALU.mult))
                    v.wait_ge(ADS, m + 1)
                    DV(lambda h=h, c=c: v.scalar_tensor_tensor(
                        pf[0:64, F_SI + h * 65:F_SI + (h + 1) * 65],
                        psS[0:64, 0:65],
                        pf[0:64, 6224 + h * 16 + c:6224 + h * 16 + c + 1],
                        pf[0:64, F_W + 256:F_W + 321],
                        ALU.mult, ALU.add))
                    DV(lambda h=h: v.tensor_copy(
                        pb[0:64, R_SB:R_SB + 65],
                        pf[0:64, F_SI + h * 65:F_SI + (h + 1) * 65]))

        if D_PHASES <= 2:
            @block.sync
            def _(sy):
                sy.wait_ge(OS, ct.v("OS"))
                sy.wait_ge(GS, ct.v("GS"))
            return nc

        ct.c["VS_bcbase"] = ct.v("VS")

        @block.tensor
        def _(t):
            t.wait_ge(DVS, VP(255, 9))
            t.wait_ge(VS, VS_i)
            t.wait_ge(GS, GS_wo)
            if D_PHASES <= 2.3:
                return
            for cb in range(2):
                if cb == 1:
                    t.wait_ge(VS, ct.v("VS_bcbase") + 2)
                t.matmul(psE[:], pf[0:1, F_ON:F_ON + 128],
                         pf[0:1, F_B + cb * 512:F_B + (cb + 1) * 512],
                         start=True, stop=True).then_inc(TS, 1)
                ct.add("TS")
                t.matmul(psF[:], pf[0:1, F_ON:F_ON + 128],
                         pf[0:1, F_B + 1024 + cb * 512:
                            F_B + 1024 + (cb + 1) * 512],
                         start=True, stop=True).then_inc(TS, 1)
                ct.add("TS")
                ct.c["TS_bc_%d" % cb] = ct.v("TS")

        @block.vector
        def _(v):
            if D_PHASES <= 2.3:
                return
            V = mk_auto(ct, "VS", VS, v)
            for cb in range(2):
                v.wait_ge(TS, ct.v("TS_bc_%d" % cb))
                V(lambda cb=cb: v.tensor_copy(
                    pf[:, F_B + cb * 512:F_B + (cb + 1) * 512], psE[:]))
                V(lambda cb=cb: v.tensor_copy(
                    pf[:, F_B + 1024 + cb * 512:F_B + 1024 + (cb + 1) * 512],
                    psF[:]))
        VS_bc = ct.v("VS")

        VS_p3base = None

        LD_p3b = ct.v("LD")
        VS_p3base = ct.v("VS")

        @block.tensor
        def _(t):
            t.wait_ge(VS, VS_bc)
            for tt in range(16):
                for cb in range(2):
                    j = tt * 2 + cb
                    ps = psE if j % 2 == 0 else psF
                    if j >= 2:
                        j2 = j - 2
                        t.wait_ge(VS, VS_p3base + (j2 // 2) * (9 if DEBUG_B
                                  else 8) + (j2 % 2) + 1)
                    for ec in range(8):
                        inst = t.matmul(
                            ps[:],
                            pb[:, R_AT + ec * 2048 + tt * 128:
                               R_AT + ec * 2048 + tt * 128 + 128],
                            pb[:, R_WO + ec * 1024 + cb * 512:
                               R_WO + ec * 1024 + (cb + 1) * 512],
                            start=(ec == 0), stop=(ec == 7))
                    inst.then_inc(TS, 1)
                    ct.add("TS")
                    ct.c["TS_p3_%d" % j] = ct.v("TS")

        for tt in range(16):
            @block.vector
            def _(v, tt=tt):
                V = mk_auto(ct, "VS", VS, v)
                v.wait_ge(LD, LD_p3b + 32 * (tt + 1))
                if tt >= 1:
                    v.wait_ge(OS, 16 * tt)
                for cb in range(2):
                    j = tt * 2 + cb
                    ps = psE if j % 2 == 0 else psF
                    v.wait_ge(TS, ct.v("TS_p3_%d" % j))
                    V(lambda ps=ps, cb=cb: v.scalar_tensor_tensor(
                        pf[:, F_HP + cb * 512:F_HP + (cb + 1) * 512],
                        ps[:], 1.0, pf[:, F_B + cb * 512:F_B + (cb + 1) * 512],
                        ALU.mult, ALU.add))
                    ct.c["VS_p3ev_%d" % j] = ct.v("VS")
                V(lambda tt=tt: v.tensor_tensor(
                    pf[:, F_HP:F_HP + 1024], pf[:, F_HP:F_HP + 1024],
                    pb[:, R_XT + (tt % 2) * 1024:R_XT + (tt % 2) * 1024 + 1024],
                    ALU.add))
                if DEBUG_B and tt == 0:
                    sy0 = None
                ct.c["VS_hp_%d" % tt] = ct.v("VS")

            @block.scalar
            def _(s, tt=tt):
                Sc = mk_auto(ct, "SS", SS, s)
                s.wait_ge(VS, ct.v("VS_hp_%d" % tt))
                Sc(lambda: s.activation(pb[:, R_JK:R_JK + 1024],
                                        pf[:, F_HP:F_HP + 1024], AF.Square,
                                        bias=0.0, scale=1.0,
                                        accum_out=pf[:, F_W:F_W + 1]))
            ct.c["SS_sq_%d" % tt] = ct.v("SS")

            @block.vector
            def _(v, tt=tt):
                V = mk_auto(ct, "VS", VS, v)
                v.wait_ge(SS, ct.v("SS_sq_%d" % tt))
                V(lambda: v.tensor_scalar(pf[:, F_W + 1:F_W + 2],
                                          pf[:, F_W:F_W + 1], 1.0 / DM, None,
                                          ALU.mult))
            ct.c["VS_var_%d" % tt] = ct.v("VS")

            @block.scalar
            def _(s, tt=tt):
                Sc = mk_auto(ct, "SS", SS, s)
                s.wait_ge(VS, ct.v("VS_var_%d" % tt))
                Sc(lambda: s.activation(pf[:, F_W + 2:F_W + 3],
                                        pf[:, F_W + 1:F_W + 2], AF.Sqrt,
                                        bias=1e-6, scale=1.0))
            ct.c["SS_sd_%d" % tt] = ct.v("SS")

            @block.vector
            def _(v, tt=tt):
                V = mk_auto(ct, "VS", VS, v)
                v.wait_ge(SS, ct.v("SS_sd_%d" % tt))
                if DEBUG_B and tt == 0:
                    v.tensor_copy(pf[:, 0:1024],
                                  pf[:, F_HP:F_HP + 1024]).then_inc(SHS, 1)
                    v.wait_ge(SHS, 129)
                V(lambda: v.reciprocal(pf[:, F_W + 3:F_W + 4],
                                       pf[:, F_W + 2:F_W + 3]))
                if DEBUG_B:
                    V(lambda tt=tt: v.tensor_copy(
                        pf[:, F_W + 16 + tt:F_W + 17 + tt],
                        pf[:, F_W + 3:F_W + 4]))
                V(lambda: v.tensor_scalar(pf[:, F_HP:F_HP + 1024],
                                          pf[:, F_HP:F_HP + 1024],
                                          pf[:, F_W + 3:F_W + 4], None,
                                          ALU.mult))
                V(lambda: v.tensor_tensor(pf[:, F_HP:F_HP + 1024],
                                          pf[:, F_HP:F_HP + 1024],
                                          pf[:, F_B + 1024:F_B + 2048],
                                          ALU.mult))
                V(lambda tt=tt: v.tensor_tensor(
                    pf[:, F_HP:F_HP + 1024], pf[:, F_HP:F_HP + 1024],
                    pf[:, F_AC + (tt % 2) * 1024:F_AC + (tt % 2) * 1024 + 1024],
                    ALU.add))
            ct.c["VS_y_%d" % tt] = ct.v("VS")

        @block.sync
        def _(sy):
            NOP3 = 9 if DEBUG_B else 8

            def outdma(k):
                sy.wait_ge(VS, VS_p3base + NOP3 * (k + 1))
                sy.dma_start(out=outf[k * 128:(k + 1) * 128, :],
                             in_=pf[:, F_HP:F_HP + 1024]).then_inc(OS, 16)
                ct.add("OS", 16)
            for tt in range(16):
                if tt >= 2:
                    outdma(tt - 2)
                sy.dma_start(out=pb[:, R_XT + (tt % 2) * 1024:
                               R_XT + (tt % 2) * 1024 + 1024],
                             in_=xntm1[tt * 128:(tt + 1) * 128, :]
                             ).then_inc(LD, 16)
                sy.dma_start(out=pf[:, F_AC + (tt % 2) * 1024:
                                F_AC + (tt % 2) * 1024 + 1024],
                             in_=accDi[tt * 128:(tt + 1) * 128, :]
                             ).then_inc(LD, 16)
                ct.add("LD", 32)
            outdma(14)
            outdma(15)
            sy.wait_ge(OS, ct.v("OS"))
            if DEBUG_B:
                sy.dma_start(out=ddat, in_=pb[:, R_AT:R_AT + 16384]).then_inc(OS, 16)
                sy.dma_start(out=drst, in_=pf[:, F_W + 16:F_W + 32]).then_inc(OS, 16)
                sy.dma_start(out=dhp0, in_=pf[:, 0:1024]).then_inc(OS, 16)
                sy.wait_ge(OS, ct.v("OS") + 48)
    return nc



def _c(a):
    return np.ascontiguousarray(a, dtype=np.float32)


def prep_a_inputs(inp):
    x = np.asarray(inp["L0_attn_x"], np.float32)
    wq = np.asarray(inp["L0_attn_wq"], np.float32)
    wk = np.asarray(inp["L0_attn_wk"], np.float32)
    wv = np.asarray(inp["L0_attn_wv"], np.float32)
    wo = np.asarray(inp["L0_attn_wo"], np.float32)
    bq = np.asarray(inp["L0_attn_bq"], np.float32)
    bk = np.asarray(inp["L0_attn_bk"], np.float32)
    bv = np.asarray(inp["L0_attn_bv"], np.float32)
    bo = np.asarray(inp["L0_attn_bo"], np.float32)
    wqf = np.asarray(inp["L0_ffn_Wq"], np.float32)
    bqf = np.asarray(inp["L0_ffn_bq"], np.float32)

    wqT = _c(wq.T[:, PERM1024])
    wkT = _c(wk.T)
    wvT = _c(wv.T)
    woT = _c(wo.T[PERM1024, :])
    bqp = bq[PERM1024]
    bq_l = _c(bqp.reshape(8, 128).T)
    bk_l = _c(bk.reshape(2, 128).T)
    bqf_l = _c(bqf.reshape(16, 128).T)
    identb = np.eye(128, dtype=bfd)
    shiftb = np.zeros((64, 128), dtype=bfd)
    for i in range(64):
        shiftb[i, i + 64] = 1
    maps = []
    for cid in range(8):
        b, p = cid // 4, cid % 4
        maps.append({
            "xb": _c(x[b]),
            "xloc": _c(x[b, p * 2048:(p + 1) * 2048]),
            "wqT": wqT, "wkT": wkT, "wvT": wvT, "woT": woT,
            "bq_l": bq_l, "bk_l": bk_l,
            "bv_r": _c(bv[None, :]), "bo_r": _c(bo[None, :]),
            "wqf": _c(wqf), "bqf_l": bqf_l,
            "identb": identb, "shiftb": shiftb,
        })
    return maps




def prep_b_consts(inp):
    wqf = _c(inp["L0_ffn_Wq"])
    bqf = np.asarray(inp["L0_ffn_bq"], np.float32)
    gam = np.asarray(inp["L0_ffn_bn_gamma"], np.float32)
    bet = np.asarray(inp["L0_ffn_bn_beta"], np.float32)
    tab = np.ascontiguousarray(np.concatenate(
        [np.asarray(inp["L0_ffn_expert_down"], np.float32),
         np.asarray(inp["L0_ffn_expert_up"], np.float32)], axis=1))
    d = {
        "wqf": wqf,
        "bqf_l": _c(bqf.reshape(16, 128).T),
        "gam_l": _c(gam.reshape(2, 128).T),
        "bet_l": _c(bet.reshape(2, 128).T),
        "keys0": _c(inp["L0_ffn_sub_keys0"]),
        "keys1": _c(inp["L0_ffn_sub_keys1"]),
        "tab": tab,
        "identb": np.eye(128, dtype=bfd),
        "iotad": _c(np.tile(127.0 - np.arange(128, dtype=np.float32),
                            (128, 1))),
        "negc": _c(np.tile(255.0 - np.arange(256, dtype=np.float32),
                           (128, 1))),
        "iota16": _c(np.tile(np.arange(16, dtype=np.float32), (128, 1))),
    }
    return d






def prep_c1_consts(inp):
    we1 = np.ascontiguousarray(np.concatenate(
        [np.asarray(inp["L1_ffn_We1"], np.float32),
         np.asarray(inp["L1_ffn_Ws1"], np.float32)[None]], axis=0))
    we2 = np.ascontiguousarray(np.concatenate(
        [np.asarray(inp["L1_ffn_We2"], np.float32),
         np.asarray(inp["L1_ffn_Ws2"], np.float32)[None]], axis=0))
    be1 = np.concatenate([np.asarray(inp["L1_ffn_be1"], np.float32),
                          np.asarray(inp["L1_ffn_bs1"], np.float32)[None]], 0)
    be2 = np.concatenate([np.asarray(inp["L1_ffn_be2"], np.float32),
                          np.asarray(inp["L1_ffn_bs2"], np.float32)[None]], 0)
    oneh8 = np.zeros((8, 1024), dtype=bfd)
    for e in range(8):
        oneh8[e, e * 128:(e + 1) * 128] = 1
    return {
        "wr1": _c(inp["L1_ffn_Wr1"]),
        "br1_l": _c(np.asarray(inp["L1_ffn_br1"], np.float32)
                    .reshape(2, 128).T),
        "wr2": _c(inp["L1_ffn_Wr2"]),
        "br2_r": _c(np.asarray(inp["L1_ffn_br2"], np.float32)[None, :]),
        "we1": we1,
        "be1_l": _c(be1.reshape(9, 32, 128).transpose(2, 0, 1)
                    .reshape(128, 288)),
        "we2": we2,
        "be2_9": _c(be2),
        "oneh8": oneh8,
        "identb": np.eye(128, dtype=bfd),
    }




def _rope_consts(pos0):
    invf = 1.0 / 10000.0 ** (np.arange(0, 64, 2, dtype=np.float32) / 64.0)
    t = np.arange(pos0, pos0 + 2048, dtype=np.float32)
    fr = t[:, None] * invf[None, :]
    cos64 = np.cos(np.concatenate([fr, fr], axis=1))
    sin64 = np.sin(np.concatenate([fr, fr], axis=1))
    cosf = np.zeros((128, 2048), np.float32)
    rsf = np.zeros((128, 2048), np.float32)
    for r in range(128):
        d = r % 64
        cosf[r] = cos64[:, d]
        rsf[r] = (-1.0 if d < 32 else 1.0) * sin64[:, d]
    return cosf, rsf


def prep_c2_consts(inp):
    swapb = np.zeros((128, 128), dtype=bfd)
    for m in range(128):
        blk, mm = (m // 64) * 64, m % 64
        p = blk + (mm + 32 if mm < 32 else mm - 32)
        swapb[p, m] = 1
    selb = np.zeros((16, 1024), dtype=bfd)
    for qc in range(8):
        for p in range(128):
            selb[qc * 2 + p // 64, qc * 128 + p] = 1
    oneh16 = np.zeros((16, 1024), np.float32)
    for h in range(16):
        oneh16[h, h * 64:(h + 1) * 64] = 1
    return {
        "wqT": _c(np.asarray(inp["L1_attn_Wq"], np.float32).T),
        "wkT": _c(np.asarray(inp["L1_attn_Wk"], np.float32).T),
        "wvT": _c(np.asarray(inp["L1_attn_Wv"], np.float32).T),
        "wbT": _c(np.asarray(inp["L1_attn_Wb"], np.float32).T),
        "bb_c": _c(np.asarray(inp["L1_attn_bb"], np.float32)[:, None]),
        "swapb": swapb, "selb": selb, "oneh16": oneh16,
        "identb": np.eye(128, dtype=bfd),
    }


def prep_d_consts(inp):
    causal = np.tril(np.ones((128, 128), np.float32)).T.astype(bfd)
    shiftb = np.zeros((64, 128), dtype=bfd)
    for i in range(64):
        shiftb[i, i + 64] = 1
    oneh16 = np.zeros((16, 1024), np.float32)
    for h in range(16):
        oneh16[h, h * 64:(h + 1) * 64] = 1
    return {
        "wo1T": _c(np.asarray(inp["L1_attn_Wo"], np.float32).T),
        "bo1_r": _c(np.asarray(inp["L1_attn_bo"], np.float32)[None, :]),
        "sc1_r": _c(np.asarray(inp["L1_attn_scale"], np.float32)[None, :]),
        "causal": causal, "shiftb": shiftb, "oneh16": oneh16,
    }


def build_fused():
    nc = bass.Bass("TRN2", target_bir_lowering=False)
    io = {}

    def ein(name, shape, dt):
        io[name] = nc.dram_tensor(name, shape, dt, kind="ExternalInput").ap()

    def inl(name, shape, dt):
        io[name] = nc.dram_tensor(name, shape, dt, kind="Internal").ap()

    ein("xb", [S, DM], F32); ein("xloc", [TLOC, DM], F32)
    ein("wqT", [DM, DM], F32); ein("wkT", [DM, 256], F32)
    ein("wvT", [DM, 256], F32); ein("woT", [DM, DM], F32)
    ein("bq_l", [128, 8], F32); ein("bk_l", [128, 2], F32)
    ein("bv_r", [1, 256], F32); ein("bo_r", [1, DM], F32)
    ein("wqf", [DM, 2048], F32); ein("bqf_l", [128, 16], F32)
    ein("identb", [128, 128], BF16); ein("shiftb", [64, 128], BF16)
    ein("gam_l", [128, 2], F32); ein("bet_l", [128, 2], F32)
    ein("keys0", [128, 128], F32); ein("keys1", [128, 128], F32)
    ein("tab", [16384, 2048], F32)
    ein("iotad", [128, 128], F32); ein("negc", [128, 256], F32)
    ein("iota16", [128, 16], F32)
    ein("wr1", [DM, 256], F32); ein("br1_l", [128, 2], F32)
    ein("wr2", [256, 8], F32); ein("br2_r", [1, 8], F32)
    ein("we1", [9, DM, 4096], F32); ein("be1_l", [128, 9 * 32], F32)
    ein("we2", [9, 4096, DM], F32); ein("be2_9", [9, DM], F32)
    ein("oneh8", [8, 1024], BF16)
    ein("q1T", [DM, DM], F32); ein("k1T", [DM, DM], F32)
    ein("v1T", [DM, DM], F32); ein("wbT", [DM, 16], F32)
    ein("bb_c", [16, 1], F32)
    ein("cosf", [128, 2048], F32); ein("rsf", [128, 2048], F32)
    ein("swapb", [128, 128], BF16); ein("selb", [16, 1024], BF16)
    ein("oneh16", [16, 1024], F32)
    ein("wo1T", [DM, DM], F32); ein("bo1_r", [1, DM], F32)
    ein("sc1_r", [1, DM], F32); ein("causal", [128, 128], BF16)
    ein("selw", [64, 24], F32)
    inl("a0", [TLOC, DM], F32); inl("bst", [128, 4], F32)
    inl("bstG", [1024, 4], F32)
    inl("xnfm", [128, 16384], BF16); inl("xntm", [TLOC, DM], BF16)
    inl("x1", [TLOC, DM], F32); inl("tabb", [16384, 2048], BF16)
    inl("accD", [TLOC, DM], F32)
    inl("xnfm1", [128, 16384], BF16); inl("xntm1", [TLOC, DM], BF16)
    inl("qh", [128, 16384], BF16); inl("kh", [128, 16384], BF16)
    inl("khtm", [128, 16384], BF16); inl("vsb", [128, 16640], BF16)
    inl("aalld", [16, 2048], F32)
    inl("sumcat", [16, 4224], F32); inl("sumcatG", [128, 4224], F32)
    io["outf"] = nc.dram_tensor("outf", [TLOC, DM], F32,
                                kind="ExternalOutput").ap()

    def allgather(tag, src, dst):
        with nc.Block() as block, nc.semaphore("ccs" + tag) as ccs:
            @block.gpsimd
            def _(g):
                g.collective_compute(
                    "AllGather", ALU.bypass,
                    replica_groups=[list(range(8))],
                    ins=[src.opt()], outs=[dst.opt()]).then_inc(ccs, 1)
                g.wait_ge(ccs, 1)

    with nc.reset_on_exit():
        phase_a(nc, io)
    with nc.reset_on_exit():
        allgather("b", io["bst"], io["bstG"])
    with nc.reset_on_exit():
        phase_b(nc, io)
    with nc.reset_on_exit():
        phase_c1(nc, io)
    with nc.reset_on_exit():
        phase_c2(nc, io)
    with nc.reset_on_exit():
        allgather("s", io["sumcat"], io["sumcatG"])
    phase_d(nc, io)
    return nc


_NC_CACHE = {}


def _get_nc(name, builder):
    if name not in _NC_CACHE:
        _NC_CACHE[name] = builder()
    return _NC_CACHE[name]


def prep_fused_inputs(inp):
    maps_a = prep_a_inputs(inp)
    bcon = prep_b_consts(inp)
    c1c = prep_c1_consts(inp)
    c2c = dict(prep_c2_consts(inp))
    c2c["q1T"] = c2c.pop("wqT")
    c2c["k1T"] = c2c.pop("wkT")
    c2c["v1T"] = c2c.pop("wvT")
    dc = prep_d_consts(inp)
    maps = []
    for cid in range(8):
        b, p = cid // 4, cid % 4
        m = {}
        m.update(maps_a[cid])
        m.update(bcon)
        m.update(c1c)
        m.update(c2c)
        cf, rf = _rope_consts(p * 2048)
        m["cosf"], m["rsf"] = cf, rf
        m.update(dc)
        selw = np.zeros((64, 24), np.float32)
        for k in (1, 2, 3):
            if p - k >= 0:
                selw[:, (k - 1) * 8 + (b * 4 + p - k)] = 1.0
        m["selw"] = _c(selw)
        maps.append(m)
    return maps


def kernel(**inputs):
    nc = _get_nc("fused", build_fused)
    maps = prep_fused_inputs(inputs)
    res = run_bass_kernel_spmd(nc, maps, core_ids=list(range(8))).results
    out = np.zeros((2, 8192, 1024), np.float32)
    for cid in range(8):
        b, p = cid // 4, cid % 4
        out[b, p * 2048:(p + 1) * 2048] = res[cid]["outf"]
    return out
